# version 05
# baseline (speedup 1.0000x reference)
import numpy as np
import ml_dtypes
import concourse.bass as bass
import concourse.mybir as mybir
from concourse.bass_utils import run_bass_kernel_spmd
from contextlib import ExitStack

F32 = mybir.dt.float32
BF16 = mybir.dt.bfloat16
I32 = mybir.dt.int32
AF = mybir.ActivationFunctionType
ALU = mybir.AluOpType
AX = mybir.AxisListType

NT = 1024
D = 4096
NCH = 32
NTT = 8
NCL = 64
NH = 16
DH = 256

class Sem:
    def __init__(self, name):
        self.name = name; self.n = 0; self.h = None

class Prog:
    ENGS = ("sync", "tensor", "vector", "scalar", "gpsimd")
    def __init__(self):
        self.q = {e: [] for e in self.ENGS}
        self.sems = []
        self.hwm = {e: {} for e in self.ENGS}
        self.canon = {}
    def sem(self, name):
        s = Sem(name); self.sems.append(s); return s
    def raw(self, eng, fn):
        self.q[eng].append(fn)
    def wait(self, eng, sem, val=None):
        v = sem.n if val is None else val
        if v <= 0: return
        if self.hwm[eng].get(sem.name, 0) >= v: return
        self.hwm[eng][sem.name] = v
        self.q[eng].append(lambda e, s=sem, v=v: e.wait_ge(s.h, v))
    def dma(self, eng, out, in_, sem, **kw):
        sem.n += 16
        self.q[eng].append(lambda e, o=out, i=in_, s=sem, kw=dict(kw): e.dma_start(out=o, in_=i, **kw).then_inc(s.h, 16))
        return sem.n
    def cc(self, kind, op, groups, in_ap, out_ap, sem):
        sem.n = 1
        self.q["gpsimd"].append(lambda e, k=kind, o=op, g=groups, i=in_ap, a=out_ap, s=sem:
                                e.collective_compute(k, o, replica_groups=g, ins=[i.opt()], outs=[a.opt()]).then_inc(s.h))
        return 1
    def op_acc(self, eng, builder, sem, fence):
        c = self.canon.get(eng)
        if c is not None and c.n > 0:
            self.q[eng].append(lambda e, s=c, v=c.n: e.wait_ge(s.h, v))
        self.q[eng].append(lambda e, b=builder: b(e))
        sem.n += 1
        self.q[eng].append(lambda e, f=fence, s=sem: f(e).then_inc(s.h, 1))
        return sem.n
    def op(self, eng, builder, sem=None):
        c = self.canon.get(eng)
        if c is not None:
            if sem is None:
                sem = c
            assert sem is c, f"compute op on {eng} must use canonical sem"
            if c.n > 0:
                self.q[eng].append(lambda e, s=c, v=c.n: e.wait_ge(s.h, v))
        if sem is not None:
            sem.n += 1
        self.q[eng].append(lambda e, b=builder, s=sem: (b(e).then_inc(s.h, 1) if s is not None else b(e)))
        return None if sem is None else sem.n
    def barrier(self, engs, sems):
        for e in engs:
            for s in sems:
                self.wait(e, s)
    def replay(self, nc):
        for s in self.sems:
            pass
        with nc.Block() as block:
            for eng in self.ENGS:
                ops = self.q[eng]
                if not ops: continue
                dec = getattr(block, eng)
                def make(ops):
                    def body(e):
                        for fn in ops:
                            fn(e)
                    return body
                dec(make(ops))

def alloc_sems(nc, prog, stack):
    for s in prog.sems:
        if s.n > 0:
            s.h = stack.enter_context(nc.semaphore(s.name))

ALL_SEM_NAMES = ["s_ld", "s_pe", "s_dve", "s_act", "s_st", "s_gp",
                 "s_x0", "s_x1", "s_c0", "s_c1", "s_st0", "s_st1", "s_rows", "s_bd",
                 "s_h0", "s_h1", "s_xt0", "s_xt1", "s_xt2", "s_xt3",
                 "s_vt0", "s_vt1", "s_kq0", "s_kq1", "s_xh0", "s_xh1",
                 "s_cc1", "s_cck", "s_ccv", "s_cc3"]

def phase_barrier(prog):
    for e in Prog.ENGS:
        for s in prog.sems:
            prog.wait(e, s)

def pack_w_panels(W):
    Wt = np.ascontiguousarray(W.T)
    t = Wt.reshape(32, 128, 32, 128)
    t = np.ascontiguousarray(t.transpose(2, 1, 0, 3))
    return t.reshape(32, 128, 4096)

def pack_col(b):
    return np.ascontiguousarray(b.reshape(32, 128).T)

def pack_cent(cent):
    centT = cent.T
    t = centT.reshape(32, 128, 64)
    return np.ascontiguousarray(t.transpose(1, 0, 2)).reshape(128, 32*64)

CONSTS = {}
def make_consts():
    CONSTS["ident_f32"] = np.eye(128, dtype=np.float32)
    CONSTS["ident_bf16"] = np.eye(128, dtype=ml_dtypes.bfloat16)
    CONSTS["ones_col_f32"] = np.ones((128, 1), np.float32)
    CONSTS["ones_col_bf16"] = np.ones((128, 1), ml_dtypes.bfloat16)
    CONSTS["ones_row_f32"] = np.ones((1, 128), np.float32)
    CONSTS["iota64"] = np.tile(np.arange(64, dtype=np.float32)[None, :], (128, 1))
make_consts()

def declare_consts(nc, names):
    return {n: nc.dram_tensor("c_" + n, list(CONSTS[n].shape),
                              F32 if CONSTS[n].dtype == np.float32 else BF16,
                              kind="ExternalInput").ap() for n in names}

def const_inputs(names):
    return {"c_" + n: CONSTS[n] for n in names}

def emit_ln_dots(prog, nc, T, xT_dram, sb, with_dots=True):
    s_ld, s_cast, s_pe, s_dve, s_act = T["s_ld"], T["s_cast"], T["s_pe"], T["s_dve"], T["s_act"]
    s_x = [T["s_x0"], T["s_x1"]]
    s_rows = T["s_rows"]
    if with_dots:
        prog.wait("tensor", s_ld)
        for c in range(NCH):
            prog.op("tensor", lambda e, c=c: e.matmul(
                T["p_scr"][0:1, 0:64], sb["ones_col_f32"][:, :], sb["cent"][:, c*64:(c+1)*64],
                start=(c == 0), stop=(c == NCH-1)), s_pe if c == NCH-1 else None)
        v_pe_csum = s_pe.n
        prog.wait("vector", s_pe, v_pe_csum)
        prog.op("vector", lambda e: e.tensor_copy(sb["csrow"][:, :], T["p_scr"][0:1, 0:64]), s_dve)
        v_csrow = s_dve.n
        prog.wait("tensor", s_dve, v_csrow)
        prog.op("tensor", lambda e: e.matmul(T["p_scr2"][:, 0:64], sb["ones_row_f32"][:, :], sb["csrow"][:, :],
                                             start=True, stop=True), s_pe)
        v_pe_csbc = s_pe.n
        prog.wait("vector", s_pe, v_pe_csbc)
        prog.op("vector", lambda e: e.tensor_copy(sb["csbc"][:, :], T["p_scr2"][:, 0:64]), s_dve)
        v_csbc = s_dve.n
        prog.wait("tensor", s_dve, v_csbc)

    ld_base = s_ld.n
    cast_base = s_cast.n
    for c in range(NCH):
        buf = c % 2
        if c >= 2:
            prog.wait("sync", s_pe, T["pe_done_chunk"][c-2])
        v_ld = prog.dma("sync", sb["xbuf"][buf][:, :], xT_dram[c*128:(c+1)*128, :], s_x[buf])
        prog.wait("scalar", s_x[buf], v_ld)
        if c >= 2:
            prog.wait("scalar", s_pe, T["pe_done_chunk"][c-2])
        prog.op("scalar", lambda e, buf=buf: e.activation(sb["sqf"][buf][:, :], sb["xbuf"][buf][:, :], AF.Square), s_act)
        T["act_done_chunk"][c] = s_act.n
        prog.wait("tensor", s_x[buf], v_ld)
        prog.wait("tensor", s_act, T["act_done_chunk"][c])
        for half in range(2):
            prog.op("tensor", lambda e, c=c, buf=buf, half=half: e.matmul(
                T["p_s1"][0:1, half*512:(half+1)*512], sb["ones_col_f32"][:, :],
                sb["xbuf"][buf][:, half*512:(half+1)*512], start=(c == 0), stop=(c == NCH-1)), None)
            prog.op("tensor", lambda e, c=c, buf=buf, half=half: e.matmul(
                T["p_s2"][0:1, half*512:(half+1)*512], sb["ones_col_f32"][:, :],
                sb["sqf"][buf][:, half*512:(half+1)*512], start=(c == 0), stop=(c == NCH-1)),
                None if half == 0 else s_pe)
        T["pe_done_chunk"][c] = s_pe.n
    v_pe_all = s_pe.n

    prog.wait("vector", s_pe, v_pe_all)
    prog.op("vector", lambda e: e.tensor_copy(sb["s1row"][:, :], T["p_s1"][0:1, :]), s_dve)
    prog.op("vector", lambda e: e.tensor_copy(sb["s2row"][:, :], T["p_s2"][0:1, :]), s_dve)
    prog.op("vector", lambda e: e.tensor_scalar(sb["murow"][:, :], sb["s1row"][:, :], 1.0/D, None, op0=ALU.mult), s_dve)
    prog.op("vector", lambda e: e.tensor_scalar(sb["msqrow"][:, :], sb["s2row"][:, :], 1.0/D, None, op0=ALU.mult), s_dve)
    prog.op("vector", lambda e: e.tensor_tensor(sb["m2row"][:, :], sb["murow"][:, :], sb["murow"][:, :], op=ALU.mult), s_dve)
    prog.op("vector", lambda e: e.tensor_tensor(sb["varrow"][:, :], sb["msqrow"][:, :], sb["m2row"][:, :], op=ALU.subtract), s_dve)
    v_var = s_dve.n
    prog.wait("scalar", s_dve, v_var)
    prog.op("scalar", lambda e: e.activation(sb["lnrow"][:, :], sb["varrow"][:, :], AF.Ln, bias=sb["eps1"][:, :]), s_act)
    prog.op("scalar", lambda e: e.activation(sb["rrow"][:, :], sb["lnrow"][:, :], AF.Exp, scale=-0.5), s_act)
    v_rrow = s_act.n
    if not with_dots:
        return v_rrow

    prog.wait("vector", s_act, v_rrow)
    prog.op("vector", lambda e: e.tensor_scalar(sb["negr10row"][:, :], sb["rrow"][:, :], -10.0, None, op0=ALU.mult), s_dve)
    v_negr10 = s_dve.n
    prog.wait("sync", s_dve, v_negr10)
    prog.dma("sync", T["scr_dram"][0:1, :], sb["murow"][:, :], s_rows)
    v_w1 = prog.dma("sync", T["scr_dram"][1:2, :], sb["negr10row"][:, :], s_rows)
    prog.wait("sync", s_rows, v_w1)
    v_rc = prog.dma("sync", sb["rowcols"][:, :], T["scr_dram"].rearrange("a (b p) -> (a b) p", p=128)[:, :], s_rows)
    prog.wait("tensor", s_rows, v_rc)
    prog.op("tensor", lambda e: e.transpose(T["p_scr"][:, 16:32], sb["rowcols"][0:16, :], sb["ident_f32"][0:16, 0:16]), s_pe)
    v_pe_cols = s_pe.n
    prog.wait("vector", s_pe, v_pe_cols)
    prog.op("vector", lambda e: e.tensor_copy(sb["mucols"][:, :], T["p_scr"][:, 16:24]), s_dve)
    prog.op("vector", lambda e: e.tensor_copy(sb["nr10cols"][:, :], T["p_scr"][:, 24:32]), s_dve)
    v_cols = s_dve.n

    v_pd_prev = 0
    v_hits_prev = 0
    v_counts_prev = 0
    v_lg_prev = {}
    for t in range(NTT):
        pd_bank = T["p_dots"][t % 2]
        NXT = len(sb["xtile"])
        prog.wait("tensor", s_dve, v_cols)
        if t >= 2:
            prog.wait("tensor", s_dve, v_lg_prev[t-2])
        for c in range(NCH):
            tb = c % NXT
            gi = t*NCH + c
            if gi >= NXT:
                pg, pc = divmod(gi - NXT, NCH)
                prog.wait("sync", s_pe, T[f"dots_mm_{pg}_{pc}"])
            v_tl = prog.dma("sync", sb["xtile"][tb][:, :], xT_dram[c*128:(c+1)*128, t*128:(t+1)*128],
                            T[f"s_xt{tb}"])
            prog.wait("tensor", T[f"s_xt{tb}"], v_tl)
            prog.op("tensor", lambda e, c=c, tb=tb, pd_bank=pd_bank: e.matmul(
                pd_bank[:, 0:64], sb["xtile"][tb][:, :], sb["cent"][:, c*64:(c+1)*64],
                start=(c == 0), stop=(c == NCH-1)), s_pe)
            T[f"dots_mm_{t}_{c}"] = s_pe.n
        v_dots_t = s_pe.n
        prog.wait("vector", s_pe, v_dots_t)
        if t > 0:
            prog.wait("vector", s_act, v_pd_prev)
        prog.op("vector", lambda e, t=t, pd_bank=pd_bank: e.scalar_tensor_tensor(
            sb["lg"][:, :], sb["csbc"][:, :], sb["mucols"][:, t:t+1], pd_bank[:, 0:64],
            op0=ALU.mult, op1=ALU.subtract), s_dve)
        prog.op("vector", lambda e, t=t: e.tensor_scalar(sb["lg"][:, :], sb["lg"][:, :], sb["nr10cols"][:, t:t+1], None, op0=ALU.mult), s_dve)
        v_lg_prev[t] = s_dve.n
        prog.op("vector", lambda e: e.tensor_reduce(sb["negmax"][:, :], sb["lg"][:, :], axis=AX.X, op=ALU.max, negate=True), s_dve)
        v_lg = s_dve.n
        prog.wait("scalar", s_dve, v_lg)
        if t > 0:
            prog.wait("scalar", s_dve, v_hits_prev)
        prog.op("scalar", lambda e: e.activation(sb["pd"][:, :], sb["lg"][:, :], AF.Exp, bias=sb["negmax"][:, :], accum_out=sb["z"][:, :]), None)
        prog.op("scalar", lambda e: e.activation(sb["z001"][:, :], sb["z"][:, :], AF.Identity, scale=0.01), None)
        prog.op("scalar", lambda e: e.activation(sb["junk"][:, :], sb["z001"][:, :], AF.Identity), s_act)
        v_pd = s_act.n
        prog.wait("vector", s_act, v_pd)
        if t > 0:
            prog.wait("vector", s_pe, v_counts_prev)
        prog.op("vector", lambda e, t=t: e.tensor_scalar(sb["hits"][:, :], sb["pd"][:, :], sb["z001"][:, :], None, op0=ALU.is_gt,
                                                          op1=ALU.max, accum_out=sb["qs_all"][:, t:t+1]), s_dve)
        v_hits_prev = s_dve.n
        v_pd_prev = v_pd
        v_hits = s_dve.n
        prog.wait("tensor", s_dve, v_hits)
        prog.op("tensor", lambda e, t=t: e.matmul(T["p_counts"][0:64, 0:1], sb["hits"][:, :], sb["ones_col_bf16"][:, :],
                                                  start=(t == 0), stop=(t == NTT-1)), s_pe)
        v_counts_prev = s_pe.n
    return s_pe.n

def phase1(nc, prog, SEMS, EX):
    xT = EX["xT"]
    cent = EX["cent_pack"]
    cns = EX["cns"]
    o_counts = EX["c1_loc"]
    o_mu = EX["mu1_d"]
    o_r = EX["r1_d"]
    o_qs = EX["qs1_d"]

    with ExitStack() as st:
        def sbuf(name, shape, dt):
            return st.enter_context(nc.sbuf_tensor("p1_"+name, shape, dt))
        sb = {}
        sb["cent"] = sbuf("cent", [128, NCH*NCL], F32)
        sb["ident_f32"] = sbuf("identf", [128, 128], F32)
        sb["ones_col_f32"] = sbuf("onescf", [128, 1], F32)
        sb["ones_col_bf16"] = sbuf("onescb", [128, 1], BF16)
        sb["ones_row_f32"] = sbuf("onesr", [1, 128], F32)
        sb["xbuf"] = [sbuf(f"xbuf{i}", [128, NT], F32) for i in range(2)]
        sb["sqf"] = [sbuf(f"sqf{i}", [128, NT], F32) for i in range(2)]
        sb["xtile"] = [sbuf(f"xtile{i}", [128, 128], F32) for i in range(4)]
        for n in ["csrow"]: sb[n] = sbuf(n, [1, NCL], F32)
        sb["csbc"] = sbuf("csbc", [128, NCL], F32)
        for n in ["s1row","s2row","murow","msqrow","m2row","varrow","lnrow","rrow","negr10row"]:
            sb[n] = sbuf(n, [1, NT], F32)
        sb["eps1"] = sbuf("eps1", [1, 1], F32)
        sb["mucols"] = sbuf("mucols", [128, NTT], F32)
        sb["nr10cols"] = sbuf("nr10cols", [128, NTT], F32)
        sb["lg"] = sbuf("lg", [128, NCL], F32)
        sb["negmax"] = sbuf("negmax", [128, 1], F32)
        sb["pd"] = sbuf("pd", [128, NCL], F32)
        sb["z"] = sbuf("z", [128, 1], F32)
        sb["zinv"] = sbuf("zinv", [128, 1], F32)
        sb["dn"] = sbuf("dn", [128, NCL], F32)
        sb["hits"] = sbuf("hits", [128, NCL], BF16)
        sb["qs_all"] = sbuf("qs_all", [128, NTT], F32)
        sb["cnt"] = sbuf("cnt", [64, 1], F32)
        sb["junk"] = sbuf("junk", [128, 1], F32)
        sb["z001"] = sbuf("z001", [128, 1], F32)
        sb["rowcols"] = sbuf("rowcols", [16, 128], F32)
        sb["qsT"] = sbuf("qsT", [NTT, 128], F32)

        T = {}
        T["scr_dram"] = EX["scr_rows1"]
        T["p_s1"] = st.enter_context(nc.psum_tensor("p1_p_s1", [1, NT], F32))
        T["p_s2"] = st.enter_context(nc.psum_tensor("p1_p_s2", [1, NT], F32))
        _pd = st.enter_context(nc.psum_tensor("p1_p_dots", [128, 1024], F32))
        T["p_dots"] = [_pd[:, 0:512], _pd[:, 512:1024]]
        T["p_scr"] = st.enter_context(nc.psum_tensor("p1_p_scr", [128, 512], F32))
        _ps2 = st.enter_context(nc.psum_tensor("p1_p_scr2", [128, 512], F32))
        T["p_scr2"] = _ps2[:, 0:256]
        T["p_counts"] = _ps2[:, 256:512]
        T.update(SEMS)
        T["s_cast"] = SEMS["s_ld"]
        T["pe_done_chunk"] = {}
        T["act_done_chunk"] = {}

        prog.dma("sync", sb["cent"][:, :], cent[:, :], T["s_ld"])
        prog.dma("sync", sb["ident_f32"][:, :], cns["ident_f32"][:, :], T["s_ld"])
        prog.dma("sync", sb["ones_col_f32"][:, :], cns["ones_col_f32"][:, :], T["s_ld"])
        prog.dma("sync", sb["ones_col_bf16"][:, :], cns["ones_col_bf16"][:, :], T["s_ld"])
        prog.dma("sync", sb["ones_row_f32"][:, :], cns["ones_row_f32"][:, :], T["s_ld"])
        prog.op("vector", lambda e: e.memset(sb["eps1"][:, :], 1e-5), T["s_dve"])

        v_done = emit_ln_dots(prog, nc, T, xT, sb, with_dots=True)

        prog.wait("vector", T["s_pe"], v_done)
        prog.op("vector", lambda e: e.tensor_copy(sb["cnt"][:, :], T["p_counts"][0:64, 0:1]), T["s_dve"])
        v_cnt = T["s_dve"].n
        prog.wait("tensor", T["s_pe"], v_done)
        prog.op("tensor", lambda e: e.transpose(T["p_scr"][0:NTT, 0:128], sb["qs_all"][:, :], sb["ident_f32"][:, :]), T["s_pe"])
        v_qt = T["s_pe"].n
        prog.wait("vector", T["s_pe"], v_qt)
        prog.op("vector", lambda e: e.tensor_copy(sb["qsT"][:, :], T["p_scr"][0:NTT, 0:128]), T["s_dve"])
        v_qsT = T["s_dve"].n
        prog.wait("sync", T["s_dve"], v_cnt)
        v_cst = prog.dma("sync", o_counts[:, :], sb["cnt"][:, :], T["s_st"])
        prog.dma("sync", o_mu[:, :], sb["murow"][:, :], T["s_st"])
        prog.dma("sync", o_r[:, :], sb["rrow"][:, :], T["s_st"])
        prog.wait("sync", T["s_dve"], v_qsT)
        prog.dma("sync", o_qs[0:1, :].rearrange("a (b p) -> (a b) p", p=128), sb["qsT"][:, :], T["s_st"])

        prog.wait("gpsimd", T["s_st"], v_cst)
        prog.cc("AllReduce", ALU.add, [[0, 1, 2, 3, 4, 5, 6, 7]], EX["c1_loc"], EX["c1_sum"], T["s_cc1"])

def emit_ln_apply(prog, nc, T, sb, xT_dram, murow, rrow, xn_sb_name="xn"):
    s_ld, s_pe, s_dve, s_act, s_gp = T["s_ld"], T["s_pe"], T["s_dve"], T["s_act"], T["s_gp"]
    s_x = [T["s_x0"], T["s_x1"]]
    prog.op("vector", lambda e: e.scalar_tensor_tensor(sb["nmrow"][:, :], murow, -1.0, rrow, op0=ALU.mult, op1=ALU.mult), s_dve)
    v_nm = s_dve.n
    prog.wait("tensor", s_dve, v_nm)
    for half in range(2):
        prog.op("tensor", lambda e, half=half: e.matmul(T["p_s1t"][:, half*512:(half+1)*512], sb["ones_row_f32"][:, :],
                                                        rrow[0:1, half*512:(half+1)*512], start=True, stop=True), None)
        prog.op("tensor", lambda e, half=half: e.matmul(T["p_s2t"][:, half*512:(half+1)*512], sb["ones_row_f32"][:, :],
                                                        sb["nmrow"][0:1, half*512:(half+1)*512], start=True, stop=True), s_pe if half == 1 else None)
    v_bc = s_pe.n
    prog.wait("vector", s_pe, v_bc)
    prog.op("vector", lambda e: e.tensor_copy(sb["rbc"][:, :], T["p_s1t"][:, :]), s_dve)
    prog.op("vector", lambda e: e.tensor_copy(sb["nmbc"][:, :], T["p_s2t"][:, :]), s_dve)
    v_bc2 = s_dve.n
    prog.wait("gpsimd", s_dve, v_bc2)
    xn = sb[xn_sb_name]
    done_mark = {}
    for c in range(NCH):
        buf = c % 2
        if c >= 2:
            prog.wait("sync", s_gp, done_mark[c-2][0])
        v_ld = prog.dma("sync", sb["xbuf"][buf][:, :], xT_dram[c*128:(c+1)*128, :], s_x[buf])
        prog.wait("gpsimd", s_x[buf], v_ld)
        if c >= 2:
            prog.wait("gpsimd", s_dve, done_mark[c-2][1])
        prog.op("gpsimd", lambda e, buf=buf: e.tensor_tensor(sb["t1"][buf][:, :], sb["xbuf"][buf][:, :], sb["nmbc"][:, :], op=ALU.add), s_gp)
        v_gp = s_gp.n
        prog.wait("vector", s_gp, v_gp)
        prog.op("vector", lambda e, buf=buf, c=c: e.tensor_tensor(xn[:, c*NT:(c+1)*NT], sb["t1"][buf][:, :], sb["rbc"][:, :], op=ALU.mult), s_dve)
        done_mark[c] = (v_gp, s_dve.n)
    return s_dve.n

def emit_dense(prog, nc, T, sb, W_dram, xn_view, epilogue, tag):
    s_pe = T["s_pe"]
    s_c = [T["s_c0"], T["s_c1"]]
    prog.wait("gpsimd", s_pe)
    epi_done = {}
    for oc in range(NCH):
        buf = oc % 2
        if oc >= 2:
            prog.wait("gpsimd", s_pe, T[f"dense_mm_done_{tag}"][oc-2])
        v_cast = prog.dma("gpsimd", sb["panel"][buf][:, :], W_dram[oc, :, :], s_c[buf], max_dma_last_dim=8192)
        prog.wait("tensor", s_c[buf], v_cast)
        if oc >= 2:
            eng, val = epi_done[oc-2]
            prog.wait("tensor", T[eng], val)
        for c in range(NCH):
            for half in range(2):
                prog.op("tensor", lambda e, buf=buf, c=c, half=half: e.matmul(
                    T["p_d"][buf][:, half*512:(half+1)*512],
                    sb["panel"][buf][:, c*128:(c+1)*128],
                    xn_view[:, c*NT + half*512: c*NT + (half+1)*512],
                    start=(c == 0), stop=(c == NCH-1)), s_pe if (c == NCH-1 and half == 1) else None)
        T[f"dense_mm_done_{tag}"][oc] = s_pe.n
        epi_done[oc] = epilogue(oc, s_pe.n)
    return epi_done


def emit_routing(prog, nc, T, sb, has_bias=True):
    s_ld, s_pe, s_dve = T["s_ld"], T["s_pe"], T["s_dve"]
    prog.op("vector", lambda e: e.tensor_scalar(sb["cscol"][:, :], sb["cntsum"][:, :], 0.5, None, op0=ALU.is_gt), s_dve)
    v2 = s_dve.n
    prog.wait("tensor", s_dve, v2)
    prog.op("tensor", lambda e: e.transpose(T["p_s2t"][0:1, 0:64], sb["cscol"][:, :], sb["ident_f32"][0:64, 0:64]), s_pe)
    v3 = s_pe.n
    prog.wait("vector", s_pe, v3)
    prog.op("vector", lambda e: e.tensor_copy(sb["csrow1"][:, :], T["p_s2t"][0:1, 0:64]), s_dve)
    v4 = s_dve.n
    prog.wait("tensor", s_dve, v4)
    prog.op("tensor", lambda e: e.matmul(T["p_s1t"][:, 0:64], sb["ones_row_f32"][:, :], sb["csrow1"][:, :],
                                         start=True, stop=True), s_pe)
    for half in range(2):
        prog.op("tensor", lambda e, half=half: e.matmul(T["p_s2t"][:, half*512:(half+1)*512], sb["ones_row_f32"][:, :],
                                             sb["qs1r"][0:1, half*512:(half+1)*512], start=True, stop=True), s_pe if half == 1 else None)
    v5 = s_pe.n
    prog.wait("vector", s_pe, v5)
    prog.op("vector", lambda e: e.tensor_copy(sb["cs_bc"][:, :], T["p_s1t"][:, 0:64]), s_dve)
    prog.op("vector", lambda e: e.tensor_copy(sb["qs_bc"][:, :], T["p_s2t"][:, :]), s_dve)
    prog.op("vector", lambda e: e.tensor_copy(sb["asn_f"][:, :], sb["asn_i"][:, :]), s_dve)
    v6 = s_dve.n
    for c in range(NCH):
        prog.op("vector", lambda e, c=c: e.tensor_scalar(sb["eq"][:, :], sb["iota64"][:, :], sb["asn_f"][:, c:c+1], None, op0=ALU.is_equal), s_dve)
        prog.op("vector", lambda e: e.tensor_tensor(sb["eqp"][:, :], sb["eq"][:, :], sb["cs_bc"][:, :], op=ALU.mult), s_dve)
        prog.op("vector", lambda e, c=c: e.tensor_reduce(sb["rs05"][:, c:c+1], sb["eqp"][:, :], axis=AX.X, op=ALU.add), s_dve)
    prog.op("vector", lambda e: e.tensor_scalar(sb["rs05"][:, :], sb["rs05"][:, :], 0.5, None, op0=ALU.mult), s_dve)
    if has_bias:
        prog.op("vector", lambda e: e.tensor_tensor(sb["brs05"][:, :], sb["bf"][:, :], sb["rs05"][:, :], op=ALU.mult), s_dve)
    return s_dve.n, v6

def phase2(nc, prog, SEMS, EX):
    xT = EX["xT"]
    qs1row = EX["qs1_d"]
    murow1 = EX["mu1_d"]
    rrow1 = EX["r1_d"]
    Wf = EX["Wf_p"]; Wq = EX["Wq_p"]; Wk = EX["Wk_p"]; Wv = EX["Wv_p"]
    bf_p = EX["bf_pack"]; bq_p = EX["bq_pack"]; bk_p = EX["bk_pack"]; bv_p = EX["bv_pack"]
    asn_p = EX["assign_pack"]
    bwT = EX["bwT"]; Tmat = EX["Tmat"]; Dmat = EX["Dmat_tiled"]
    cns = EX["cns"]
    o_x2T = EX["x2T_d"]
    o_kT = EX["kT_loc"]
    o_vT = EX["vT_loc"]
    o_qhT = EX["qhT_d"]
    A_dram = EX["A_scr"]

    with ExitStack() as st:
        def sbuf(name, shape, dt):
            return st.enter_context(nc.sbuf_tensor("p2_" + name, shape, dt))
        sb = {}
        for n in ["ident_f32", "ones_col_f32", "ones_row_f32", "iota64"]:
            sb[n] = sbuf(n, list(CONSTS[n].shape), F32)
        for n in ["ident_bf16", "ones_col_bf16"]:
            sb[n] = sbuf(n, list(CONSTS[n].shape), BF16)
        sb["xn"] = sbuf("xn", [128, NCH*NT], BF16)
        sb["panel"] = [sbuf(f"panel{i}", [128, D], BF16) for i in range(2)]
        sb["xbuf"] = [sbuf(f"xbuf{i}", [128, NT], F32) for i in range(2)]
        sb["t1"] = [sbuf(f"t1_{i}", [128, NT], F32) for i in range(2)]
        sb["scr8k"] = sbuf("scr8k", [128, 2*NT], F32)
        sb["masked"] = [sb["scr8k"][:, 0:NT], sb["scr8k"][:, NT:2*NT]]
        sb["x1row"] = sb["xbuf"]
        sb["x2row"] = [sbuf(f"x2row{i}", [128, NT], F32) for i in range(2)]
        sb["sqf"] = [sbuf(f"sqf{i}", [128, NT], F32) for i in range(2)]
        sb["kvout"] = [sbuf(f"kvout{i}", [128, NT], BF16) for i in range(2)]
        sb["qout"] = [sbuf(f"qout{i}", [128, NT], BF16) for i in range(2)]
        sb["qhout"] = sb["kvout"]
        sb["qs_bc"] = sbuf("qs_bc", [128, NT], F32)
        sb["rbc"] = sbuf("rbc", [128, NT], F32)
        sb["nmbc"] = sbuf("nmbc", [128, NT], F32)
        sb["cntsum"] = sbuf("cntsum", [NCL, 1], F32)
        sb["cscol"] = sbuf("cscol", [NCL, 1], F32)
        sb["csrow1"] = sbuf("csrow1", [1, NCL], F32)
        sb["cs_bc"] = sbuf("cs_bc", [128, NCL], F32)
        sb["asn_i"] = sbuf("asn_i", [128, NCH], I32)
        sb["asn_f"] = sbuf("asn_f", [128, NCH], F32)
        sb["eq"] = sbuf("eq", [128, NCL], F32)
        sb["eqp"] = sbuf("eqp", [128, NCL], F32)
        sb["rs05"] = sbuf("rs05", [128, NCH], F32)
        sb["brs05"] = sbuf("brs05", [128, NCH], F32)
        sb["bf"] = sbuf("bf", [128, NCH], F32)
        sb["bq"] = sbuf("bq", [128, NCH], F32)
        sb["bk"] = sbuf("bk", [128, NCH], F32)
        sb["bv"] = sbuf("bv", [128, NCH], F32)
        sb["bwT"] = sbuf("bwT", [3, NH], F32)
        sb["Tmat"] = sbuf("Tmat", [3, NCL], F32)
        sb["Dmat"] = sbuf("Dmat", [NH, NCL], F32)
        sb["Aflat"] = sbuf("Aflat", [NH, NCL], F32)
        sb["BDf"] = sb["scr8k"]
        sb["BD"] = sbuf("BD", [128, NH*128], BF16)
        sb["mu1r"] = sbuf("mu1r", [1, NT], F32)
        sb["r1r"] = sbuf("r1r", [1, NT], F32)
        sb["qs1r"] = sbuf("qs1r", [1, NT], F32)
        for n in ["murow","rrow","nmrow","rowA","rowB"]:
            sb[n] = sbuf(n, [1, NT], F32)
        sb["eps1"] = sbuf("eps1", [1, 1], F32)

        T = {}
        T["p_d"] = [st.enter_context(nc.psum_tensor(f"p2_p_d{i}", [128, NT], F32)) for i in range(2)]
        T["p_s1t"] = st.enter_context(nc.psum_tensor("p2_p_s1t", [128, NT], F32))
        T["p_s2t"] = st.enter_context(nc.psum_tensor("p2_p_s2t", [128, NT], F32))
        T["p_s1"] = T["p_s1t"]
        T["p_s2"] = T["p_s2t"]
        T.update(SEMS)
        T["s_cast"] = SEMS["s_ld"]
        T["pe_done_chunk"] = {}
        T["act_done_chunk"] = {}
        for tag in ["d1", "wq", "wk", "wv"]:
            T[f"dense_mm_done_{tag}"] = {}

        s_ld, s_cast, s_pe, s_dve, s_act, s_st, s_gp = (T[k] for k in ["s_ld","s_cast","s_pe","s_dve","s_act","s_st","s_gp"])

        for n in ["ident_f32", "ones_col_f32", "ones_row_f32", "iota64", "ident_bf16", "ones_col_bf16"]:
            prog.dma("sync", sb[n][:, :], cns[n][:, :], s_ld)
        prog.wait("sync", T["s_cc1"], 1)
        prog.dma("sync", sb["cntsum"][:, :], EX["c1_sum"][:, :], s_ld)
        prog.dma("sync", sb["qs1r"][:, :], qs1row[:, :], s_ld)
        prog.dma("sync", sb["mu1r"][:, :], murow1[:, :], s_ld)
        prog.dma("sync", sb["r1r"][:, :], rrow1[:, :], s_ld)
        prog.dma("sync", sb["bf"][:, :], bf_p[:, :], s_ld)
        prog.dma("sync", sb["bq"][:, :], bq_p[:, :], s_ld)
        prog.dma("sync", sb["bk"][:, :], bk_p[:, :], s_ld)
        prog.dma("sync", sb["bv"][:, :], bv_p[:, :], s_ld)
        prog.dma("sync", sb["asn_i"][:, :], asn_p[:, :], s_ld)
        prog.dma("sync", sb["bwT"][:, :], bwT[:, :], s_ld)
        prog.dma("sync", sb["Tmat"][:, :], Tmat[:, :], s_ld)
        prog.dma("sync", sb["Dmat"][:, :], Dmat[:, :], s_ld)
        v_setup_ld = s_ld.n
        prog.op("vector", lambda e: e.memset(sb["eps1"][:, :], 1e-5), s_dve)

        prog.wait("vector", s_ld, v_setup_ld)
        emit_ln_apply(prog, nc, T, sb, xT, sb["mu1r"][:, :], sb["r1r"][:, :], "xn")
        v_xn1 = T["s_dve"].n

        prog.wait("tensor", s_ld, v_setup_ld)
        prog.wait("tensor", s_dve)
        v_mask_ready, v6 = emit_routing(prog, nc, T, sb, has_bias=True)

        prog.op("vector", lambda e: e.memset(sb["BDf"][:, :], 0.0), s_dve)
        v_bdz = s_dve.n
        prog.wait("tensor", s_ld, v_setup_ld)
        prog.wait("tensor", s_dve, v6)
        prog.op("tensor", lambda e: e.matmul(T["p_s1t"][0:NH, 64:128], sb["bwT"][:, :], sb["Tmat"][:, :], start=True, stop=True), s_pe)
        v_bd1 = s_pe.n
        prog.wait("vector", s_pe, v_bd1)
        prog.op("vector", lambda e: e.tensor_tensor(sb["Aflat"][:, :], T["p_s1t"][0:NH, 64:128], sb["Dmat"][:, :], op=ALU.add), s_dve)
        v_bd2 = s_dve.n
        prog.wait("sync", s_dve, v_bd2)
        v_bd3 = prog.dma("sync", A_dram[:, :], sb["Aflat"][:, :], T["s_bd"])
        prog.wait("sync", T["s_bd"], v_bd3)
        prog.wait("sync", s_dve, v_bdz)
        for h in range(NH):
            for d in range(16):
                prog.dma("sync", sb["BDf"][8*d:8*d+8, h*128+8*d:h*128+8*d+8],
                         A_dram[h:h+1, :].rearrange("o (i j) -> (o i) j", i=8), T["s_bd"])
        v_bd4 = T["s_bd"].n
        prog.wait("vector", T["s_bd"], v_bd4)
        prog.op("vector", lambda e: e.tensor_copy(sb["BD"][:, :], sb["BDf"][:, :]), s_dve)
        v_bd_ready = s_dve.n

        prog.barrier(["sync", "tensor", "vector", "scalar", "gpsimd"], [s_gp, s_dve])
        prog.wait("tensor", s_dve, v_xn1)
        prog.wait("scalar", s_dve, v_mask_ready)
        x2_state = {}
        stats_pend = {}
        def emit_stats(oc2):
            buf2 = oc2 % 2
            prog.wait("tensor", s_dve, stats_pend[oc2][0])
            prog.wait("tensor", s_act, stats_pend[oc2][1])
            for half in range(2):
                prog.op("tensor", lambda e, buf2=buf2, oc2=oc2, half=half: e.matmul(
                    T["p_s1t"][0:1, half*512:(half+1)*512], sb["ones_col_f32"][:, :],
                    sb["x2row"][buf2][:, half*512:(half+1)*512], start=(oc2 == 0), stop=(oc2 == NCH-1)), None)
                prog.op("tensor", lambda e, buf2=buf2, oc2=oc2, half=half: e.matmul(
                    T["p_s2t"][0:1, half*512:(half+1)*512], sb["ones_col_f32"][:, :],
                    sb["sqf"][buf2][:, half*512:(half+1)*512], start=(oc2 == 0), stop=(oc2 == NCH-1)),
                    s_pe if half == 1 else None)
            x2_state[oc2][2] = s_pe.n
        def epi_d1(oc, v_mm):
            buf = oc % 2
            if oc >= 2:
                emit_stats(oc - 2)
            prog.wait("scalar", s_pe, v_mm)
            prog.wait("scalar", s_dve, v_bd_ready)
            if oc >= 2:
                prog.wait("scalar", s_gp, x2_state[oc-2][0])
            prog.op("scalar", lambda e, buf=buf, oc=oc: e.activation(sb["masked"][buf][:, :], T["p_d"][buf][:, :],
                    AF.Identity, bias=sb["brs05"][:, oc:oc+1], scale=sb["rs05"][:, oc:oc+1]), s_act)
            v_epi = s_act.n
            if oc >= 2:
                prog.wait("sync", s_dve, x2_state[oc-2][1])
            v_x1 = prog.dma("sync", sb["x1row"][buf][:, :], xT[oc*128:(oc+1)*128, :], T["s_x0" if buf == 0 else "s_x1"])
            prog.wait("gpsimd", s_act, v_epi)
            if oc >= 2:
                prog.wait("gpsimd", s_dve, x2_state[oc-2][1])
            prog.op("gpsimd", lambda e, buf=buf: e.tensor_tensor(sb["t1"][buf][:, :], sb["masked"][buf][:, :], sb["qs_bc"][:, :], op=ALU.mult), s_gp)
            v_gp = s_gp.n
            prog.wait("vector", s_gp, v_gp)
            prog.wait("vector", T["s_x0" if buf == 0 else "s_x1"], v_x1)
            if oc >= 2:
                prog.wait("vector", s_pe, x2_state[oc-2][2])
                prog.wait("vector", s_act, stats_pend[oc-2][1])
                prog.wait("vector", T["s_st0" if buf == 0 else "s_st1"], x2_state[oc-2][3])
            prog.op("vector", lambda e, buf=buf: e.tensor_tensor(sb["x2row"][buf][:, :], sb["t1"][buf][:, :], sb["x1row"][buf][:, :], op=ALU.add), s_dve)
            v_x2 = s_dve.n
            prog.wait("sync", s_dve, v_x2)
            v_store = prog.dma("sync", o_x2T[oc*128:(oc+1)*128, :], sb["x2row"][buf][:, :], T["s_st0" if buf == 0 else "s_st1"])
            prog.wait("scalar", s_dve, v_x2)
            prog.op("scalar", lambda e, buf=buf: e.activation(sb["sqf"][buf][:, :], sb["x2row"][buf][:, :], AF.Square), s_act)
            v_sq = s_act.n
            stats_pend[oc] = (v_x2, v_sq)
            x2_state[oc] = [v_gp, v_x2, 0, v_store]
            if oc == NCH - 1:
                emit_stats(NCH - 2)
                emit_stats(NCH - 1)
            return ("s_act", v_epi)
        emit_dense(prog, nc, T, sb, Wf, sb["xn"][:, :], epi_d1, "d1")
        v_d1_stats = T["s_pe"].n

        prog.wait("vector", s_pe, v_d1_stats)
        prog.op("vector", lambda e: e.tensor_scalar(sb["murow"][:, :], T["p_s1t"][0:1, :], 1.0/D, None, op0=ALU.mult), s_dve)
        prog.op("vector", lambda e: e.tensor_scalar(sb["rowA"][:, :], T["p_s2t"][0:1, :], 1.0/D, None, op0=ALU.mult), s_dve)
        prog.op("vector", lambda e: e.tensor_tensor(sb["rowB"][:, :], sb["murow"][:, :], sb["murow"][:, :], op=ALU.mult), s_dve)
        prog.op("vector", lambda e: e.tensor_tensor(sb["rowA"][:, :], sb["rowA"][:, :], sb["rowB"][:, :], op=ALU.subtract), s_dve)
        v_var = s_dve.n
        prog.wait("scalar", s_dve, v_var)
        prog.op("scalar", lambda e: e.activation(sb["rowB"][:, :], sb["rowA"][:, :], AF.Ln, bias=sb["eps1"][:, :]), s_act)
        prog.op("scalar", lambda e: e.activation(sb["rrow"][:, :], sb["rowB"][:, :], AF.Exp, scale=-0.5), s_act)
        v_r2 = s_act.n

        prog.barrier(["sync", "tensor", "vector", "scalar", "gpsimd"], [s_gp, s_dve, s_pe, s_act])
        prog.wait("sync", T["s_st0"])
        prog.wait("sync", T["s_st1"])
        prog.wait("vector", s_act, v_r2)
        emit_ln_apply(prog, nc, T, sb, o_x2T, sb["murow"][:, :], sb["rrow"][:, :], "xn")
        v_xn2 = T["s_dve"].n

        prog.barrier(["sync", "tensor", "vector", "scalar", "gpsimd"], [s_gp, s_dve, s_pe, s_act])
        prog.wait("tensor", s_dve, v_xn2)
        def mk_epi_kv(bias_name, out_dram):
            state = {}
            def epi(oc, v_mm):
                buf = oc % 2
                prog.wait("scalar", s_pe, v_mm)
                if oc >= 2:
                    prog.wait("scalar", T["s_st0" if buf == 0 else "s_st1"], state[oc-2])
                prog.op("scalar", lambda e, buf=buf, oc=oc: e.activation(sb["kvout"][buf][:, :], T["p_d"][buf][:, :],
                        AF.Identity, bias=sb[bias_name][:, oc:oc+1]), s_act)
                v_epi = s_act.n
                prog.wait("sync", s_act, v_epi)
                v_store = prog.dma("sync", out_dram[oc*128:(oc+1)*128, :], sb["kvout"][buf][:, :], T["s_st0" if buf == 0 else "s_st1"])
                state[oc] = v_store
                return ("s_act", v_epi)
            return epi
        emit_dense(prog, nc, T, sb, Wk, sb["xn"][:, :], mk_epi_kv("bk", o_kT), "wk")
        prog.wait("gpsimd", T["s_st0"], T["s_st0"].n)
        prog.wait("gpsimd", T["s_st1"], T["s_st1"].n)
        prog.cc("AllGather", ALU.bypass, GROUPS4, EX["kT_loc"], EX["Kst_d"], T["s_cck"])
        prog.barrier(["sync", "tensor", "vector", "scalar"], [s_act, T["s_st0"], T["s_st1"], s_pe])
        emit_dense(prog, nc, T, sb, Wv, sb["xn"][:, :], mk_epi_kv("bv", o_vT), "wv")
        prog.wait("gpsimd", T["s_st0"], T["s_st0"].n)
        prog.wait("gpsimd", T["s_st1"], T["s_st1"].n)
        prog.cc("AllGather", ALU.bypass, GROUPS4, EX["vT_loc"], EX["Vst_d"], T["s_ccv"])
        prog.barrier(["sync", "tensor", "vector", "scalar"], [s_act, T["s_st0"], T["s_st1"], s_pe, s_dve])
        prog.wait("tensor", s_dve, v_bd_ready)
        q_state = {}
        def epi_q(oc, v_mm):
            buf = oc % 2
            h = oc // 2
            prog.wait("scalar", s_pe, v_mm)
            if oc >= 2:
                prog.wait("scalar", s_pe, q_state[oc-2][0])
            prog.op("scalar", lambda e, buf=buf, oc=oc: e.activation(sb["qout"][buf][:, :], T["p_d"][buf][:, :],
                    AF.Identity, bias=sb["bq"][:, oc:oc+1]), s_act)
            v_epi = s_act.n
            prog.wait("tensor", s_act, v_epi)
            if oc >= 1:
                prog.wait("tensor", s_dve, q_state[oc-1][1])
            for half in range(2):
                prog.op("tensor", lambda e, buf=buf, h=h, half=half: e.matmul(
                    T["p_s1t"][:, half*512:(half+1)*512], sb["BD"][:, h*128:(h+1)*128],
                    sb["qout"][buf][:, half*512:(half+1)*512], start=True, stop=True), s_pe if half == 1 else None)
            v_bdmm = s_pe.n
            prog.wait("vector", s_pe, v_bdmm)
            if oc >= 2:
                prog.wait("vector", T["s_st0" if buf == 0 else "s_st1"], q_state[oc-2][2])
            prog.op("vector", lambda e, buf=buf: e.tensor_copy(sb["qhout"][buf][:, :], T["p_s1t"][:, :]), s_dve)
            v_ev = s_dve.n
            prog.wait("sync", s_dve, v_ev)
            v_store = prog.dma("sync", o_qhT[oc*128:(oc+1)*128, :], sb["qhout"][buf][:, :], T["s_st0" if buf == 0 else "s_st1"])
            q_state[oc] = (v_bdmm, v_ev, v_store)
            return ("s_act", v_epi)
        emit_dense(prog, nc, T, sb, Wq, sb["xn"][:, :], epi_q, "wq")

def phase3(nc, prog, SEMS, EX):
    x2T = EX["x2T_d"]
    Kst = EX["Kst_d"]
    Vst = EX["Vst_d"]
    qhT = EX["qhT_d"]
    Wf = EX["Wf_p"]
    bf_p = EX["bf_pack"]
    cent = EX["cent_pack"]
    cns = EX["cns"]
    o_x3T = EX["x3T_d"]
    o_d2T = EX["d2T_d"]
    o_counts = EX["c3_loc"]
    o_qs = EX["qs3_d"]

    with ExitStack() as st:
        def sbuf(name, shape, dt):
            return st.enter_context(nc.sbuf_tensor("p3_" + name, shape, dt))
        sb = {}
        for n in ["ident_f32", "ones_col_f32", "ones_row_f32"]:
            sb[n] = sbuf(n, list(CONSTS[n].shape), F32)
        for n in ["ident_bf16", "ones_col_bf16"]:
            sb[n] = sbuf(n, list(CONSTS[n].shape), BF16)
        sb["cent"] = sbuf("cent", [128, NCH*NCL], F32)
        sb["xn"] = sbuf("xn", [128, NCH*NT], BF16)
        sb["K"] = sbuf("K", [128, 2*4096], BF16)
        sb["VT"] = sbuf("VT", [128, 2*4096], BF16)
        sb["Vtok"] = sbuf("Vtok", [128, NCH*DH], BF16)
        sb["Q"] = sbuf("Q", [128, 2*NT], BF16)
        sb["X2"] = sbuf("X2", [128, 2*NT], F32)
        sb["S"] = sbuf("S", [128, 4096], F32)
        sb["P"] = sbuf("P", [128, 4096], BF16)
        sb["P2"] = sbuf("P2", [128, 4096], BF16)
        _xnb = sb["xn"][:, :]
        sb["Sb"] = _xnb[:, 0:8192].bitcast(F32)
        sb["Pb"] = _xnb[:, 8192:12288]
        sb["P2b"] = _xnb[:, 12288:16384]
        sb["negmax2"] = sbuf("negmax2", [128, 1], F32)
        sb["z2"] = sbuf("z2", [128, 1], F32)
        sb["lnz2"] = sbuf("lnz2", [128, 1], F32)
        sb["zinv2"] = sbuf("zinv2", [128, 1], F32)
        sb["junk2"] = sbuf("junk2", [128, 1], F32)
        sb["PT"] = sbuf("PT", [128, NCH*DH], BF16)
        sb["negmax"] = sbuf("negmax", [128, 1], F32)
        sb["z"] = sbuf("z", [128, 1], F32)
        sb["lnz"] = sbuf("lnz", [128, 1], F32)
        sb["zinv"] = sbuf("zinv", [128, 1], F32)
        sb["junk"] = sbuf("junk", [128, 1], F32)
        sb["xbuf"] = [sb["S"][:, 0:NT], sb["S"][:, NT:2*NT]]
        _pf32 = sb["P"][:, :].bitcast(F32)
        sb["xtile"] = [_pf32[:, i*128:(i+1)*128] for i in range(4)]
        sb["sqf"] = [sb["S"][:, 2*NT:3*NT], sb["S"][:, 3*NT:4*NT]]
        sb["t1"] = [sbuf(f"t1_{i}", [128, NT], F32) for i in range(2)]
        kf32 = sb["K"][:, :].bitcast(F32)
        sb["rbc"] = kf32[:, 0:NT]
        sb["nmbc"] = kf32[:, NT:2*NT]
        sb["panel"] = [sb["P"][:, :], sb["P2"][:, :]]
        sb["d2out"] = sb["t1"]
        sb["bf"] = sbuf("bf", [128, NCH], F32)
        sb["csrow"] = sbuf("csrow", [1, NCL], F32)
        sb["csbc"] = sbuf("csbc", [128, NCL], F32)
        vtf32 = sb["VT"][:, :].bitcast(F32)
        vtokf32 = sb["Vtok"][:, :].bitcast(F32)
        sb["murow"] = vtf32[0:1, 0:NT]; sb["rrow"] = vtf32[0:1, NT:2*NT]
        sb["nmrow"] = vtf32[0:1, 2*NT:3*NT]; sb["negr10row"] = vtf32[0:1, 3*NT:4*NT]
        sb["s1row"] = vtokf32[0:1, 0:NT]; sb["s2row"] = vtokf32[0:1, NT:2*NT]
        sb["msqrow"] = vtokf32[0:1, 2*NT:3*NT]; sb["m2row"] = vtokf32[0:1, 3*NT:4*NT]
        sb["varrow"] = sb["msqrow"]; sb["lnrow"] = sb["m2row"]
        sb["eps1"] = sbuf("eps1", [1, 1], F32)
        sb["mucols"] = sbuf("mucols", [128, NTT], F32)
        sb["nr10cols"] = sbuf("nr10cols", [128, NTT], F32)
        sb["lg"] = sbuf("lg", [128, NCL], F32)
        sb["pd"] = sbuf("pd", [128, NCL], F32)
        sb["z001"] = sbuf("z001", [128, 1], F32)
        sb["hits"] = sbuf("hits", [128, NCL], BF16)
        sb["qs_all"] = sbuf("qs_all", [128, NTT], F32)
        sb["cnt"] = sbuf("cnt", [64, 1], F32)
        sb["rowcols"] = sbuf("rowcols", [16, 128], F32)
        sb["qsT"] = sbuf("qsT", [NTT, 128], F32)

        T = {}
        T["scr_dram"] = EX["scr_rows3"]
        T["p_q"] = [st.enter_context(nc.psum_tensor(f"p3_p_q{i}", [128, NT], F32)) for i in range(2)]
        T["p_av"] = st.enter_context(nc.psum_tensor("p3_p_av", [128, NT], F32))
        T["p_t"] = [st.enter_context(nc.psum_tensor(f"p3_p_t{i}", [128, NT], BF16)) for i in range(2)]
        T["p_s1"] = T["p_q"][0]
        T["p_s2"] = T["p_q"][1]
        T["p_s1t"] = T["p_q"][0]
        T["p_s2t"] = T["p_q"][1]
        T["p_dots"] = [T["p_av"][:, 0:512], T["p_av"][:, 512:1024]]
        T["p_scr"] = T["p_t"][0][:, :].bitcast(F32)
        T["p_scr2"] = T["p_t"][1][:, :].bitcast(F32)[:, 0:256]
        T["p_counts"] = T["p_t"][1][:, :].bitcast(F32)[:, 256:512]
        T["p_d"] = T["p_q"]
        T.update(SEMS)
        T["s_cast"] = SEMS["s_ld"]
        T["pe_done_chunk"] = {}
        T["act_done_chunk"] = {}
        T["dense_mm_done_d2"] = {}
        s_ld, s_pe, s_dve, s_act, s_st, s_gp = (T[k] for k in ["s_ld","s_pe","s_dve","s_act","s_st","s_gp"])
        s_h = [T["s_h0"], T["s_h1"]]

        for n in ["ident_f32", "ones_col_f32", "ones_row_f32", "ident_bf16", "ones_col_bf16"]:
            prog.dma("sync", sb[n][:, :], cns[n][:, :], s_ld)
        prog.dma("sync", sb["cent"][:, :], cent[:, :], s_ld)
        prog.dma("sync", sb["bf"][:, :], bf_p[:, :], s_ld)
        v_setup = s_ld.n
        prog.op("vector", lambda e: e.memset(sb["eps1"][:, :], 1e-5), s_dve)

        prog.wait("tensor", s_ld, v_setup)
        head_done = {}
        for h in range(NH):
            s_vt = T["s_vt0" if h % 2 == 0 else "s_vt1"]
            s_kq = T["s_kq0" if h % 2 == 0 else "s_kq1"]
            s_xh = T["s_xh0" if h % 2 == 0 else "s_xh1"]
            if h >= 1:
                prog.wait("sync", s_pe, head_done[h-1]["vtrans"])
            for dc in range(2):
                for r2 in range(4):
                    prog.dma("sync", sb["VT"][:, dc*4096 + r2*1024 : dc*4096 + (r2+1)*1024],
                             Vst[r2*D + h*DH + dc*128 : r2*D + h*DH + (dc+1)*128, :], s_vt)
            if h >= 1:
                prog.wait("sync", s_pe, head_done[h-1]["lastS"])
            for dc in range(2):
                for r2 in range(4):
                    prog.dma("sync", sb["K"][:, dc*4096 + r2*1024 : dc*4096 + (r2+1)*1024],
                             Kst[r2*D + h*DH + dc*128 : r2*D + h*DH + (dc+1)*128, :], s_kq)
                prog.dma("sync", sb["Q"][:, dc*NT:(dc+1)*NT], qhT[h*DH + dc*128 : h*DH + (dc+1)*128, :], s_kq)
            if h >= 1:
                prog.wait("sync", T["s_st0" if (h-1) % 2 == 0 else "s_st1"], head_done[h-1]["store"])
            for dc in range(2):
                prog.dma("sync", sb["X2"][:, dc*NT:(dc+1)*NT], x2T[h*DH + dc*128 : h*DH + (dc+1)*128, :], s_xh)
            v_vt, v_kq, v_xh = s_vt.n, s_kq.n, s_xh.n

            prog.wait("tensor", s_vt, v_vt)
            vt_evac = {}
            if h >= 1:
                prog.wait("vector", s_pe, head_done[h-1]["av"])
            for rnd in range(8):
                pt = T["p_t"][rnd % 2]
                if rnd >= 2:
                    prog.wait("tensor", s_act, vt_evac[rnd-2])
                for j in range(8):
                    g = rnd*8 + j
                    mc, dc = g // 2, g % 2
                    prog.op("tensor", lambda e, pt=pt, j=j, mc=mc, dc=dc: e.transpose(
                        pt[:, j*128:(j+1)*128],
                        sb["VT"][:, dc*4096 + mc*128 : dc*4096 + (mc+1)*128],
                        sb["ident_bf16"][:, :]), s_pe if j == 7 else None)
                v_t = s_pe.n
                prog.wait("scalar", s_pe, v_t)
                prog.op("scalar", lambda e, pt=pt, rnd=rnd: e.activation(
                    sb["Vtok"][:, rnd*1024:(rnd+1)*1024], pt[:, 0:1024], AF.Identity), s_act)
                vt_evac[rnd] = s_act.n
            v_vtok = s_act.n
            T["v_vtrans_end"] = s_pe.n
            T.pop("v_max_prev", None); T.pop("v_pscale_prev", None); T.pop("v_ptrans_prev", None); T.pop("v_exp_prev", None)
            st_pscale = {}
            st_exp = {}
            st_max = {}
            st_evac = {}
            st_pt = {}
            st_av = {}
            v_resid = {}
            def SS(qt): return sb["S"][:, :] if qt % 2 == 0 else sb["Sb"]
            def PP(qt): return sb["P"][:, :] if qt % 2 == 0 else sb["Pb"]
            def PP2(qt): return sb["P2"][:, :] if qt % 2 == 0 else sb["P2b"]
            def NM(qt): return sb["negmax"][:, :] if qt % 2 == 0 else sb["negmax2"][:, :]
            def ZZ(qt): return (sb["z"][:, :], sb["lnz"][:, :], sb["zinv"][:, :], sb["junk"][:, :]) if qt % 2 == 0 else (sb["z2"][:, :], sb["lnz2"][:, :], sb["zinv2"][:, :], sb["junk2"][:, :])
            def emit_S(qt):
                prog.wait("tensor", s_kq, v_kq)
                for qq in range(4):
                    pq = T["p_q"][qq % 2]
                    key = f"pq_reader_{qq % 2}"
                    if key in T:
                        eng, val = T[key]
                        prog.wait("tensor", T[eng], val)
                    for msub in range(2):
                        for dc in range(2):
                            prog.op("tensor", lambda e, pq=pq, msub=msub, dc=dc, qq=qq, qt=qt: e.matmul(
                                pq[:, msub*512:(msub+1)*512],
                                sb["Q"][:, dc*NT + qt*128 : dc*NT + (qt+1)*128],
                                sb["K"][:, dc*4096 + qq*1024 + msub*512 : dc*4096 + qq*1024 + (msub+1)*512],
                                start=(dc == 0), stop=(dc == 1)), s_pe if (msub == 1 and dc == 1) else None)
                    v_s = s_pe.n
                    if qq % 2 == 0:
                        prog.wait("vector", s_pe, v_s)
                        prog.wait("vector", s_act, st_exp.get(qt-2, 0))
                        prog.op("vector", lambda e, pq=pq, qq=qq, qt=qt: e.tensor_copy(SS(qt)[:, qq*1024:(qq+1)*1024], pq[:, :]), s_dve)
                        T[key] = ("s_dve", s_dve.n)
                    else:
                        prog.wait("scalar", s_pe, v_s)
                        prog.wait("scalar", s_act, st_exp.get(qt-2, 0))
                        prog.op("scalar", lambda e, pq=pq, qq=qq, qt=qt: e.activation(SS(qt)[:, qq*1024:(qq+1)*1024], pq[:, :], AF.Identity), s_act)
                        T[key] = ("s_act", s_act.n)
                st_evac[qt] = (s_dve.n, s_act.n)
            def emit_chain(qt):
                prog.wait("vector", s_act, st_evac[qt][1])
                prog.wait("vector", s_dve, st_evac[qt][0])
                prog.wait("vector", s_act, st_exp.get(qt-2, 0))
                prog.op("vector", lambda e, qt=qt: e.tensor_reduce(NM(qt), SS(qt), axis=AX.X, op=ALU.max, negate=True), s_dve)
                st_max[qt] = s_dve.n
                z_, lnz_, zinv_, junk_ = ZZ(qt)
                prog.wait("scalar", s_dve, st_max[qt])
                prog.wait("scalar", s_gp, st_pscale.get(qt-2, 0))
                prog.op("scalar", lambda e, qt=qt, z_=z_: e.activation(PP(qt), SS(qt), AF.Exp, bias=NM(qt), accum_out=z_), None)
                prog.op("scalar", lambda e, z_=z_, lnz_=lnz_: e.activation(lnz_, z_, AF.Ln), None)
                prog.op("scalar", lambda e, lnz_=lnz_, zinv_=zinv_: e.activation(zinv_, lnz_, AF.Exp, scale=-1.0), None)
                prog.op("scalar", lambda e, zinv_=zinv_, junk_=junk_: e.activation(junk_, zinv_, AF.Identity), s_act)
                st_exp[qt] = s_act.n
                prog.wait("gpsimd", s_act, st_exp[qt])
                prog.wait("gpsimd", s_pe, st_pt.get(qt-2, 0))
                prog.op("gpsimd", lambda e, qt=qt, zinv_=zinv_: e.tensor_scalar(PP2(qt), PP(qt), zinv_, 1.0, op0=ALU.mult, op1=ALU.mult), s_gp)
                st_pscale[qt] = s_gp.n
            def emit_PT(qt):
                prog.wait("tensor", s_gp, st_pscale[qt])
                prog.wait("tensor", s_pe, st_av.get(qt // 2 - 1, 0))
                for rnd in range(4):
                    pt = T["p_t"][rnd % 2]
                    prog.wait("tensor", s_act, T.get(f"pt_evac_{rnd % 2}", vt_evac[6 + rnd % 2]))
                    for j in range(8):
                        mc = rnd*8 + j
                        prog.op("tensor", lambda e, pt=pt, j=j, mc=mc, qt=qt: e.transpose(
                            pt[:, j*128:(j+1)*128],
                            PP2(qt)[:, mc*128:(mc+1)*128],
                            sb["ident_bf16"][:, :]), s_pe if j == 7 else None)
                    v_tr = s_pe.n
                    prog.wait("scalar", s_pe, v_tr)
                    prog.op("scalar", lambda e, pt=pt, rnd=rnd, qt=qt: e.activation(
                        sb["PT"][:, :].rearrange("p (mc q) -> p mc q", q=DH)[:, rnd*8:(rnd+1)*8, (qt % 2)*128:(qt % 2 + 1)*128],
                        pt[:, 0:1024].rearrange("p (j q) -> p j q", q=128), AF.Identity), s_act)
                    T[f"pt_evac_{rnd % 2}"] = s_act.n
                st_pt[qt] = s_pe.n
                st_pt[(qt, "dve")] = s_act.n
            def emit_AV(qst):
                prog.wait("tensor", s_act, st_pt[(qst*2+1, "dve")])
                if qst - 1 in v_resid:
                    prog.wait("tensor", s_dve, v_resid[qst-1])
                for mc in range(NCH):
                    for dc in range(2):
                        prog.op("tensor", lambda e, mc=mc, dc=dc: e.matmul(
                            T["p_av"][:, dc*512 : dc*512 + 256],
                            sb["Vtok"][:, mc*DH + dc*128 : mc*DH + (dc+1)*128],
                            sb["PT"][:, mc*DH : (mc+1)*DH],
                            start=(mc == 0), stop=(mc == NCH-1)), s_pe if (mc == NCH-1 and dc == 1) else None)
                st_av[qst] = s_pe.n
                prog.wait("vector", s_pe, st_av[qst])
                prog.wait("vector", s_xh, v_xh)
                for dc in range(2):
                    prog.op("vector", lambda e, dc=dc, qst=qst: e.tensor_tensor(
                        sb["X2"][:, dc*NT + qst*256 : dc*NT + (qst+1)*256],
                        T["p_av"][:, dc*512 : dc*512 + 256],
                        sb["X2"][:, dc*NT + qst*256 : dc*NT + (qst+1)*256], op=ALU.add), s_dve)
                v_resid[qst] = s_dve.n
            for idx in range(9):
                if idx < 8:
                    emit_S(idx)
                    if idx == 7:
                        T["v_lastS"] = s_pe.n
                if idx >= 1:
                    emit_PT(idx-1)
                if idx < 8:
                    emit_chain(idx)
                if idx >= 2 and idx % 2 == 0:
                    emit_AV(idx // 2 - 1)
            v_resid_prev = v_resid[3]
            prog.wait("sync", s_dve, v_resid_prev)
            sst = T["s_st0" if h % 2 == 0 else "s_st1"]
            for dc in range(2):
                prog.dma("sync", o_x3T[h*DH + dc*128 : h*DH + (dc+1)*128, :], sb["X2"][:, dc*NT:(dc+1)*NT], sst)
            head_done[h] = {"vtrans": T["v_vtrans_end"], "lastS": T["v_lastS"],
                            "av": s_pe.n, "resid": s_dve.n, "store": sst.n}

        prog.barrier(["sync", "tensor", "vector", "scalar", "gpsimd"],
                     [s_pe, s_dve, s_act, T["s_st0"], T["s_st1"], T["s_h0"], T["s_h1"]])
        v_done3 = emit_ln_dots(prog, nc, T, sb, o_x3T, sb, with_dots=True) if False else None
        v_done3 = emit_ln_dots(prog, nc, T, sb if False else T, o_x3T, sb, with_dots=True) if False else emit_ln_dots(prog, nc, T, o_x3T, sb, with_dots=True)
        prog.wait("vector", s_pe, v_done3)
        prog.op("vector", lambda e: e.tensor_copy(sb["cnt"][:, :], T["p_counts"][0:64, 0:1]), s_dve)
        v_cnt = s_dve.n
        prog.wait("tensor", s_pe, v_done3)
        prog.op("tensor", lambda e: e.transpose(T["p_scr"][0:NTT, 0:128], sb["qs_all"][:, :], sb["ident_f32"][:, :]), s_pe)
        v_qt = s_pe.n
        prog.wait("vector", s_pe, v_qt)
        prog.op("vector", lambda e: e.tensor_copy(sb["qsT"][:, :], T["p_scr"][0:NTT, 0:128]), s_dve)
        v_qsT = s_dve.n
        prog.wait("sync", s_dve, v_cnt)
        v_cst3 = prog.dma("sync", o_counts[:, :], sb["cnt"][:, :], s_st)
        prog.wait("sync", s_dve, v_qsT)
        prog.dma("sync", o_qs[0:1, :].rearrange("a (b p) -> (a b) p", p=128), sb["qsT"][:, :], s_st)
        prog.wait("gpsimd", s_st, v_cst3)
        prog.cc("AllReduce", ALU.add, [[0, 1, 2, 3, 4, 5, 6, 7]], EX["c3_loc"], EX["c3_sum"], T["s_cc3"])

        prog.barrier(["sync", "tensor", "vector", "scalar", "gpsimd"], [s_pe, s_dve, s_act, s_gp])
        emit_ln_apply(prog, nc, T, sb, o_x3T, sb["murow"][:, :], sb["rrow"][:, :], "xn")
        v_xn3 = s_dve.n

        prog.barrier(["sync", "tensor", "vector", "scalar", "gpsimd"], [s_pe, s_dve, s_act, s_gp])
        prog.wait("tensor", s_dve, v_xn3)
        d2_state = {}
        def epi_d2(oc, v_mm):
            buf = oc % 2
            prog.wait("scalar", s_pe, v_mm)
            if oc >= 2:
                prog.wait("scalar", T["s_st0" if buf == 0 else "s_st1"], d2_state[oc-2])
            prog.op("scalar", lambda e, buf=buf, oc=oc: e.activation(sb["d2out"][buf][:, :], T["p_d"][buf][:, :],
                    AF.Identity, bias=sb["bf"][:, oc:oc+1]), s_act)
            v_epi = s_act.n
            prog.wait("sync", s_act, v_epi)
            v_store = prog.dma("sync", o_d2T[oc*128:(oc+1)*128, :], sb["d2out"][buf][:, :], T["s_st0" if buf == 0 else "s_st1"])
            d2_state[oc] = v_store
            return ("s_act", v_epi)
        emit_dense(prog, nc, T, sb, Wf, sb["xn"][:, :], epi_d2, "d2")

def phase4(nc, prog, SEMS, EX):
    x3T = EX["x3T_d"]
    d2T = EX["d2T_d"]
    qs3row = EX["qs3_d"]
    asn_p = EX["assign_pack"]
    cns = EX["cns"]
    out = EX["out_tok"]

    with ExitStack() as st:
        def sbuf(name, shape, dt):
            return st.enter_context(nc.sbuf_tensor("p4_" + name, shape, dt))
        sb = {}
        for n in ["ident_f32", "ones_col_f32", "ones_row_f32", "iota64"]:
            sb[n] = sbuf(n, list(CONSTS[n].shape), F32)
        sb["cntsum"] = sbuf("cntsum", [NCL, 1], F32)
        sb["qs1r"] = sbuf("qs1r", [1, NT], F32)
        sb["cscol"] = sbuf("cscol", [NCL, 1], F32)
        sb["csrow1"] = sbuf("csrow1", [1, NCL], F32)
        sb["cs_bc"] = sbuf("cs_bc", [128, NCL], F32)
        sb["qs_bc"] = sbuf("qs_bc", [128, NT], F32)
        sb["asn_i"] = sbuf("asn_i", [128, NCH], I32)
        sb["asn_f"] = sbuf("asn_f", [128, NCH], F32)
        sb["eq"] = sbuf("eq", [128, NCL], F32)
        sb["eqp"] = sbuf("eqp", [128, NCL], F32)
        sb["rs05"] = sbuf("rs05", [128, NCH], F32)
        sb["d2row"] = [sbuf(f"d2row{i}", [128, NT], F32) for i in range(2)]
        sb["x3row"] = [sbuf(f"x3row{i}", [128, NT], F32) for i in range(2)]
        sb["m1"] = [sbuf(f"m1_{i}", [128, NT], F32) for i in range(2)]
        sb["x4"] = [sbuf(f"x4_{i}", [128, NT], F32) for i in range(4)]
        sb["obuf"] = [sbuf(f"obuf{i}", [128, 512], F32) for i in range(2)]

        T = {}
        T["p_s1t"] = st.enter_context(nc.psum_tensor("p4_p_s1t", [128, NT], F32))
        T["p_s2t"] = st.enter_context(nc.psum_tensor("p4_p_s2t", [128, NT], F32))
        T["p_o"] = [st.enter_context(nc.psum_tensor(f"p4_p_o{i}", [128, 512], F32)) for i in range(2)]
        T.update(SEMS)
        s_ld, s_pe, s_dve, s_act, s_st, s_gp = (T[k] for k in ["s_ld","s_pe","s_dve","s_act","s_st","s_gp"])
        s_x = [T["s_x0"], T["s_x1"]]
        s_sto = [T["s_st0"], T["s_st1"]]

        for n in ["ident_f32", "ones_col_f32", "ones_row_f32", "iota64"]:
            prog.dma("sync", sb[n][:, :], cns[n][:, :], s_ld)
        prog.wait("sync", T["s_cc3"], 1)
        prog.dma("sync", sb["cntsum"][:, :], EX["c3_sum"][:, :], s_ld)
        prog.dma("sync", sb["qs1r"][:, :], qs3row[:, :], s_ld)
        prog.dma("sync", sb["asn_i"][:, :], asn_p[:, :], s_ld)
        v_setup = s_ld.n
        prog.wait("tensor", s_ld, v_setup)
        prog.wait("vector", s_ld, v_setup)
        v_mask, _ = emit_routing(prog, nc, T, sb, has_bias=False)

        ostate = {}
        last_m1 = {}
        last_x4 = {}
        last_gp = {}
        for g in range(8):
            x4done = {}
            for i in range(4):
                oc = g*4 + i
                buf = oc % 2
                if buf in last_gp:
                    prog.wait("sync", s_gp, last_gp[buf])
                if buf in last_m1:
                    prog.wait("sync", s_act, last_m1[buf])
                if buf in last_x4:
                    prog.wait("sync", s_dve, last_x4[buf])
                v_d2 = prog.dma("sync", sb["d2row"][buf][:, :], d2T[oc*128:(oc+1)*128, :], s_x[buf])
                v_x3 = prog.dma("sync", sb["x3row"][buf][:, :], x3T[oc*128:(oc+1)*128, :], s_x[buf])
                prog.wait("scalar", s_x[buf], v_x3)
                prog.wait("scalar", s_dve, v_mask)
                if buf in last_gp:
                    prog.wait("scalar", s_gp, last_gp[buf])
                prog.op("scalar", lambda e, buf=buf, oc=oc: e.activation(sb["m1"][buf][:, :], sb["d2row"][buf][:, :],
                        AF.Identity, scale=sb["rs05"][:, oc:oc+1]), s_act)
                v_m1 = s_act.n
                last_m1[buf] = v_m1
                prog.wait("gpsimd", s_act, v_m1)
                if buf in last_x4:
                    prog.wait("gpsimd", s_dve, last_x4[buf])
                prog.op("gpsimd", lambda e, buf=buf: e.tensor_tensor(sb["m1"][buf][:, :], sb["m1"][buf][:, :], sb["qs_bc"][:, :], op=ALU.mult), s_gp)
                v_gp = s_gp.n
                last_gp[buf] = v_gp
                prog.wait("vector", s_gp, v_gp)
                prog.wait("vector", s_x[buf], v_x3)
                if g > 0:
                    prog.wait("vector", s_pe, ostate[g-1][2])
                prog.op("vector", lambda e, buf=buf, i=i: e.tensor_tensor(sb["x4"][i][:, :], sb["m1"][buf][:, :], sb["x3row"][buf][:, :], op=ALU.add), s_dve)
                x4done[i] = s_dve.n
                last_x4[buf] = s_dve.n
            v_tr_last = 0
            for tt in range(NTT):
                po = T["p_o"][tt % 2]
                prog.wait("tensor", s_dve, x4done[3])
                if tt >= 2:
                    prog.wait("tensor", s_dve, T.get(f"o_evac_{tt-2}_{g}", 0))
                elif g > 0:
                    prog.wait("tensor", s_dve, T.get(f"o_evac_{NTT-2+tt}_{g-1}", 0))
                for i in range(4):
                    prog.op("tensor", lambda e, po=po, i=i, tt=tt: e.transpose(
                        po[:, i*128:(i+1)*128], sb["x4"][i][:, tt*128:(tt+1)*128], sb["ident_f32"][:, :]),
                        s_pe if i == 3 else None)
                v_tr = s_pe.n
                v_tr_last = v_tr
                prog.wait("vector", s_pe, v_tr)
                if tt >= 2:
                    prog.wait("vector", s_sto[tt % 2], T.get(f"o_store_{tt-2}_{g}", 0))
                elif g > 0:
                    prog.wait("vector", s_sto[tt % 2], T.get(f"o_store_{NTT-2+tt}_{g-1}", 0))
                prog.op("vector", lambda e, po=po, tt=tt: e.tensor_copy(sb["obuf"][tt % 2][:, :], po[:, :]), s_dve)
                v_ev = s_dve.n
                T[f"o_evac_{tt}_{g}"] = v_ev
                prog.wait("sync", s_dve, v_ev)
                v_st = prog.dma("sync", out[tt*128:(tt+1)*128, g*512:(g+1)*512], sb["obuf"][tt % 2][:, :], s_sto[tt % 2])
                T[f"o_store_{tt}_{g}"] = v_st
            ostate[g] = (s_gp.n, s_dve.n, v_tr_last)
        prog.wait("sync", T["s_st0"])
        prog.wait("sync", T["s_st1"])

GROUPS4 = [[0, 1, 2, 3], [4, 5, 6, 7]]

def build_fused():
    nc = bass.Bass("TRN2", target_bir_lowering=False)
    EX = {}
    EX["xT"] = nc.dram_tensor("xT", [D, NT], F32, kind="ExternalInput").ap()
    EX["cent_pack"] = nc.dram_tensor("cent_pack", [128, NCH*NCL], F32, kind="ExternalInput").ap()
    for w in ["Wf_p", "Wq_p", "Wk_p", "Wv_p"]:
        EX[w] = nc.dram_tensor(w, [NCH, 128, D], F32, kind="ExternalInput").ap()
    for b in ["bf_pack", "bq_pack", "bk_pack", "bv_pack"]:
        EX[b] = nc.dram_tensor(b, [128, NCH], F32, kind="ExternalInput").ap()
    EX["assign_pack"] = nc.dram_tensor("assign_pack", [128, NCH], I32, kind="ExternalInput").ap()
    EX["bwT"] = nc.dram_tensor("bwT", [3, NH], F32, kind="ExternalInput").ap()
    EX["Tmat"] = nc.dram_tensor("Tmat", [3, NCL], F32, kind="ExternalInput").ap()
    EX["Dmat_tiled"] = nc.dram_tensor("Dmat_tiled", [NH, NCL], F32, kind="ExternalInput").ap()
    EX["cns"] = declare_consts(nc, ["ident_f32", "ident_bf16", "ones_col_f32", "ones_col_bf16",
                                    "ones_row_f32", "iota64"])
    EX["out_tok"] = nc.dram_tensor("out_tok", [NT, D], F32, kind="ExternalOutput").ap()
    EX["c1_loc"] = nc.dram_tensor("c1_loc", [NCL, 1], F32, kind="Internal").ap()
    EX["c1_sum"] = nc.dram_tensor("c1_sum", [NCL, 1], F32, kind="Internal").ap()
    EX["c3_loc"] = nc.dram_tensor("c3_loc", [NCL, 1], F32, kind="Internal").ap()
    EX["c3_sum"] = nc.dram_tensor("c3_sum", [NCL, 1], F32, kind="Internal").ap()
    EX["mu1_d"] = nc.dram_tensor("mu1_d", [1, NT], F32, kind="Internal").ap()
    EX["r1_d"] = nc.dram_tensor("r1_d", [1, NT], F32, kind="Internal").ap()
    EX["qs1_d"] = nc.dram_tensor("qs1_d", [1, NT], F32, kind="Internal").ap()
    EX["qs3_d"] = nc.dram_tensor("qs3_d", [1, NT], F32, kind="Internal").ap()
    EX["x2T_d"] = nc.dram_tensor("x2T_d", [D, NT], F32, kind="Internal").ap()
    EX["kT_loc"] = nc.dram_tensor("kT_loc", [D, NT], BF16, kind="Internal").ap()
    EX["vT_loc"] = nc.dram_tensor("vT_loc", [D, NT], BF16, kind="Internal").ap()
    EX["qhT_d"] = nc.dram_tensor("qhT_d", [D, NT], BF16, kind="Internal").ap()
    EX["Kst_d"] = nc.dram_tensor("Kst_d", [4*D, NT], BF16, kind="Internal").ap()
    EX["Vst_d"] = nc.dram_tensor("Vst_d", [4*D, NT], BF16, kind="Internal").ap()
    EX["x3T_d"] = nc.dram_tensor("x3T_d", [D, NT], F32, kind="Internal").ap()
    EX["d2T_d"] = nc.dram_tensor("d2T_d", [D, NT], F32, kind="Internal").ap()
    EX["scr_rows1"] = nc.dram_tensor("scr_rows1", [2, NT], F32, kind="Internal").ap()
    EX["scr_rows3"] = nc.dram_tensor("scr_rows3", [2, NT], F32, kind="Internal").ap()
    EX["A_scr"] = nc.dram_tensor("A_scr", [NH, NCL], F32, kind="Internal").ap()

    prog = Prog()
    SEMS = {n: prog.sem(n) for n in ALL_SEM_NAMES}
    prog.canon = {"vector": SEMS["s_dve"], "scalar": SEMS["s_act"], "gpsimd": SEMS["s_gp"]}
    with ExitStack() as st_outer:
        phase1(nc, prog, SEMS, EX)
        phase_barrier(prog)
        phase2(nc, prog, SEMS, EX)
        phase_barrier(prog)
        phase3(nc, prog, SEMS, EX)
        phase_barrier(prog)
        phase4(nc, prog, SEMS, EX)
        phase_barrier(prog)
        alloc_sems(nc, prog, st_outer)
        prog.replay(nc)
    return nc

_CACHE = {}

def _tables():
    g = [1.0, 1.0, 1.0]; dim = 3; n = 8
    masks = sorted(range(n), key=lambda m: (bin(m).count('1'), m))
    inv = {m: i for i, m in enumerate(masks)}
    C = np.zeros((n, n, n), dtype=np.float64)
    for i, a in enumerate(masks):
        for j, b in enumerate(masks):
            sw, at = (0, a >> 1)
            while at:
                sw += bin(at & b).count('1'); at >>= 1
            s = -1.0 if sw & 1 else 1.0
            for t in range(dim):
                if a >> t & 1 and b >> t & 1:
                    s *= float(g[t])
            C[i, j, inv[a ^ b]] = s
    grades = np.array([bin(m).count('1') for m in masks])
    rev = np.where(grades * (grades - 1) // 2 % 2 == 1, -1.0, 1.0)
    met = np.array([float(np.prod([g[t] for t in range(dim) if m >> t & 1] + [1.0])) for m in masks])
    return C, rev, met

def kernel(L0_attn_x, L0_attn_Wq, L0_attn_bq, L0_attn_Wk, L0_attn_bk, L0_attn_Wv, L0_attn_bv,
           L0_attn_bias_weight, L0_ffn_weight, L0_ffn_bias, L0_ffn_centroids, L0_ffn_assignments):
    x = np.asarray(L0_attn_x, dtype=np.float32)
    Wq = np.asarray(L0_attn_Wq, dtype=np.float32); bq = np.asarray(L0_attn_bq, dtype=np.float32)
    Wk = np.asarray(L0_attn_Wk, dtype=np.float32); bk = np.asarray(L0_attn_bk, dtype=np.float32)
    Wv = np.asarray(L0_attn_Wv, dtype=np.float32); bv = np.asarray(L0_attn_bv, dtype=np.float32)
    bw = np.asarray(L0_attn_bias_weight, dtype=np.float32)
    Wf = np.asarray(L0_ffn_weight, dtype=np.float32); bff = np.asarray(L0_ffn_bias, dtype=np.float32)
    cent = np.asarray(L0_ffn_centroids, dtype=np.float32)
    assign = np.asarray(L0_ffn_assignments)

    C, rev, met = _tables()
    biv = C[:, :, [4, 5, 6]]
    scale = 1.0 / np.sqrt(DH)
    Tm = np.stack([(rev[:, None] * biv[:, :, c]).reshape(64) for c in range(3)]).astype(np.float32)
    Dm = np.tile(((rev[:, None] * (scale * np.diag(met))).reshape(64))[None, :], (NH, 1)).astype(np.float32)

    if "nc" not in _CACHE:
        _CACHE["nc"] = build_fused()
    nc = _CACHE["nc"]

    shards_tok = [np.ascontiguousarray(x.reshape(8, NT, D)[c]) for c in range(8)]
    shards = [np.ascontiguousarray(s.T) for s in shards_tok]
    cent_pack = pack_cent(cent)
    consts = const_inputs(["ident_f32", "ident_bf16", "ones_col_f32", "ones_col_bf16", "ones_row_f32", "iota64"])
    asn_pack = np.ascontiguousarray(assign.reshape(NCH, 128).T).astype(np.int32)
    packs = dict(Wf_p=pack_w_panels(Wf), Wq_p=pack_w_panels(Wq), Wk_p=pack_w_panels(Wk), Wv_p=pack_w_panels(Wv),
                 bf_pack=pack_col(bff), bq_pack=pack_col(bq), bk_pack=pack_col(bk), bv_pack=pack_col(bv),
                 assign_pack=asn_pack, bwT=np.ascontiguousarray(bw.T), Tmat=Tm, Dmat_tiled=Dm,
                 cent_pack=cent_pack)

    ins = [dict(xT=shards[c], **packs, **consts) for c in range(8)]
    r = run_bass_kernel_spmd(nc, ins, core_ids=list(range(8))).results

    out = np.concatenate([np.asarray(r[c]["out_tok"]) for c in range(8)], axis=0)
    return out.reshape(2, 4096, D).astype(np.float32)


# version 06
# speedup vs baseline: 1.0665x; 1.0665x over previous
import numpy as np
import ml_dtypes
import concourse.bass as bass
import concourse.mybir as mybir
from concourse.bass_utils import run_bass_kernel_spmd
from contextlib import ExitStack

F32 = mybir.dt.float32
BF16 = mybir.dt.bfloat16
I32 = mybir.dt.int32
AF = mybir.ActivationFunctionType
ALU = mybir.AluOpType
AX = mybir.AxisListType

NT = 1024
D = 4096
NCH = 32
NTT = 8
NCL = 64
NH = 16
DH = 256

class Sem:
    def __init__(self, name):
        self.name = name; self.n = 0; self.h = None

class Prog:
    ENGS = ("sync", "tensor", "vector", "scalar", "gpsimd")
    def __init__(self):
        self.q = {e: [] for e in self.ENGS}
        self.sems = []
        self.hwm = {e: {} for e in self.ENGS}
        self.canon = {}
    def sem(self, name):
        s = Sem(name); self.sems.append(s); return s
    def raw(self, eng, fn):
        self.q[eng].append(fn)
    def wait(self, eng, sem, val=None):
        v = sem.n if val is None else val
        if v <= 0: return
        if self.hwm[eng].get(sem.name, 0) >= v: return
        self.hwm[eng][sem.name] = v
        self.q[eng].append(lambda e, s=sem, v=v: e.wait_ge(s.h, v))
    def dma(self, eng, out, in_, sem, **kw):
        sem.n += 16
        self.q[eng].append(lambda e, o=out, i=in_, s=sem, kw=dict(kw): e.dma_start(out=o, in_=i, **kw).then_inc(s.h, 16))
        return sem.n
    def cc(self, kind, op, groups, in_ap, out_ap, sem):
        sem.n = 1
        self.q["gpsimd"].append(lambda e, k=kind, o=op, g=groups, i=in_ap, a=out_ap, s=sem:
                                e.collective_compute(k, o, replica_groups=g, ins=[i.opt()], outs=[a.opt()]).then_inc(s.h))
        return 1
    def op_acc(self, eng, builder, sem, fence):
        c = self.canon.get(eng)
        if c is not None and c.n > 0:
            self.q[eng].append(lambda e, s=c, v=c.n: e.wait_ge(s.h, v))
        self.q[eng].append(lambda e, b=builder: b(e))
        sem.n += 1
        self.q[eng].append(lambda e, f=fence, s=sem: f(e).then_inc(s.h, 1))
        return sem.n
    def op(self, eng, builder, sem=None):
        c = self.canon.get(eng)
        if c is not None:
            if sem is None:
                sem = c
            assert sem is c, f"compute op on {eng} must use canonical sem"
            if c.n > 0:
                self.q[eng].append(lambda e, s=c, v=c.n: e.wait_ge(s.h, v))
        if sem is not None:
            sem.n += 1
        self.q[eng].append(lambda e, b=builder, s=sem: (b(e).then_inc(s.h, 1) if s is not None else b(e)))
        return None if sem is None else sem.n
    def barrier(self, engs, sems):
        for e in engs:
            for s in sems:
                self.wait(e, s)
    def replay(self, nc):
        for s in self.sems:
            pass
        with nc.Block() as block:
            for eng in self.ENGS:
                ops = self.q[eng]
                if not ops: continue
                dec = getattr(block, eng)
                def make(ops):
                    def body(e):
                        for fn in ops:
                            fn(e)
                    return body
                dec(make(ops))

def alloc_sems(nc, prog, stack):
    for s in prog.sems:
        if s.n > 0:
            s.h = stack.enter_context(nc.semaphore(s.name))

ALL_SEM_NAMES = ["s_ld", "s_pe", "s_dve", "s_act", "s_st", "s_gp",
                 "s_x0", "s_x1", "s_c0", "s_c1", "s_st0", "s_st1", "s_rows", "s_bd",
                 "s_h0", "s_h1", "s_xt0", "s_xt1", "s_xt2", "s_xt3",
                 "s_vt0", "s_vt1", "s_kq0", "s_kq1", "s_xh0", "s_xh1",
                 "s_cc1", "s_cck", "s_ccv", "s_cc3"]

def phase_barrier(prog):
    for e in Prog.ENGS:
        for s in prog.sems:
            if s.name.startswith("s_cc"):
                continue
            prog.wait(e, s)

def pack_w_panels(W):
    Wt = np.ascontiguousarray(W.T)
    t = Wt.reshape(32, 128, 32, 128)
    t = np.ascontiguousarray(t.transpose(2, 1, 0, 3))
    return t.reshape(32, 128, 4096)

def pack_col(b):
    return np.ascontiguousarray(b.reshape(32, 128).T)

def pack_cent(cent):
    centT = cent.T
    t = centT.reshape(32, 128, 64)
    return np.ascontiguousarray(t.transpose(1, 0, 2)).reshape(128, 32*64)

CONSTS = {}
def make_consts():
    CONSTS["ident_f32"] = np.eye(128, dtype=np.float32)
    CONSTS["ident_bf16"] = np.eye(128, dtype=ml_dtypes.bfloat16)
    CONSTS["ones_col_f32"] = np.ones((128, 1), np.float32)
    CONSTS["ones_col_bf16"] = np.ones((128, 1), ml_dtypes.bfloat16)
    CONSTS["ones_row_f32"] = np.ones((1, 128), np.float32)
    CONSTS["iota64"] = np.tile(np.arange(64, dtype=np.float32)[None, :], (128, 1))
make_consts()

def declare_consts(nc, names):
    return {n: nc.dram_tensor("c_" + n, list(CONSTS[n].shape),
                              F32 if CONSTS[n].dtype == np.float32 else BF16,
                              kind="ExternalInput").ap() for n in names}

def const_inputs(names):
    return {"c_" + n: CONSTS[n] for n in names}

def emit_ln_dots(prog, nc, T, xT_dram, sb, with_dots=True):
    s_ld, s_cast, s_pe, s_dve, s_act = T["s_ld"], T["s_cast"], T["s_pe"], T["s_dve"], T["s_act"]
    s_x = [T["s_x0"], T["s_x1"]]
    s_rows = T["s_rows"]
    if with_dots:
        prog.wait("tensor", s_ld)
        for c in range(NCH):
            prog.op("tensor", lambda e, c=c: e.matmul(
                T["p_scr"][0:1, 0:64], sb["ones_col_f32"][:, :], sb["cent"][:, c*64:(c+1)*64],
                start=(c == 0), stop=(c == NCH-1)), s_pe if c == NCH-1 else None)
        v_pe_csum = s_pe.n
        prog.wait("vector", s_pe, v_pe_csum)
        prog.op("vector", lambda e: e.tensor_copy(sb["csrow"][:, :], T["p_scr"][0:1, 0:64]), s_dve)
        v_csrow = s_dve.n
        prog.wait("tensor", s_dve, v_csrow)
        prog.op("tensor", lambda e: e.matmul(T["p_scr2"][:, 0:64], sb["ones_row_f32"][:, :], sb["csrow"][:, :],
                                             start=True, stop=True), s_pe)
        v_pe_csbc = s_pe.n
        prog.wait("vector", s_pe, v_pe_csbc)
        prog.op("vector", lambda e: e.tensor_copy(sb["csbc"][:, :], T["p_scr2"][:, 0:64]), s_dve)
        v_csbc = s_dve.n
        prog.wait("tensor", s_dve, v_csbc)

    ld_base = s_ld.n
    cast_base = s_cast.n
    for c in range(NCH):
        buf = c % 2
        if c >= 2:
            prog.wait("sync", s_pe, T["pe_done_chunk"][c-2])
        v_ld = prog.dma("sync", sb["xbuf"][buf][:, :], xT_dram[c*128:(c+1)*128, :], s_x[buf])
        prog.wait("scalar", s_x[buf], v_ld)
        if c >= 2:
            prog.wait("scalar", s_pe, T["pe_done_chunk"][c-2])
        prog.op("scalar", lambda e, buf=buf: e.activation(sb["sqf"][buf][:, :], sb["xbuf"][buf][:, :], AF.Square), s_act)
        T["act_done_chunk"][c] = s_act.n
        prog.wait("tensor", s_x[buf], v_ld)
        prog.wait("tensor", s_act, T["act_done_chunk"][c])
        for half in range(2):
            prog.op("tensor", lambda e, c=c, buf=buf, half=half: e.matmul(
                T["p_s1"][0:1, half*512:(half+1)*512], sb["ones_col_f32"][:, :],
                sb["xbuf"][buf][:, half*512:(half+1)*512], start=(c == 0), stop=(c == NCH-1)), None)
            prog.op("tensor", lambda e, c=c, buf=buf, half=half: e.matmul(
                T["p_s2"][0:1, half*512:(half+1)*512], sb["ones_col_f32"][:, :],
                sb["sqf"][buf][:, half*512:(half+1)*512], start=(c == 0), stop=(c == NCH-1)),
                None if half == 0 else s_pe)
        T["pe_done_chunk"][c] = s_pe.n
    v_pe_all = s_pe.n

    prog.wait("vector", s_pe, v_pe_all)
    prog.op("vector", lambda e: e.tensor_copy(sb["s1row"][:, :], T["p_s1"][0:1, :]), s_dve)
    prog.op("vector", lambda e: e.tensor_copy(sb["s2row"][:, :], T["p_s2"][0:1, :]), s_dve)
    prog.op("vector", lambda e: e.tensor_scalar(sb["murow"][:, :], sb["s1row"][:, :], 1.0/D, None, op0=ALU.mult), s_dve)
    prog.op("vector", lambda e: e.tensor_scalar(sb["msqrow"][:, :], sb["s2row"][:, :], 1.0/D, None, op0=ALU.mult), s_dve)
    prog.op("vector", lambda e: e.tensor_tensor(sb["m2row"][:, :], sb["murow"][:, :], sb["murow"][:, :], op=ALU.mult), s_dve)
    prog.op("vector", lambda e: e.tensor_tensor(sb["varrow"][:, :], sb["msqrow"][:, :], sb["m2row"][:, :], op=ALU.subtract), s_dve)
    v_var = s_dve.n
    prog.wait("scalar", s_dve, v_var)
    prog.op("scalar", lambda e: e.activation(sb["lnrow"][:, :], sb["varrow"][:, :], AF.Ln, bias=sb["eps1"][:, :]), s_act)
    prog.op("scalar", lambda e: e.activation(sb["rrow"][:, :], sb["lnrow"][:, :], AF.Exp, scale=-0.5), s_act)
    v_rrow = s_act.n
    if not with_dots:
        return v_rrow

    prog.wait("vector", s_act, v_rrow)
    prog.op("vector", lambda e: e.tensor_scalar(sb["negr10row"][:, :], sb["rrow"][:, :], -10.0, None, op0=ALU.mult), s_dve)
    v_negr10 = s_dve.n
    prog.wait("sync", s_dve, v_negr10)
    prog.dma("sync", T["scr_dram"][0:1, :], sb["murow"][:, :], s_rows)
    v_w1 = prog.dma("sync", T["scr_dram"][1:2, :], sb["negr10row"][:, :], s_rows)
    prog.wait("sync", s_rows, v_w1)
    v_rc = prog.dma("sync", sb["rowcols"][:, :], T["scr_dram"].rearrange("a (b p) -> (a b) p", p=128)[:, :], s_rows)
    prog.wait("tensor", s_rows, v_rc)
    prog.op("tensor", lambda e: e.transpose(T["p_scr"][:, 16:32], sb["rowcols"][0:16, :], sb["ident_f32"][0:16, 0:16]), s_pe)
    v_pe_cols = s_pe.n
    prog.wait("vector", s_pe, v_pe_cols)
    prog.op("vector", lambda e: e.tensor_copy(sb["mucols"][:, :], T["p_scr"][:, 16:24]), s_dve)
    prog.op("vector", lambda e: e.tensor_copy(sb["nr10cols"][:, :], T["p_scr"][:, 24:32]), s_dve)
    v_cols = s_dve.n

    v_pd_prev = 0
    v_hits_prev = 0
    v_counts_prev = 0
    v_lg_prev = {}
    for t in range(NTT):
        pd_bank = T["p_dots"][t % 2]
        NXT = len(sb["xtile"])
        prog.wait("tensor", s_dve, v_cols)
        if t >= 2:
            prog.wait("tensor", s_dve, v_lg_prev[t-2])
        for c in range(NCH):
            tb = c % NXT
            gi = t*NCH + c
            if gi >= NXT:
                pg, pc = divmod(gi - NXT, NCH)
                prog.wait("sync", s_pe, T[f"dots_mm_{pg}_{pc}"])
            v_tl = prog.dma("sync", sb["xtile"][tb][:, :], xT_dram[c*128:(c+1)*128, t*128:(t+1)*128],
                            T[f"s_xt{tb}"])
            prog.wait("tensor", T[f"s_xt{tb}"], v_tl)
            prog.op("tensor", lambda e, c=c, tb=tb, pd_bank=pd_bank: e.matmul(
                pd_bank[:, 0:64], sb["xtile"][tb][:, :], sb["cent"][:, c*64:(c+1)*64],
                start=(c == 0), stop=(c == NCH-1)), s_pe)
            T[f"dots_mm_{t}_{c}"] = s_pe.n
        v_dots_t = s_pe.n
        prog.wait("vector", s_pe, v_dots_t)
        if t > 0:
            prog.wait("vector", s_act, v_pd_prev)
        prog.op("vector", lambda e, t=t, pd_bank=pd_bank: e.scalar_tensor_tensor(
            sb["lg"][:, :], sb["csbc"][:, :], sb["mucols"][:, t:t+1], pd_bank[:, 0:64],
            op0=ALU.mult, op1=ALU.subtract), s_dve)
        prog.op("vector", lambda e, t=t: e.tensor_scalar(sb["lg"][:, :], sb["lg"][:, :], sb["nr10cols"][:, t:t+1], None, op0=ALU.mult), s_dve)
        v_lg_prev[t] = s_dve.n
        prog.op("vector", lambda e: e.tensor_reduce(sb["negmax"][:, :], sb["lg"][:, :], axis=AX.X, op=ALU.max, negate=True), s_dve)
        v_lg = s_dve.n
        prog.wait("scalar", s_dve, v_lg)
        if t > 0:
            prog.wait("scalar", s_dve, v_hits_prev)
        prog.op("scalar", lambda e: e.activation(sb["pd"][:, :], sb["lg"][:, :], AF.Exp, bias=sb["negmax"][:, :], accum_out=sb["z"][:, :]), None)
        prog.op("scalar", lambda e: e.activation(sb["z001"][:, :], sb["z"][:, :], AF.Identity, scale=0.01), None)
        prog.op("scalar", lambda e: e.activation(sb["junk"][:, :], sb["z001"][:, :], AF.Identity), s_act)
        v_pd = s_act.n
        prog.wait("vector", s_act, v_pd)
        if t > 0:
            prog.wait("vector", s_pe, v_counts_prev)
        prog.op("vector", lambda e, t=t: e.tensor_scalar(sb["hits"][:, :], sb["pd"][:, :], sb["z001"][:, :], None, op0=ALU.is_gt,
                                                          op1=ALU.max, accum_out=sb["qs_all"][:, t:t+1]), s_dve)
        v_hits_prev = s_dve.n
        v_pd_prev = v_pd
        v_hits = s_dve.n
        prog.wait("tensor", s_dve, v_hits)
        prog.op("tensor", lambda e, t=t: e.matmul(T["p_counts"][0:64, 0:1], sb["hits"][:, :], sb["ones_col_bf16"][:, :],
                                                  start=(t == 0), stop=(t == NTT-1)), s_pe)
        v_counts_prev = s_pe.n
    return s_pe.n

def phase1(nc, prog, SEMS, EX):
    xT = EX["xT"]
    cent = EX["cent_pack"]
    cns = EX["cns"]
    o_counts = EX["c1_loc"]
    o_mu = EX["mu1_d"]
    o_r = EX["r1_d"]
    o_qs = EX["qs1_d"]

    with ExitStack() as st:
        def sbuf(name, shape, dt):
            return st.enter_context(nc.sbuf_tensor("p1_"+name, shape, dt))
        sb = {}
        sb["cent"] = sbuf("cent", [128, NCH*NCL], F32)
        sb["ident_f32"] = sbuf("identf", [128, 128], F32)
        sb["ones_col_f32"] = sbuf("onescf", [128, 1], F32)
        sb["ones_col_bf16"] = sbuf("onescb", [128, 1], BF16)
        sb["ones_row_f32"] = sbuf("onesr", [1, 128], F32)
        sb["xbuf"] = [sbuf(f"xbuf{i}", [128, NT], F32) for i in range(2)]
        sb["sqf"] = [sbuf(f"sqf{i}", [128, NT], F32) for i in range(2)]
        sb["xtile"] = [sbuf(f"xtile{i}", [128, 128], F32) for i in range(4)]
        for n in ["csrow"]: sb[n] = sbuf(n, [1, NCL], F32)
        sb["csbc"] = sbuf("csbc", [128, NCL], F32)
        for n in ["s1row","s2row","murow","msqrow","m2row","varrow","lnrow","rrow","negr10row"]:
            sb[n] = sbuf(n, [1, NT], F32)
        sb["eps1"] = sbuf("eps1", [1, 1], F32)
        sb["mucols"] = sbuf("mucols", [128, NTT], F32)
        sb["nr10cols"] = sbuf("nr10cols", [128, NTT], F32)
        sb["lg"] = sbuf("lg", [128, NCL], F32)
        sb["negmax"] = sbuf("negmax", [128, 1], F32)
        sb["pd"] = sbuf("pd", [128, NCL], F32)
        sb["z"] = sbuf("z", [128, 1], F32)
        sb["zinv"] = sbuf("zinv", [128, 1], F32)
        sb["dn"] = sbuf("dn", [128, NCL], F32)
        sb["hits"] = sbuf("hits", [128, NCL], BF16)
        sb["qs_all"] = sbuf("qs_all", [128, NTT], F32)
        sb["cnt"] = sbuf("cnt", [64, 1], F32)
        sb["junk"] = sbuf("junk", [128, 1], F32)
        sb["z001"] = sbuf("z001", [128, 1], F32)
        sb["rowcols"] = sbuf("rowcols", [16, 128], F32)
        sb["qsT"] = sbuf("qsT", [NTT, 128], F32)

        T = {}
        T["scr_dram"] = EX["scr_rows1"]
        T["p_s1"] = st.enter_context(nc.psum_tensor("p1_p_s1", [1, NT], F32))
        T["p_s2"] = st.enter_context(nc.psum_tensor("p1_p_s2", [1, NT], F32))
        _pd = st.enter_context(nc.psum_tensor("p1_p_dots", [128, 1024], F32))
        T["p_dots"] = [_pd[:, 0:512], _pd[:, 512:1024]]
        T["p_scr"] = st.enter_context(nc.psum_tensor("p1_p_scr", [128, 512], F32))
        _ps2 = st.enter_context(nc.psum_tensor("p1_p_scr2", [128, 512], F32))
        T["p_scr2"] = _ps2[:, 0:256]
        T["p_counts"] = _ps2[:, 256:512]
        T.update(SEMS)
        T["s_cast"] = SEMS["s_ld"]
        T["pe_done_chunk"] = {}
        T["act_done_chunk"] = {}

        prog.dma("sync", sb["cent"][:, :], cent[:, :], T["s_ld"])
        prog.dma("sync", sb["ident_f32"][:, :], cns["ident_f32"][:, :], T["s_ld"])
        prog.dma("sync", sb["ones_col_f32"][:, :], cns["ones_col_f32"][:, :], T["s_ld"])
        prog.dma("sync", sb["ones_col_bf16"][:, :], cns["ones_col_bf16"][:, :], T["s_ld"])
        prog.dma("sync", sb["ones_row_f32"][:, :], cns["ones_row_f32"][:, :], T["s_ld"])
        prog.op("vector", lambda e: e.memset(sb["eps1"][:, :], 1e-5), T["s_dve"])

        v_done = emit_ln_dots(prog, nc, T, xT, sb, with_dots=True)

        prog.wait("vector", T["s_pe"], v_done)
        prog.op("vector", lambda e: e.tensor_copy(sb["cnt"][:, :], T["p_counts"][0:64, 0:1]), T["s_dve"])
        v_cnt = T["s_dve"].n
        prog.wait("tensor", T["s_pe"], v_done)
        prog.op("tensor", lambda e: e.transpose(T["p_scr"][0:NTT, 0:128], sb["qs_all"][:, :], sb["ident_f32"][:, :]), T["s_pe"])
        v_qt = T["s_pe"].n
        prog.wait("vector", T["s_pe"], v_qt)
        prog.op("vector", lambda e: e.tensor_copy(sb["qsT"][:, :], T["p_scr"][0:NTT, 0:128]), T["s_dve"])
        v_qsT = T["s_dve"].n
        prog.wait("sync", T["s_dve"], v_cnt)
        v_cst = prog.dma("sync", o_counts[:, :], sb["cnt"][:, :], T["s_st"])
        prog.dma("sync", o_mu[:, :], sb["murow"][:, :], T["s_st"])
        prog.dma("sync", o_r[:, :], sb["rrow"][:, :], T["s_st"])
        prog.wait("sync", T["s_dve"], v_qsT)
        prog.dma("sync", o_qs[0:1, :].rearrange("a (b p) -> (a b) p", p=128), sb["qsT"][:, :], T["s_st"])

        prog.wait("gpsimd", T["s_st"], v_cst)
        prog.cc("AllReduce", ALU.add, [[0, 1, 2, 3, 4, 5, 6, 7]], EX["c1_loc"], EX["c1_sum"], T["s_cc1"])

def emit_ln_apply(prog, nc, T, sb, xT_dram, murow, rrow, xn_sb_name="xn"):
    s_ld, s_pe, s_dve, s_act, s_gp = T["s_ld"], T["s_pe"], T["s_dve"], T["s_act"], T["s_gp"]
    s_x = [T["s_x0"], T["s_x1"]]
    prog.op("vector", lambda e: e.scalar_tensor_tensor(sb["nmrow"][:, :], murow, -1.0, rrow, op0=ALU.mult, op1=ALU.mult), s_dve)
    v_nm = s_dve.n
    prog.wait("tensor", s_dve, v_nm)
    for half in range(2):
        prog.op("tensor", lambda e, half=half: e.matmul(T["p_s1t"][:, half*512:(half+1)*512], sb["ones_row_f32"][:, :],
                                                        rrow[0:1, half*512:(half+1)*512], start=True, stop=True), None)
        prog.op("tensor", lambda e, half=half: e.matmul(T["p_s2t"][:, half*512:(half+1)*512], sb["ones_row_f32"][:, :],
                                                        sb["nmrow"][0:1, half*512:(half+1)*512], start=True, stop=True), s_pe if half == 1 else None)
    v_bc = s_pe.n
    prog.wait("vector", s_pe, v_bc)
    prog.op("vector", lambda e: e.tensor_copy(sb["rbc"][:, :], T["p_s1t"][:, :]), s_dve)
    prog.op("vector", lambda e: e.tensor_copy(sb["nmbc"][:, :], T["p_s2t"][:, :]), s_dve)
    xn = sb[xn_sb_name]
    done_mark = {}
    for c in range(NCH):
        buf = c % 2
        if c >= 2:
            prog.wait("sync", s_dve, done_mark[c-2][0])
        v_ld = prog.dma("sync", sb["xbuf"][buf][:, :], xT_dram[c*128:(c+1)*128, :], s_x[buf])
        prog.wait("vector", s_x[buf], v_ld)
        prog.op("vector", lambda e, buf=buf: e.tensor_tensor(sb["t1"][buf][:, :], sb["xbuf"][buf][:, :], sb["nmbc"][:, :], op=ALU.add), s_dve)
        v_add = s_dve.n
        prog.op("vector", lambda e, buf=buf, c=c: e.tensor_tensor(xn[:, c*NT:(c+1)*NT], sb["t1"][buf][:, :], sb["rbc"][:, :], op=ALU.mult), s_dve)
        done_mark[c] = (v_add, s_dve.n)
    return s_dve.n

def emit_dense(prog, nc, T, sb, W_dram, xn_view, epilogue, tag):
    s_pe = T["s_pe"]
    s_c = [T["s_c0"], T["s_c1"]]
    prog.wait("gpsimd", s_pe)
    epi_done = {}
    for oc in range(NCH):
        buf = oc % 2
        if oc >= 2:
            prog.wait("gpsimd", s_pe, T[f"dense_mm_done_{tag}"][oc-2])
        v_cast = prog.dma("gpsimd", sb["panel"][buf][:, :], W_dram[oc, :, :], s_c[buf], max_dma_last_dim=8192)
        prog.wait("tensor", s_c[buf], v_cast)
        if oc >= 2:
            eng, val = epi_done[oc-2]
            prog.wait("tensor", T[eng], val)
        for c in range(NCH):
            for half in range(2):
                prog.op("tensor", lambda e, buf=buf, c=c, half=half: e.matmul(
                    T["p_d"][buf][:, half*512:(half+1)*512],
                    sb["panel"][buf][:, c*128:(c+1)*128],
                    xn_view[:, c*NT + half*512: c*NT + (half+1)*512],
                    start=(c == 0), stop=(c == NCH-1)), s_pe if (c == NCH-1 and half == 1) else None)
        T[f"dense_mm_done_{tag}"][oc] = s_pe.n
        epi_done[oc] = epilogue(oc, s_pe.n)
    return epi_done


def emit_routing(prog, nc, T, sb, has_bias=True):
    s_ld, s_pe, s_dve = T["s_ld"], T["s_pe"], T["s_dve"]
    prog.op("vector", lambda e: e.tensor_scalar(sb["cscol"][:, :], sb["cntsum"][:, :], 0.5, None, op0=ALU.is_gt), s_dve)
    v2 = s_dve.n
    prog.wait("tensor", s_dve, v2)
    prog.op("tensor", lambda e: e.transpose(T["p_s2t"][0:1, 0:64], sb["cscol"][:, :], sb["ident_f32"][0:64, 0:64]), s_pe)
    v3 = s_pe.n
    prog.wait("vector", s_pe, v3)
    prog.op("vector", lambda e: e.tensor_copy(sb["csrow1"][:, :], T["p_s2t"][0:1, 0:64]), s_dve)
    v4 = s_dve.n
    prog.wait("tensor", s_dve, v4)
    prog.op("tensor", lambda e: e.matmul(T["p_s1t"][:, 0:64], sb["ones_row_f32"][:, :], sb["csrow1"][:, :],
                                         start=True, stop=True), s_pe)
    for half in range(2):
        prog.op("tensor", lambda e, half=half: e.matmul(T["p_s2t"][:, half*512:(half+1)*512], sb["ones_row_f32"][:, :],
                                             sb["qs1r"][0:1, half*512:(half+1)*512], start=True, stop=True), s_pe if half == 1 else None)
    v5 = s_pe.n
    prog.wait("vector", s_pe, v5)
    prog.op("vector", lambda e: e.tensor_copy(sb["cs_bc"][:, :], T["p_s1t"][:, 0:64]), s_dve)
    prog.op("vector", lambda e: e.tensor_copy(sb["qs_bc"][:, :], T["p_s2t"][:, :]), s_dve)
    prog.op("vector", lambda e: e.tensor_copy(sb["asn_f"][:, :], sb["asn_i"][:, :]), s_dve)
    v6 = s_dve.n
    for c in range(NCH):
        prog.op("vector", lambda e, c=c: e.tensor_scalar(sb["eq"][:, :], sb["iota64"][:, :], sb["asn_f"][:, c:c+1], None, op0=ALU.is_equal), s_dve)
        prog.op("vector", lambda e: e.tensor_tensor(sb["eqp"][:, :], sb["eq"][:, :], sb["cs_bc"][:, :], op=ALU.mult), s_dve)
        prog.op("vector", lambda e, c=c: e.tensor_reduce(sb["rs05"][:, c:c+1], sb["eqp"][:, :], axis=AX.X, op=ALU.add), s_dve)
    prog.op("vector", lambda e: e.tensor_scalar(sb["rs05"][:, :], sb["rs05"][:, :], 0.5, None, op0=ALU.mult), s_dve)
    if has_bias:
        prog.op("vector", lambda e: e.tensor_tensor(sb["brs05"][:, :], sb["bf"][:, :], sb["rs05"][:, :], op=ALU.mult), s_dve)
    return s_dve.n, v6

def phase2(nc, prog, SEMS, EX):
    xT = EX["xT"]
    qs1row = EX["qs1_d"]
    murow1 = EX["mu1_d"]
    rrow1 = EX["r1_d"]
    Wf = EX["Wf_p"]; Wq = EX["Wq_p"]; Wk = EX["Wk_p"]; Wv = EX["Wv_p"]
    bf_p = EX["bf_pack"]; bq_p = EX["bq_pack"]; bk_p = EX["bk_pack"]; bv_p = EX["bv_pack"]
    asn_p = EX["assign_pack"]
    bwT = EX["bwT"]; Tmat = EX["Tmat"]; Dmat = EX["Dmat_tiled"]
    cns = EX["cns"]
    o_x2T = EX["x2T_d"]
    o_kT = EX["kT_loc"]
    o_vT = EX["vT_loc"]
    o_qhT = EX["qhT_d"]
    A_dram = EX["A_scr"]

    with ExitStack() as st:
        def sbuf(name, shape, dt):
            return st.enter_context(nc.sbuf_tensor("p2_" + name, shape, dt))
        sb = {}
        for n in ["ident_f32", "ones_col_f32", "ones_row_f32", "iota64"]:
            sb[n] = sbuf(n, list(CONSTS[n].shape), F32)
        for n in ["ident_bf16", "ones_col_bf16"]:
            sb[n] = sbuf(n, list(CONSTS[n].shape), BF16)
        sb["xn"] = sbuf("xn", [128, NCH*NT], BF16)
        sb["panel"] = [sbuf(f"panel{i}", [128, D], BF16) for i in range(2)]
        sb["xbuf"] = [sbuf(f"xbuf{i}", [128, NT], F32) for i in range(2)]
        sb["t1"] = [sbuf(f"t1_{i}", [128, NT], F32) for i in range(2)]
        sb["scr8k"] = sbuf("scr8k", [128, 2*NT], F32)
        sb["masked"] = [sb["scr8k"][:, 0:NT], sb["scr8k"][:, NT:2*NT]]
        sb["x1row"] = sb["xbuf"]
        sb["x2row"] = [sbuf(f"x2row{i}", [128, NT], F32) for i in range(2)]
        sb["sqf"] = [sbuf(f"sqf{i}", [128, NT], F32) for i in range(2)]
        sb["kvout"] = [sbuf(f"kvout{i}", [128, NT], BF16) for i in range(2)]
        sb["qout"] = [sbuf(f"qout{i}", [128, NT], BF16) for i in range(2)]
        sb["qhout"] = sb["kvout"]
        sb["qs_bc"] = sbuf("qs_bc", [128, NT], F32)
        sb["rbc"] = sbuf("rbc", [128, NT], F32)
        sb["nmbc"] = sbuf("nmbc", [128, NT], F32)
        sb["cntsum"] = sbuf("cntsum", [NCL, 1], F32)
        sb["cscol"] = sbuf("cscol", [NCL, 1], F32)
        sb["csrow1"] = sbuf("csrow1", [1, NCL], F32)
        sb["cs_bc"] = sbuf("cs_bc", [128, NCL], F32)
        sb["asn_i"] = sbuf("asn_i", [128, NCH], I32)
        sb["asn_f"] = sbuf("asn_f", [128, NCH], F32)
        sb["eq"] = sbuf("eq", [128, NCL], F32)
        sb["eqp"] = sbuf("eqp", [128, NCL], F32)
        sb["rs05"] = sbuf("rs05", [128, NCH], F32)
        sb["brs05"] = sbuf("brs05", [128, NCH], F32)
        sb["bf"] = sbuf("bf", [128, NCH], F32)
        sb["bq"] = sbuf("bq", [128, NCH], F32)
        sb["bk"] = sbuf("bk", [128, NCH], F32)
        sb["bv"] = sbuf("bv", [128, NCH], F32)
        sb["bwT"] = sbuf("bwT", [3, NH], F32)
        sb["Tmat"] = sbuf("Tmat", [3, NCL], F32)
        sb["Dmat"] = sbuf("Dmat", [NH, NCL], F32)
        sb["Aflat"] = sbuf("Aflat", [NH, NCL], F32)
        sb["BDf"] = sb["scr8k"]
        sb["BD"] = sbuf("BD", [128, NH*128], BF16)
        sb["mu1r"] = sbuf("mu1r", [1, NT], F32)
        sb["r1r"] = sbuf("r1r", [1, NT], F32)
        sb["qs1r"] = sbuf("qs1r", [1, NT], F32)
        for n in ["murow","rrow","nmrow","rowA","rowB"]:
            sb[n] = sbuf(n, [1, NT], F32)
        sb["eps1"] = sbuf("eps1", [1, 1], F32)

        T = {}
        T["p_d"] = [st.enter_context(nc.psum_tensor(f"p2_p_d{i}", [128, NT], F32)) for i in range(2)]
        T["p_s1t"] = st.enter_context(nc.psum_tensor("p2_p_s1t", [128, NT], F32))
        T["p_s2t"] = st.enter_context(nc.psum_tensor("p2_p_s2t", [128, NT], F32))
        T["p_s1"] = T["p_s1t"]
        T["p_s2"] = T["p_s2t"]
        T.update(SEMS)
        T["s_cast"] = SEMS["s_ld"]
        T["pe_done_chunk"] = {}
        T["act_done_chunk"] = {}
        for tag in ["d1", "wq", "wk", "wv"]:
            T[f"dense_mm_done_{tag}"] = {}

        s_ld, s_cast, s_pe, s_dve, s_act, s_st, s_gp = (T[k] for k in ["s_ld","s_cast","s_pe","s_dve","s_act","s_st","s_gp"])

        for n in ["ident_f32", "ones_col_f32", "ones_row_f32", "iota64", "ident_bf16", "ones_col_bf16"]:
            prog.dma("sync", sb[n][:, :], cns[n][:, :], s_ld)
        prog.wait("sync", T["s_cc1"], 1)
        prog.dma("sync", sb["cntsum"][:, :], EX["c1_sum"][:, :], s_ld)
        prog.dma("sync", sb["qs1r"][:, :], qs1row[:, :], s_ld)
        prog.dma("sync", sb["mu1r"][:, :], murow1[:, :], s_ld)
        prog.dma("sync", sb["r1r"][:, :], rrow1[:, :], s_ld)
        prog.dma("sync", sb["bf"][:, :], bf_p[:, :], s_ld)
        prog.dma("sync", sb["bq"][:, :], bq_p[:, :], s_ld)
        prog.dma("sync", sb["bk"][:, :], bk_p[:, :], s_ld)
        prog.dma("sync", sb["bv"][:, :], bv_p[:, :], s_ld)
        prog.dma("sync", sb["asn_i"][:, :], asn_p[:, :], s_ld)
        prog.dma("sync", sb["bwT"][:, :], bwT[:, :], s_ld)
        prog.dma("sync", sb["Tmat"][:, :], Tmat[:, :], s_ld)
        prog.dma("sync", sb["Dmat"][:, :], Dmat[:, :], s_ld)
        v_setup_ld = s_ld.n
        prog.op("vector", lambda e: e.memset(sb["eps1"][:, :], 1e-5), s_dve)

        prog.wait("vector", s_ld, v_setup_ld)
        emit_ln_apply(prog, nc, T, sb, xT, sb["mu1r"][:, :], sb["r1r"][:, :], "xn")
        v_xn1 = T["s_dve"].n

        prog.wait("tensor", s_ld, v_setup_ld)
        prog.wait("tensor", s_dve)
        v_mask_ready, v6 = emit_routing(prog, nc, T, sb, has_bias=True)

        prog.op("vector", lambda e: e.memset(sb["BDf"][:, :], 0.0), s_dve)
        v_bdz = s_dve.n
        prog.wait("tensor", s_ld, v_setup_ld)
        prog.wait("tensor", s_dve, v6)
        prog.op("tensor", lambda e: e.matmul(T["p_s1t"][0:NH, 64:128], sb["bwT"][:, :], sb["Tmat"][:, :], start=True, stop=True), s_pe)
        v_bd1 = s_pe.n
        prog.wait("vector", s_pe, v_bd1)
        prog.op("vector", lambda e: e.tensor_tensor(sb["Aflat"][:, :], T["p_s1t"][0:NH, 64:128], sb["Dmat"][:, :], op=ALU.add), s_dve)
        v_bd2 = s_dve.n
        prog.wait("sync", s_dve, v_bd2)
        v_bd3 = prog.dma("sync", A_dram[:, :], sb["Aflat"][:, :], T["s_bd"])
        prog.wait("sync", T["s_bd"], v_bd3)
        prog.wait("sync", s_dve, v_bdz)
        for h in range(NH):
            for d in range(16):
                prog.dma("sync", sb["BDf"][8*d:8*d+8, h*128+8*d:h*128+8*d+8],
                         A_dram[h:h+1, :].rearrange("o (i j) -> (o i) j", i=8), T["s_bd"])
        v_bd4 = T["s_bd"].n
        prog.wait("vector", T["s_bd"], v_bd4)
        prog.op("vector", lambda e: e.tensor_copy(sb["BD"][:, :], sb["BDf"][:, :]), s_dve)
        v_bd_ready = s_dve.n

        prog.barrier(["sync", "tensor", "vector", "scalar", "gpsimd"], [s_gp, s_dve])
        prog.wait("tensor", s_dve, v_xn1)
        prog.wait("scalar", s_dve, v_mask_ready)
        x2_state = {}
        stats_pend = {}
        def emit_stats(oc2):
            buf2 = oc2 % 2
            prog.wait("tensor", s_dve, stats_pend[oc2][0])
            prog.wait("tensor", s_act, stats_pend[oc2][1])
            for half in range(2):
                prog.op("tensor", lambda e, buf2=buf2, oc2=oc2, half=half: e.matmul(
                    T["p_s1t"][0:1, half*512:(half+1)*512], sb["ones_col_f32"][:, :],
                    sb["x2row"][buf2][:, half*512:(half+1)*512], start=(oc2 == 0), stop=(oc2 == NCH-1)), None)
                prog.op("tensor", lambda e, buf2=buf2, oc2=oc2, half=half: e.matmul(
                    T["p_s2t"][0:1, half*512:(half+1)*512], sb["ones_col_f32"][:, :],
                    sb["sqf"][buf2][:, half*512:(half+1)*512], start=(oc2 == 0), stop=(oc2 == NCH-1)),
                    s_pe if half == 1 else None)
            x2_state[oc2][2] = s_pe.n
        def epi_d1(oc, v_mm):
            buf = oc % 2
            if oc >= 2:
                emit_stats(oc - 2)
            prog.wait("scalar", s_pe, v_mm)
            prog.wait("scalar", s_dve, v_bd_ready)
            if oc >= 2:
                prog.wait("scalar", s_dve, x2_state[oc-2][0])
            prog.op("scalar", lambda e, buf=buf, oc=oc: e.activation(sb["masked"][buf][:, :], T["p_d"][buf][:, :],
                    AF.Identity, bias=sb["brs05"][:, oc:oc+1], scale=sb["rs05"][:, oc:oc+1]), s_act)
            v_epi = s_act.n
            if oc >= 2:
                prog.wait("sync", s_dve, x2_state[oc-2][1])
            v_x1 = prog.dma("sync", sb["x1row"][buf][:, :], xT[oc*128:(oc+1)*128, :], T["s_x0" if buf == 0 else "s_x1"])
            prog.wait("vector", s_act, v_epi)
            prog.op("vector", lambda e, buf=buf: e.tensor_tensor(sb["t1"][buf][:, :], sb["masked"][buf][:, :], sb["qs_bc"][:, :], op=ALU.mult), s_dve)
            v_mult = s_dve.n
            prog.wait("vector", T["s_x0" if buf == 0 else "s_x1"], v_x1)
            if oc >= 2:
                prog.wait("vector", s_pe, x2_state[oc-2][2])
                prog.wait("vector", s_act, stats_pend[oc-2][1])
                prog.wait("vector", T["s_st0" if buf == 0 else "s_st1"], x2_state[oc-2][3])
            prog.op("vector", lambda e, buf=buf: e.tensor_tensor(sb["x2row"][buf][:, :], sb["t1"][buf][:, :], sb["x1row"][buf][:, :], op=ALU.add), s_dve)
            v_x2 = s_dve.n
            prog.wait("sync", s_dve, v_x2)
            v_store = prog.dma("sync", o_x2T[oc*128:(oc+1)*128, :], sb["x2row"][buf][:, :], T["s_st0" if buf == 0 else "s_st1"])
            prog.wait("scalar", s_dve, v_x2)
            prog.op("scalar", lambda e, buf=buf: e.activation(sb["sqf"][buf][:, :], sb["x2row"][buf][:, :], AF.Square), s_act)
            v_sq = s_act.n
            stats_pend[oc] = (v_x2, v_sq)
            x2_state[oc] = [v_mult, v_x2, 0, v_store]
            if oc == NCH - 1:
                emit_stats(NCH - 2)
                emit_stats(NCH - 1)
            return ("s_act", v_epi)
        emit_dense(prog, nc, T, sb, Wf, sb["xn"][:, :], epi_d1, "d1")
        v_d1_stats = T["s_pe"].n

        prog.wait("vector", s_pe, v_d1_stats)
        prog.op("vector", lambda e: e.tensor_scalar(sb["murow"][:, :], T["p_s1t"][0:1, :], 1.0/D, None, op0=ALU.mult), s_dve)
        prog.op("vector", lambda e: e.tensor_scalar(sb["rowA"][:, :], T["p_s2t"][0:1, :], 1.0/D, None, op0=ALU.mult), s_dve)
        prog.op("vector", lambda e: e.tensor_tensor(sb["rowB"][:, :], sb["murow"][:, :], sb["murow"][:, :], op=ALU.mult), s_dve)
        prog.op("vector", lambda e: e.tensor_tensor(sb["rowA"][:, :], sb["rowA"][:, :], sb["rowB"][:, :], op=ALU.subtract), s_dve)
        v_var = s_dve.n
        prog.wait("scalar", s_dve, v_var)
        prog.op("scalar", lambda e: e.activation(sb["rowB"][:, :], sb["rowA"][:, :], AF.Ln, bias=sb["eps1"][:, :]), s_act)
        prog.op("scalar", lambda e: e.activation(sb["rrow"][:, :], sb["rowB"][:, :], AF.Exp, scale=-0.5), s_act)
        v_r2 = s_act.n

        prog.barrier(["sync", "tensor", "vector", "scalar", "gpsimd"], [s_gp, s_dve, s_pe, s_act])
        prog.wait("sync", T["s_st0"])
        prog.wait("sync", T["s_st1"])
        prog.wait("vector", s_act, v_r2)
        emit_ln_apply(prog, nc, T, sb, o_x2T, sb["murow"][:, :], sb["rrow"][:, :], "xn")
        v_xn2 = T["s_dve"].n

        prog.barrier(["sync", "tensor", "vector", "scalar", "gpsimd"], [s_gp, s_dve, s_pe, s_act])
        prog.wait("tensor", s_dve, v_xn2)
        def mk_epi_kv(bias_name, out_dram):
            state = {}
            def epi(oc, v_mm):
                buf = oc % 2
                prog.wait("scalar", s_pe, v_mm)
                if oc >= 2:
                    prog.wait("scalar", T["s_st0" if buf == 0 else "s_st1"], state[oc-2])
                prog.op("scalar", lambda e, buf=buf, oc=oc: e.activation(sb["kvout"][buf][:, :], T["p_d"][buf][:, :],
                        AF.Identity, bias=sb[bias_name][:, oc:oc+1]), s_act)
                v_epi = s_act.n
                prog.wait("sync", s_act, v_epi)
                v_store = prog.dma("sync", out_dram[oc*128:(oc+1)*128, :], sb["kvout"][buf][:, :], T["s_st0" if buf == 0 else "s_st1"])
                state[oc] = v_store
                return ("s_act", v_epi)
            return epi
        emit_dense(prog, nc, T, sb, Wk, sb["xn"][:, :], mk_epi_kv("bk", o_kT), "wk")
        prog.wait("gpsimd", T["s_st0"], T["s_st0"].n)
        prog.wait("gpsimd", T["s_st1"], T["s_st1"].n)
        prog.cc("AllGather", ALU.bypass, GROUPS4, EX["kT_loc"], EX["Kst_d"], T["s_cck"])
        prog.barrier(["sync", "tensor", "vector", "scalar"], [s_act, T["s_st0"], T["s_st1"], s_pe])
        emit_dense(prog, nc, T, sb, Wv, sb["xn"][:, :], mk_epi_kv("bv", o_vT), "wv")
        prog.wait("gpsimd", T["s_st0"], T["s_st0"].n)
        prog.wait("gpsimd", T["s_st1"], T["s_st1"].n)
        prog.cc("AllGather", ALU.bypass, GROUPS4, EX["vT_loc"], EX["Vst_d"], T["s_ccv"])
        prog.barrier(["sync", "tensor", "vector", "scalar"], [s_act, T["s_st0"], T["s_st1"], s_pe, s_dve])
        prog.wait("tensor", s_dve, v_bd_ready)
        q_state = {}
        def epi_q(oc, v_mm):
            buf = oc % 2
            h = oc // 2
            prog.wait("scalar", s_pe, v_mm)
            if oc >= 2:
                prog.wait("scalar", s_pe, q_state[oc-2][0])
            prog.op("scalar", lambda e, buf=buf, oc=oc: e.activation(sb["qout"][buf][:, :], T["p_d"][buf][:, :],
                    AF.Identity, bias=sb["bq"][:, oc:oc+1]), s_act)
            v_epi = s_act.n
            prog.wait("tensor", s_act, v_epi)
            if oc >= 1:
                prog.wait("tensor", s_dve, q_state[oc-1][1])
            for half in range(2):
                prog.op("tensor", lambda e, buf=buf, h=h, half=half: e.matmul(
                    T["p_s1t"][:, half*512:(half+1)*512], sb["BD"][:, h*128:(h+1)*128],
                    sb["qout"][buf][:, half*512:(half+1)*512], start=True, stop=True), s_pe if half == 1 else None)
            v_bdmm = s_pe.n
            prog.wait("vector", s_pe, v_bdmm)
            if oc >= 2:
                prog.wait("vector", T["s_st0" if buf == 0 else "s_st1"], q_state[oc-2][2])
            prog.op("vector", lambda e, buf=buf: e.tensor_copy(sb["qhout"][buf][:, :], T["p_s1t"][:, :]), s_dve)
            v_ev = s_dve.n
            prog.wait("sync", s_dve, v_ev)
            v_store = prog.dma("sync", o_qhT[oc*128:(oc+1)*128, :], sb["qhout"][buf][:, :], T["s_st0" if buf == 0 else "s_st1"])
            q_state[oc] = (v_bdmm, v_ev, v_store)
            return ("s_act", v_epi)
        emit_dense(prog, nc, T, sb, Wq, sb["xn"][:, :], epi_q, "wq")

def phase3(nc, prog, SEMS, EX):
    x2T = EX["x2T_d"]
    Kst = EX["Kst_d"]
    Vst = EX["Vst_d"]
    qhT = EX["qhT_d"]
    Wf = EX["Wf_p"]
    bf_p = EX["bf_pack"]
    cent = EX["cent_pack"]
    cns = EX["cns"]
    o_x3T = EX["x3T_d"]
    o_d2T = EX["d2T_d"]
    o_counts = EX["c3_loc"]
    o_qs = EX["qs3_d"]

    with ExitStack() as st:
        def sbuf(name, shape, dt):
            return st.enter_context(nc.sbuf_tensor("p3_" + name, shape, dt))
        sb = {}
        for n in ["ident_f32", "ones_col_f32", "ones_row_f32"]:
            sb[n] = sbuf(n, list(CONSTS[n].shape), F32)
        for n in ["ident_bf16", "ones_col_bf16"]:
            sb[n] = sbuf(n, list(CONSTS[n].shape), BF16)
        sb["cent"] = sbuf("cent", [128, NCH*NCL], F32)
        sb["xn"] = sbuf("xn", [128, NCH*NT], BF16)
        sb["K"] = sbuf("K", [128, 2*4096], BF16)
        sb["VT"] = sbuf("VT", [128, 2*4096], BF16)
        sb["Vtok"] = sbuf("Vtok", [128, NCH*DH], BF16)
        sb["Q"] = sbuf("Q", [128, 2*NT], BF16)
        sb["X2"] = sbuf("X2", [128, 2*NT], F32)
        sb["S"] = sbuf("S", [128, 4096], F32)
        sb["P"] = sbuf("P", [128, 4096], BF16)
        sb["P2"] = sbuf("P2", [128, 4096], BF16)
        _xnb = sb["xn"][:, :]
        sb["Sb"] = _xnb[:, 0:8192].bitcast(F32)
        sb["Pb"] = _xnb[:, 8192:12288]
        sb["P2b"] = _xnb[:, 12288:16384]
        sb["negmax2"] = sbuf("negmax2", [128, 1], F32)
        sb["z2"] = sbuf("z2", [128, 1], F32)
        sb["lnz2"] = sbuf("lnz2", [128, 1], F32)
        sb["zinv2"] = sbuf("zinv2", [128, 1], F32)
        sb["junk2"] = sbuf("junk2", [128, 1], F32)
        sb["PT"] = sbuf("PT", [128, NCH*DH], BF16)
        sb["negmax"] = sbuf("negmax", [128, 1], F32)
        sb["z"] = sbuf("z", [128, 1], F32)
        sb["lnz"] = sbuf("lnz", [128, 1], F32)
        sb["zinv"] = sbuf("zinv", [128, 1], F32)
        sb["junk"] = sbuf("junk", [128, 1], F32)
        sb["xbuf"] = [sb["S"][:, 0:NT], sb["S"][:, NT:2*NT]]
        _pf32 = sb["P"][:, :].bitcast(F32)
        sb["xtile"] = [_pf32[:, i*128:(i+1)*128] for i in range(4)]
        sb["sqf"] = [sb["S"][:, 2*NT:3*NT], sb["S"][:, 3*NT:4*NT]]
        sb["t1"] = [sbuf(f"t1_{i}", [128, NT], F32) for i in range(2)]
        kf32 = sb["K"][:, :].bitcast(F32)
        sb["rbc"] = kf32[:, 0:NT]
        sb["nmbc"] = kf32[:, NT:2*NT]
        sb["panel"] = [sb["P"][:, :], sb["P2"][:, :]]
        sb["d2out"] = sb["t1"]
        sb["bf"] = sbuf("bf", [128, NCH], F32)
        sb["csrow"] = sbuf("csrow", [1, NCL], F32)
        sb["csbc"] = sbuf("csbc", [128, NCL], F32)
        vtf32 = sb["VT"][:, :].bitcast(F32)
        vtokf32 = sb["Vtok"][:, :].bitcast(F32)
        sb["murow"] = vtf32[0:1, 0:NT]; sb["rrow"] = vtf32[0:1, NT:2*NT]
        sb["nmrow"] = vtf32[0:1, 2*NT:3*NT]; sb["negr10row"] = vtf32[0:1, 3*NT:4*NT]
        sb["s1row"] = vtokf32[0:1, 0:NT]; sb["s2row"] = vtokf32[0:1, NT:2*NT]
        sb["msqrow"] = vtokf32[0:1, 2*NT:3*NT]; sb["m2row"] = vtokf32[0:1, 3*NT:4*NT]
        sb["varrow"] = sb["msqrow"]; sb["lnrow"] = sb["m2row"]
        sb["eps1"] = sbuf("eps1", [1, 1], F32)
        sb["mucols"] = sbuf("mucols", [128, NTT], F32)
        sb["nr10cols"] = sbuf("nr10cols", [128, NTT], F32)
        sb["lg"] = sbuf("lg", [128, NCL], F32)
        sb["pd"] = sbuf("pd", [128, NCL], F32)
        sb["z001"] = sbuf("z001", [128, 1], F32)
        sb["hits"] = sbuf("hits", [128, NCL], BF16)
        sb["qs_all"] = sbuf("qs_all", [128, NTT], F32)
        sb["cnt"] = sbuf("cnt", [64, 1], F32)
        sb["rowcols"] = sbuf("rowcols", [16, 128], F32)
        sb["qsT"] = sbuf("qsT", [NTT, 128], F32)

        T = {}
        T["scr_dram"] = EX["scr_rows3"]
        T["p_q"] = [st.enter_context(nc.psum_tensor(f"p3_p_q{i}", [128, NT], F32)) for i in range(2)]
        T["p_av"] = st.enter_context(nc.psum_tensor("p3_p_av", [128, NT], F32))
        T["p_t"] = [st.enter_context(nc.psum_tensor(f"p3_p_t{i}", [128, NT], BF16)) for i in range(2)]
        T["p_s1"] = T["p_q"][0]
        T["p_s2"] = T["p_q"][1]
        T["p_s1t"] = T["p_q"][0]
        T["p_s2t"] = T["p_q"][1]
        T["p_dots"] = [T["p_av"][:, 0:512], T["p_av"][:, 512:1024]]
        T["p_scr"] = T["p_t"][0][:, :].bitcast(F32)
        T["p_scr2"] = T["p_t"][1][:, :].bitcast(F32)[:, 0:256]
        T["p_counts"] = T["p_t"][1][:, :].bitcast(F32)[:, 256:512]
        T["p_d"] = T["p_q"]
        T.update(SEMS)
        T["s_cast"] = SEMS["s_ld"]
        T["pe_done_chunk"] = {}
        T["act_done_chunk"] = {}
        T["dense_mm_done_d2"] = {}
        s_ld, s_pe, s_dve, s_act, s_st, s_gp = (T[k] for k in ["s_ld","s_pe","s_dve","s_act","s_st","s_gp"])
        s_h = [T["s_h0"], T["s_h1"]]

        for n in ["ident_f32", "ones_col_f32", "ones_row_f32", "ident_bf16", "ones_col_bf16"]:
            prog.dma("sync", sb[n][:, :], cns[n][:, :], s_ld)
        prog.dma("sync", sb["cent"][:, :], cent[:, :], s_ld)
        prog.dma("sync", sb["bf"][:, :], bf_p[:, :], s_ld)
        v_setup = s_ld.n
        prog.op("vector", lambda e: e.memset(sb["eps1"][:, :], 1e-5), s_dve)

        prog.wait("tensor", s_ld, v_setup)
        prog.wait("sync", T["s_cck"], 1)
        prog.wait("sync", T["s_ccv"], 1)
        head_done = {}
        for h in range(NH):
            s_vt = T["s_vt0" if h % 2 == 0 else "s_vt1"]
            s_kq = T["s_kq0" if h % 2 == 0 else "s_kq1"]
            s_xh = T["s_xh0" if h % 2 == 0 else "s_xh1"]
            if h >= 1:
                prog.wait("sync", s_pe, head_done[h-1]["vtrans"])
            for dc in range(2):
                for r2 in range(4):
                    prog.dma("sync", sb["VT"][:, dc*4096 + r2*1024 : dc*4096 + (r2+1)*1024],
                             Vst[r2*D + h*DH + dc*128 : r2*D + h*DH + (dc+1)*128, :], s_vt)
            if h >= 1:
                prog.wait("sync", s_pe, head_done[h-1]["lastS"])
            for dc in range(2):
                for r2 in range(4):
                    prog.dma("sync", sb["K"][:, dc*4096 + r2*1024 : dc*4096 + (r2+1)*1024],
                             Kst[r2*D + h*DH + dc*128 : r2*D + h*DH + (dc+1)*128, :], s_kq)
                prog.dma("sync", sb["Q"][:, dc*NT:(dc+1)*NT], qhT[h*DH + dc*128 : h*DH + (dc+1)*128, :], s_kq)
            if h >= 1:
                prog.wait("sync", T["s_st0" if (h-1) % 2 == 0 else "s_st1"], head_done[h-1]["store"])
            for dc in range(2):
                prog.dma("sync", sb["X2"][:, dc*NT:(dc+1)*NT], x2T[h*DH + dc*128 : h*DH + (dc+1)*128, :], s_xh)
            v_vt, v_kq, v_xh = s_vt.n, s_kq.n, s_xh.n

            prog.wait("tensor", s_vt, v_vt)
            vt_evac = {}
            if h >= 1:
                prog.wait("vector", s_pe, head_done[h-1]["av"])
            for rnd in range(8):
                pt = T["p_t"][rnd % 2]
                if rnd >= 2:
                    prog.wait("tensor", s_act, vt_evac[rnd-2])
                for j in range(8):
                    g = rnd*8 + j
                    mc, dc = g // 2, g % 2
                    prog.op("tensor", lambda e, pt=pt, j=j, mc=mc, dc=dc: e.transpose(
                        pt[:, j*128:(j+1)*128],
                        sb["VT"][:, dc*4096 + mc*128 : dc*4096 + (mc+1)*128],
                        sb["ident_bf16"][:, :]), s_pe if j == 7 else None)
                v_t = s_pe.n
                prog.wait("scalar", s_pe, v_t)
                prog.op("scalar", lambda e, pt=pt, rnd=rnd: e.activation(
                    sb["Vtok"][:, rnd*1024:(rnd+1)*1024], pt[:, 0:1024], AF.Identity), s_act)
                vt_evac[rnd] = s_act.n
            v_vtok = s_act.n
            T["v_vtrans_end"] = s_pe.n
            T.pop("v_max_prev", None); T.pop("v_pscale_prev", None); T.pop("v_ptrans_prev", None); T.pop("v_exp_prev", None)
            st_pscale = {}
            st_exp = {}
            st_max = {}
            st_evac = {}
            st_pt = {}
            st_av = {}
            v_resid = {}
            def SS(qt): return sb["S"][:, :] if qt % 2 == 0 else sb["Sb"]
            def PP(qt): return sb["P"][:, :] if qt % 2 == 0 else sb["Pb"]
            def PP2(qt): return sb["P2"][:, :] if qt % 2 == 0 else sb["P2b"]
            def NM(qt): return sb["negmax"][:, :] if qt % 2 == 0 else sb["negmax2"][:, :]
            def ZZ(qt): return (sb["z"][:, :], sb["lnz"][:, :], sb["zinv"][:, :], sb["junk"][:, :]) if qt % 2 == 0 else (sb["z2"][:, :], sb["lnz2"][:, :], sb["zinv2"][:, :], sb["junk2"][:, :])
            def emit_S(qt):
                prog.wait("tensor", s_kq, v_kq)
                for qq in range(4):
                    pq = T["p_q"][qq % 2]
                    key = f"pq_reader_{qq % 2}"
                    if key in T:
                        eng, val = T[key]
                        prog.wait("tensor", T[eng], val)
                    for msub in range(2):
                        for dc in range(2):
                            prog.op("tensor", lambda e, pq=pq, msub=msub, dc=dc, qq=qq, qt=qt: e.matmul(
                                pq[:, msub*512:(msub+1)*512],
                                sb["Q"][:, dc*NT + qt*128 : dc*NT + (qt+1)*128],
                                sb["K"][:, dc*4096 + qq*1024 + msub*512 : dc*4096 + qq*1024 + (msub+1)*512],
                                start=(dc == 0), stop=(dc == 1)), s_pe if (msub == 1 and dc == 1) else None)
                    v_s = s_pe.n
                    if qq % 2 == 0:
                        prog.wait("vector", s_pe, v_s)
                        prog.wait("vector", s_act, st_exp.get(qt-2, 0))
                        prog.op("vector", lambda e, pq=pq, qq=qq, qt=qt: e.tensor_copy(SS(qt)[:, qq*1024:(qq+1)*1024], pq[:, :]), s_dve)
                        T[key] = ("s_dve", s_dve.n)
                    else:
                        prog.wait("scalar", s_pe, v_s)
                        prog.wait("scalar", s_act, st_exp.get(qt-2, 0))
                        prog.op("scalar", lambda e, pq=pq, qq=qq, qt=qt: e.activation(SS(qt)[:, qq*1024:(qq+1)*1024], pq[:, :], AF.Identity), s_act)
                        T[key] = ("s_act", s_act.n)
                st_evac[qt] = (s_dve.n, s_act.n)
            def emit_chain(qt):
                prog.wait("vector", s_act, st_evac[qt][1])
                prog.wait("vector", s_dve, st_evac[qt][0])
                prog.wait("vector", s_act, st_exp.get(qt-2, 0))
                prog.op("vector", lambda e, qt=qt: e.tensor_reduce(NM(qt), SS(qt), axis=AX.X, op=ALU.max, negate=True), s_dve)
                st_max[qt] = s_dve.n
                z_, lnz_, zinv_, junk_ = ZZ(qt)
                prog.wait("scalar", s_dve, st_max[qt])
                prog.wait("scalar", s_gp, st_pscale.get(qt-2, 0))
                prog.op("scalar", lambda e, qt=qt, z_=z_: e.activation(PP(qt), SS(qt), AF.Exp, bias=NM(qt), accum_out=z_), None)
                prog.op("scalar", lambda e, z_=z_, lnz_=lnz_: e.activation(lnz_, z_, AF.Ln), None)
                prog.op("scalar", lambda e, lnz_=lnz_, zinv_=zinv_: e.activation(zinv_, lnz_, AF.Exp, scale=-1.0), None)
                prog.op("scalar", lambda e, zinv_=zinv_, junk_=junk_: e.activation(junk_, zinv_, AF.Identity), s_act)
                st_exp[qt] = s_act.n
                prog.wait("gpsimd", s_act, st_exp[qt])
                prog.wait("gpsimd", s_pe, st_pt.get(qt-2, 0))
                prog.op("gpsimd", lambda e, qt=qt, zinv_=zinv_: e.tensor_scalar(PP2(qt), PP(qt), zinv_, 1.0, op0=ALU.mult, op1=ALU.mult), s_gp)
                st_pscale[qt] = s_gp.n
            def emit_PT(qt):
                prog.wait("tensor", s_gp, st_pscale[qt])
                prog.wait("tensor", s_pe, st_av.get(qt // 2 - 1, 0))
                for rnd in range(4):
                    pt = T["p_t"][rnd % 2]
                    prog.wait("tensor", s_act, T.get(f"pt_evac_{rnd % 2}", vt_evac[6 + rnd % 2]))
                    for j in range(8):
                        mc = rnd*8 + j
                        prog.op("tensor", lambda e, pt=pt, j=j, mc=mc, qt=qt: e.transpose(
                            pt[:, j*128:(j+1)*128],
                            PP2(qt)[:, mc*128:(mc+1)*128],
                            sb["ident_bf16"][:, :]), s_pe if j == 7 else None)
                    v_tr = s_pe.n
                    prog.wait("scalar", s_pe, v_tr)
                    prog.op("scalar", lambda e, pt=pt, rnd=rnd, qt=qt: e.activation(
                        sb["PT"][:, :].rearrange("p (mc q) -> p mc q", q=DH)[:, rnd*8:(rnd+1)*8, (qt % 2)*128:(qt % 2 + 1)*128],
                        pt[:, 0:1024].rearrange("p (j q) -> p j q", q=128), AF.Identity), s_act)
                    T[f"pt_evac_{rnd % 2}"] = s_act.n
                st_pt[qt] = s_pe.n
                st_pt[(qt, "dve")] = s_act.n
            def emit_AV(qst):
                prog.wait("tensor", s_act, st_pt[(qst*2+1, "dve")])
                if qst - 1 in v_resid:
                    prog.wait("tensor", s_dve, v_resid[qst-1])
                for mc in range(NCH):
                    for dc in range(2):
                        prog.op("tensor", lambda e, mc=mc, dc=dc: e.matmul(
                            T["p_av"][:, dc*512 : dc*512 + 256],
                            sb["Vtok"][:, mc*DH + dc*128 : mc*DH + (dc+1)*128],
                            sb["PT"][:, mc*DH : (mc+1)*DH],
                            start=(mc == 0), stop=(mc == NCH-1)), s_pe if (mc == NCH-1 and dc == 1) else None)
                st_av[qst] = s_pe.n
                prog.wait("vector", s_pe, st_av[qst])
                prog.wait("vector", s_xh, v_xh)
                for dc in range(2):
                    prog.op("vector", lambda e, dc=dc, qst=qst: e.tensor_tensor(
                        sb["X2"][:, dc*NT + qst*256 : dc*NT + (qst+1)*256],
                        T["p_av"][:, dc*512 : dc*512 + 256],
                        sb["X2"][:, dc*NT + qst*256 : dc*NT + (qst+1)*256], op=ALU.add), s_dve)
                v_resid[qst] = s_dve.n
            for idx in range(9):
                if idx < 8:
                    emit_S(idx)
                    if idx == 7:
                        T["v_lastS"] = s_pe.n
                if idx >= 1:
                    emit_PT(idx-1)
                if idx < 8:
                    emit_chain(idx)
                if idx >= 2 and idx % 2 == 0:
                    emit_AV(idx // 2 - 1)
            v_resid_prev = v_resid[3]
            prog.wait("sync", s_dve, v_resid_prev)
            sst = T["s_st0" if h % 2 == 0 else "s_st1"]
            for dc in range(2):
                prog.dma("sync", o_x3T[h*DH + dc*128 : h*DH + (dc+1)*128, :], sb["X2"][:, dc*NT:(dc+1)*NT], sst)
            head_done[h] = {"vtrans": T["v_vtrans_end"], "lastS": T["v_lastS"],
                            "av": s_pe.n, "resid": s_dve.n, "store": sst.n}

        prog.barrier(["sync", "tensor", "vector", "scalar", "gpsimd"],
                     [s_pe, s_dve, s_act, T["s_st0"], T["s_st1"], T["s_h0"], T["s_h1"]])
        v_done3 = emit_ln_dots(prog, nc, T, sb, o_x3T, sb, with_dots=True) if False else None
        v_done3 = emit_ln_dots(prog, nc, T, sb if False else T, o_x3T, sb, with_dots=True) if False else emit_ln_dots(prog, nc, T, o_x3T, sb, with_dots=True)
        prog.wait("vector", s_pe, v_done3)
        prog.op("vector", lambda e: e.tensor_copy(sb["cnt"][:, :], T["p_counts"][0:64, 0:1]), s_dve)
        v_cnt = s_dve.n
        prog.wait("tensor", s_pe, v_done3)
        prog.op("tensor", lambda e: e.transpose(T["p_scr"][0:NTT, 0:128], sb["qs_all"][:, :], sb["ident_f32"][:, :]), s_pe)
        v_qt = s_pe.n
        prog.wait("vector", s_pe, v_qt)
        prog.op("vector", lambda e: e.tensor_copy(sb["qsT"][:, :], T["p_scr"][0:NTT, 0:128]), s_dve)
        v_qsT = s_dve.n
        prog.wait("sync", s_dve, v_cnt)
        v_cst3 = prog.dma("sync", o_counts[:, :], sb["cnt"][:, :], s_st)
        prog.wait("sync", s_dve, v_qsT)
        prog.dma("sync", o_qs[0:1, :].rearrange("a (b p) -> (a b) p", p=128), sb["qsT"][:, :], s_st)
        prog.wait("gpsimd", s_st, v_cst3)
        prog.cc("AllReduce", ALU.add, [[0, 1, 2, 3, 4, 5, 6, 7]], EX["c3_loc"], EX["c3_sum"], T["s_cc3"])

        prog.barrier(["sync", "tensor", "vector", "scalar", "gpsimd"], [s_pe, s_dve, s_act, s_gp])
        emit_ln_apply(prog, nc, T, sb, o_x3T, sb["murow"][:, :], sb["rrow"][:, :], "xn")
        v_xn3 = s_dve.n

        prog.barrier(["sync", "tensor", "vector", "scalar", "gpsimd"], [s_pe, s_dve, s_act, s_gp])
        prog.wait("tensor", s_dve, v_xn3)
        d2_state = {}
        def epi_d2(oc, v_mm):
            buf = oc % 2
            prog.wait("scalar", s_pe, v_mm)
            if oc >= 2:
                prog.wait("scalar", T["s_st0" if buf == 0 else "s_st1"], d2_state[oc-2])
            prog.op("scalar", lambda e, buf=buf, oc=oc: e.activation(sb["d2out"][buf][:, :], T["p_d"][buf][:, :],
                    AF.Identity, bias=sb["bf"][:, oc:oc+1]), s_act)
            v_epi = s_act.n
            prog.wait("sync", s_act, v_epi)
            v_store = prog.dma("sync", o_d2T[oc*128:(oc+1)*128, :], sb["d2out"][buf][:, :], T["s_st0" if buf == 0 else "s_st1"])
            d2_state[oc] = v_store
            return ("s_act", v_epi)
        emit_dense(prog, nc, T, sb, Wf, sb["xn"][:, :], epi_d2, "d2")

def phase4(nc, prog, SEMS, EX):
    x3T = EX["x3T_d"]
    d2T = EX["d2T_d"]
    qs3row = EX["qs3_d"]
    asn_p = EX["assign_pack"]
    cns = EX["cns"]
    out = EX["out_tok"]

    with ExitStack() as st:
        def sbuf(name, shape, dt):
            return st.enter_context(nc.sbuf_tensor("p4_" + name, shape, dt))
        sb = {}
        for n in ["ident_f32", "ones_col_f32", "ones_row_f32", "iota64"]:
            sb[n] = sbuf(n, list(CONSTS[n].shape), F32)
        sb["cntsum"] = sbuf("cntsum", [NCL, 1], F32)
        sb["qs1r"] = sbuf("qs1r", [1, NT], F32)
        sb["cscol"] = sbuf("cscol", [NCL, 1], F32)
        sb["csrow1"] = sbuf("csrow1", [1, NCL], F32)
        sb["cs_bc"] = sbuf("cs_bc", [128, NCL], F32)
        sb["qs_bc"] = sbuf("qs_bc", [128, NT], F32)
        sb["asn_i"] = sbuf("asn_i", [128, NCH], I32)
        sb["asn_f"] = sbuf("asn_f", [128, NCH], F32)
        sb["eq"] = sbuf("eq", [128, NCL], F32)
        sb["eqp"] = sbuf("eqp", [128, NCL], F32)
        sb["rs05"] = sbuf("rs05", [128, NCH], F32)
        sb["d2row"] = [sbuf(f"d2row{i}", [128, NT], F32) for i in range(2)]
        sb["x3row"] = [sbuf(f"x3row{i}", [128, NT], F32) for i in range(2)]
        sb["m1"] = [sbuf(f"m1_{i}", [128, NT], F32) for i in range(2)]
        sb["x4"] = [sbuf(f"x4_{i}", [128, NT], F32) for i in range(4)]
        sb["obuf"] = [sbuf(f"obuf{i}", [128, 512], F32) for i in range(2)]

        T = {}
        T["p_s1t"] = st.enter_context(nc.psum_tensor("p4_p_s1t", [128, NT], F32))
        T["p_s2t"] = st.enter_context(nc.psum_tensor("p4_p_s2t", [128, NT], F32))
        T["p_o"] = [st.enter_context(nc.psum_tensor(f"p4_p_o{i}", [128, 512], F32)) for i in range(2)]
        T.update(SEMS)
        s_ld, s_pe, s_dve, s_act, s_st, s_gp = (T[k] for k in ["s_ld","s_pe","s_dve","s_act","s_st","s_gp"])
        s_x = [T["s_x0"], T["s_x1"]]
        s_sto = [T["s_st0"], T["s_st1"]]

        for n in ["ident_f32", "ones_col_f32", "ones_row_f32", "iota64"]:
            prog.dma("sync", sb[n][:, :], cns[n][:, :], s_ld)
        prog.wait("sync", T["s_cc3"], 1)
        prog.dma("sync", sb["cntsum"][:, :], EX["c3_sum"][:, :], s_ld)
        prog.dma("sync", sb["qs1r"][:, :], qs3row[:, :], s_ld)
        prog.dma("sync", sb["asn_i"][:, :], asn_p[:, :], s_ld)
        v_setup = s_ld.n
        prog.wait("tensor", s_ld, v_setup)
        prog.wait("vector", s_ld, v_setup)
        v_mask, _ = emit_routing(prog, nc, T, sb, has_bias=False)

        ostate = {}
        last_m1 = {}
        last_x4 = {}
        last_gp = {}
        for g in range(8):
            x4done = {}
            for i in range(4):
                oc = g*4 + i
                buf = oc % 2
                if buf in last_gp:
                    prog.wait("sync", s_gp, last_gp[buf])
                if buf in last_m1:
                    prog.wait("sync", s_act, last_m1[buf])
                if buf in last_x4:
                    prog.wait("sync", s_dve, last_x4[buf])
                v_d2 = prog.dma("sync", sb["d2row"][buf][:, :], d2T[oc*128:(oc+1)*128, :], s_x[buf])
                v_x3 = prog.dma("sync", sb["x3row"][buf][:, :], x3T[oc*128:(oc+1)*128, :], s_x[buf])
                prog.wait("scalar", s_x[buf], v_x3)
                prog.wait("scalar", s_dve, v_mask)
                if buf in last_gp:
                    prog.wait("scalar", s_gp, last_gp[buf])
                prog.op("scalar", lambda e, buf=buf, oc=oc: e.activation(sb["m1"][buf][:, :], sb["d2row"][buf][:, :],
                        AF.Identity, scale=sb["rs05"][:, oc:oc+1]), s_act)
                v_m1 = s_act.n
                last_m1[buf] = v_m1
                prog.wait("gpsimd", s_act, v_m1)
                if buf in last_x4:
                    prog.wait("gpsimd", s_dve, last_x4[buf])
                prog.op("gpsimd", lambda e, buf=buf: e.tensor_tensor(sb["m1"][buf][:, :], sb["m1"][buf][:, :], sb["qs_bc"][:, :], op=ALU.mult), s_gp)
                v_gp = s_gp.n
                last_gp[buf] = v_gp
                prog.wait("vector", s_gp, v_gp)
                prog.wait("vector", s_x[buf], v_x3)
                if g > 0:
                    prog.wait("vector", s_pe, ostate[g-1][2])
                prog.op("vector", lambda e, buf=buf, i=i: e.tensor_tensor(sb["x4"][i][:, :], sb["m1"][buf][:, :], sb["x3row"][buf][:, :], op=ALU.add), s_dve)
                x4done[i] = s_dve.n
                last_x4[buf] = s_dve.n
            v_tr_last = 0
            for tt in range(NTT):
                po = T["p_o"][tt % 2]
                prog.wait("tensor", s_dve, x4done[3])
                if tt >= 2:
                    prog.wait("tensor", s_dve, T.get(f"o_evac_{tt-2}_{g}", 0))
                elif g > 0:
                    prog.wait("tensor", s_dve, T.get(f"o_evac_{NTT-2+tt}_{g-1}", 0))
                for i in range(4):
                    prog.op("tensor", lambda e, po=po, i=i, tt=tt: e.transpose(
                        po[:, i*128:(i+1)*128], sb["x4"][i][:, tt*128:(tt+1)*128], sb["ident_f32"][:, :]),
                        s_pe if i == 3 else None)
                v_tr = s_pe.n
                v_tr_last = v_tr
                prog.wait("vector", s_pe, v_tr)
                if tt >= 2:
                    prog.wait("vector", s_sto[tt % 2], T.get(f"o_store_{tt-2}_{g}", 0))
                elif g > 0:
                    prog.wait("vector", s_sto[tt % 2], T.get(f"o_store_{NTT-2+tt}_{g-1}", 0))
                prog.op("vector", lambda e, po=po, tt=tt: e.tensor_copy(sb["obuf"][tt % 2][:, :], po[:, :]), s_dve)
                v_ev = s_dve.n
                T[f"o_evac_{tt}_{g}"] = v_ev
                prog.wait("sync", s_dve, v_ev)
                v_st = prog.dma("sync", out[tt*128:(tt+1)*128, g*512:(g+1)*512], sb["obuf"][tt % 2][:, :], s_sto[tt % 2])
                T[f"o_store_{tt}_{g}"] = v_st
            ostate[g] = (s_gp.n, s_dve.n, v_tr_last)
        prog.wait("sync", T["s_st0"])
        prog.wait("sync", T["s_st1"])

GROUPS4 = [[0, 1, 2, 3], [4, 5, 6, 7]]

def build_fused():
    nc = bass.Bass("TRN2", target_bir_lowering=False)
    EX = {}
    EX["xT"] = nc.dram_tensor("xT", [D, NT], F32, kind="ExternalInput").ap()
    EX["cent_pack"] = nc.dram_tensor("cent_pack", [128, NCH*NCL], F32, kind="ExternalInput").ap()
    for w in ["Wf_p", "Wq_p", "Wk_p", "Wv_p"]:
        EX[w] = nc.dram_tensor(w, [NCH, 128, D], F32, kind="ExternalInput").ap()
    for b in ["bf_pack", "bq_pack", "bk_pack", "bv_pack"]:
        EX[b] = nc.dram_tensor(b, [128, NCH], F32, kind="ExternalInput").ap()
    EX["assign_pack"] = nc.dram_tensor("assign_pack", [128, NCH], I32, kind="ExternalInput").ap()
    EX["bwT"] = nc.dram_tensor("bwT", [3, NH], F32, kind="ExternalInput").ap()
    EX["Tmat"] = nc.dram_tensor("Tmat", [3, NCL], F32, kind="ExternalInput").ap()
    EX["Dmat_tiled"] = nc.dram_tensor("Dmat_tiled", [NH, NCL], F32, kind="ExternalInput").ap()
    EX["cns"] = declare_consts(nc, ["ident_f32", "ident_bf16", "ones_col_f32", "ones_col_bf16",
                                    "ones_row_f32", "iota64"])
    EX["out_tok"] = nc.dram_tensor("out_tok", [NT, D], F32, kind="ExternalOutput").ap()
    EX["c1_loc"] = nc.dram_tensor("c1_loc", [NCL, 1], F32, kind="Internal").ap()
    EX["c1_sum"] = nc.dram_tensor("c1_sum", [NCL, 1], F32, kind="Internal").ap()
    EX["c3_loc"] = nc.dram_tensor("c3_loc", [NCL, 1], F32, kind="Internal").ap()
    EX["c3_sum"] = nc.dram_tensor("c3_sum", [NCL, 1], F32, kind="Internal").ap()
    EX["mu1_d"] = nc.dram_tensor("mu1_d", [1, NT], F32, kind="Internal").ap()
    EX["r1_d"] = nc.dram_tensor("r1_d", [1, NT], F32, kind="Internal").ap()
    EX["qs1_d"] = nc.dram_tensor("qs1_d", [1, NT], F32, kind="Internal").ap()
    EX["qs3_d"] = nc.dram_tensor("qs3_d", [1, NT], F32, kind="Internal").ap()
    EX["x2T_d"] = nc.dram_tensor("x2T_d", [D, NT], F32, kind="Internal").ap()
    EX["kT_loc"] = nc.dram_tensor("kT_loc", [D, NT], BF16, kind="Internal").ap()
    EX["vT_loc"] = nc.dram_tensor("vT_loc", [D, NT], BF16, kind="Internal").ap()
    EX["qhT_d"] = nc.dram_tensor("qhT_d", [D, NT], BF16, kind="Internal").ap()
    EX["Kst_d"] = nc.dram_tensor("Kst_d", [4*D, NT], BF16, kind="Internal").ap()
    EX["Vst_d"] = nc.dram_tensor("Vst_d", [4*D, NT], BF16, kind="Internal").ap()
    EX["x3T_d"] = nc.dram_tensor("x3T_d", [D, NT], F32, kind="Internal").ap()
    EX["d2T_d"] = nc.dram_tensor("d2T_d", [D, NT], F32, kind="Internal").ap()
    EX["scr_rows1"] = nc.dram_tensor("scr_rows1", [2, NT], F32, kind="Internal").ap()
    EX["scr_rows3"] = nc.dram_tensor("scr_rows3", [2, NT], F32, kind="Internal").ap()
    EX["A_scr"] = nc.dram_tensor("A_scr", [NH, NCL], F32, kind="Internal").ap()

    prog = Prog()
    SEMS = {n: prog.sem(n) for n in ALL_SEM_NAMES}
    prog.canon = {"vector": SEMS["s_dve"], "scalar": SEMS["s_act"], "gpsimd": SEMS["s_gp"]}
    with ExitStack() as st_outer:
        phase1(nc, prog, SEMS, EX)
        phase_barrier(prog)
        phase2(nc, prog, SEMS, EX)
        phase_barrier(prog)
        phase3(nc, prog, SEMS, EX)
        phase_barrier(prog)
        phase4(nc, prog, SEMS, EX)
        phase_barrier(prog)
        alloc_sems(nc, prog, st_outer)
        prog.replay(nc)
    return nc

_CACHE = {}

def _tables():
    g = [1.0, 1.0, 1.0]; dim = 3; n = 8
    masks = sorted(range(n), key=lambda m: (bin(m).count('1'), m))
    inv = {m: i for i, m in enumerate(masks)}
    C = np.zeros((n, n, n), dtype=np.float64)
    for i, a in enumerate(masks):
        for j, b in enumerate(masks):
            sw, at = (0, a >> 1)
            while at:
                sw += bin(at & b).count('1'); at >>= 1
            s = -1.0 if sw & 1 else 1.0
            for t in range(dim):
                if a >> t & 1 and b >> t & 1:
                    s *= float(g[t])
            C[i, j, inv[a ^ b]] = s
    grades = np.array([bin(m).count('1') for m in masks])
    rev = np.where(grades * (grades - 1) // 2 % 2 == 1, -1.0, 1.0)
    met = np.array([float(np.prod([g[t] for t in range(dim) if m >> t & 1] + [1.0])) for m in masks])
    return C, rev, met

def kernel(L0_attn_x, L0_attn_Wq, L0_attn_bq, L0_attn_Wk, L0_attn_bk, L0_attn_Wv, L0_attn_bv,
           L0_attn_bias_weight, L0_ffn_weight, L0_ffn_bias, L0_ffn_centroids, L0_ffn_assignments):
    x = np.asarray(L0_attn_x, dtype=np.float32)
    Wq = np.asarray(L0_attn_Wq, dtype=np.float32); bq = np.asarray(L0_attn_bq, dtype=np.float32)
    Wk = np.asarray(L0_attn_Wk, dtype=np.float32); bk = np.asarray(L0_attn_bk, dtype=np.float32)
    Wv = np.asarray(L0_attn_Wv, dtype=np.float32); bv = np.asarray(L0_attn_bv, dtype=np.float32)
    bw = np.asarray(L0_attn_bias_weight, dtype=np.float32)
    Wf = np.asarray(L0_ffn_weight, dtype=np.float32); bff = np.asarray(L0_ffn_bias, dtype=np.float32)
    cent = np.asarray(L0_ffn_centroids, dtype=np.float32)
    assign = np.asarray(L0_ffn_assignments)

    C, rev, met = _tables()
    biv = C[:, :, [4, 5, 6]]
    scale = 1.0 / np.sqrt(DH)
    Tm = np.stack([(rev[:, None] * biv[:, :, c]).reshape(64) for c in range(3)]).astype(np.float32)
    Dm = np.tile(((rev[:, None] * (scale * np.diag(met))).reshape(64))[None, :], (NH, 1)).astype(np.float32)

    if "nc" not in _CACHE:
        _CACHE["nc"] = build_fused()
    nc = _CACHE["nc"]

    shards_tok = [np.ascontiguousarray(x.reshape(8, NT, D)[c]) for c in range(8)]
    shards = [np.ascontiguousarray(s.T) for s in shards_tok]
    cent_pack = pack_cent(cent)
    consts = const_inputs(["ident_f32", "ident_bf16", "ones_col_f32", "ones_col_bf16", "ones_row_f32", "iota64"])
    asn_pack = np.ascontiguousarray(assign.reshape(NCH, 128).T).astype(np.int32)
    packs = dict(Wf_p=pack_w_panels(Wf), Wq_p=pack_w_panels(Wq), Wk_p=pack_w_panels(Wk), Wv_p=pack_w_panels(Wv),
                 bf_pack=pack_col(bff), bq_pack=pack_col(bq), bk_pack=pack_col(bk), bv_pack=pack_col(bv),
                 assign_pack=asn_pack, bwT=np.ascontiguousarray(bw.T), Tmat=Tm, Dmat_tiled=Dm,
                 cent_pack=cent_pack)

    ins = [dict(xT=shards[c], **packs, **consts) for c in range(8)]
    r = run_bass_kernel_spmd(nc, ins, core_ids=list(range(8))).results

    out = np.concatenate([np.asarray(r[c]["out_tok"]) for c in range(8)], axis=0)
    return out.reshape(2, 4096, D).astype(np.float32)


# version 09
# speedup vs baseline: 1.2184x; 1.1424x over previous
import numpy as np
import ml_dtypes
import concourse.bass as bass
import concourse.mybir as mybir
from concourse.bass_utils import run_bass_kernel_spmd
from contextlib import ExitStack

F32 = mybir.dt.float32
BF16 = mybir.dt.bfloat16
I32 = mybir.dt.int32
AF = mybir.ActivationFunctionType
ALU = mybir.AluOpType
AX = mybir.AxisListType

NT = 1024
D = 4096
NCH = 32
NTT = 8
NCL = 64
NH = 16
DH = 256

class Sem:
    def __init__(self, name):
        self.name = name; self.n = 0; self.h = None

class Prog:
    ENGS = ("sync", "tensor", "vector", "scalar", "gpsimd")
    def __init__(self):
        self.q = {e: [] for e in self.ENGS}
        self.sems = []
        self.hwm = {e: {} for e in self.ENGS}
        self.canon = {}
    def sem(self, name):
        s = Sem(name); self.sems.append(s); return s
    def raw(self, eng, fn):
        self.q[eng].append(fn)
    def wait(self, eng, sem, val=None):
        v = sem.n if val is None else val
        if v <= 0: return
        if self.hwm[eng].get(sem.name, 0) >= v: return
        self.hwm[eng][sem.name] = v
        self.q[eng].append(lambda e, s=sem, v=v: e.wait_ge(s.h, v))
    def dma(self, eng, out, in_, sem, **kw):
        sem.n += 16
        self.q[eng].append(lambda e, o=out, i=in_, s=sem, kw=dict(kw): e.dma_start(out=o, in_=i, **kw).then_inc(s.h, 16))
        return sem.n
    def cc(self, kind, op, groups, in_ap, out_ap, sem):
        sem.n = 1
        self.q["gpsimd"].append(lambda e, k=kind, o=op, g=groups, i=in_ap, a=out_ap, s=sem:
                                e.collective_compute(k, o, replica_groups=g, ins=[i.opt()], outs=[a.opt()]).then_inc(s.h))
        return 1
    def op_acc(self, eng, builder, sem, fence):
        c = self.canon.get(eng)
        if c is not None and c.n > 0:
            self.q[eng].append(lambda e, s=c, v=c.n: e.wait_ge(s.h, v))
        self.q[eng].append(lambda e, b=builder: b(e))
        sem.n += 1
        self.q[eng].append(lambda e, f=fence, s=sem: f(e).then_inc(s.h, 1))
        return sem.n
    def op(self, eng, builder, sem=None):
        c = self.canon.get(eng)
        if c is not None:
            if sem is None:
                sem = c
            assert sem is c, f"compute op on {eng} must use canonical sem"
            if c.n > 0:
                self.q[eng].append(lambda e, s=c, v=c.n: e.wait_ge(s.h, v))
        if sem is not None:
            sem.n += 1
        self.q[eng].append(lambda e, b=builder, s=sem: (b(e).then_inc(s.h, 1) if s is not None else b(e)))
        return None if sem is None else sem.n
    def barrier(self, engs, sems):
        for e in engs:
            for s in sems:
                self.wait(e, s)
    def replay(self, nc):
        for s in self.sems:
            pass
        with nc.Block() as block:
            for eng in self.ENGS:
                ops = self.q[eng]
                if not ops: continue
                dec = getattr(block, eng)
                def make(ops):
                    def body(e):
                        for fn in ops:
                            fn(e)
                    return body
                dec(make(ops))

def alloc_sems(nc, prog, stack):
    for s in prog.sems:
        if s.n > 0:
            s.h = stack.enter_context(nc.semaphore(s.name))

ALL_SEM_NAMES = ["s_ld", "s_pe", "s_dve", "s_act", "s_st", "s_gp",
                 "s_x0", "s_x1", "s_c0", "s_c1", "s_st0", "s_st1", "s_rows", "s_bd",
                 "s_h0", "s_h1", "s_xt0", "s_xt1", "s_xt2", "s_xt3",
                 "s_vt0", "s_vt1", "s_kq0", "s_kq1", "s_xh0", "s_xh1",
                 "s_cc1", "s_cck", "s_ccv", "s_cc3"]

def phase_barrier(prog):
    for e in Prog.ENGS:
        for s in prog.sems:
            if s.name.startswith("s_cc"):
                continue
            prog.wait(e, s)

def pack_w_panels(W):
    Wt = np.ascontiguousarray(W.T)
    t = Wt.reshape(32, 128, 32, 128)
    t = np.ascontiguousarray(t.transpose(2, 1, 0, 3))
    return t.reshape(32, 128, 4096)

def pack_col(b):
    return np.ascontiguousarray(b.reshape(32, 128).T)

def pack_cent(cent):
    centT = cent.T
    t = centT.reshape(32, 128, 64)
    return np.ascontiguousarray(t.transpose(1, 0, 2)).reshape(128, 32*64)

CONSTS = {}
def make_consts():
    CONSTS["ident_f32"] = np.eye(128, dtype=np.float32)
    CONSTS["ident_bf16"] = np.eye(128, dtype=ml_dtypes.bfloat16)
    CONSTS["ones_col_f32"] = np.ones((128, 1), np.float32)
    CONSTS["ones_col_bf16"] = np.ones((128, 1), ml_dtypes.bfloat16)
    CONSTS["ones_row_f32"] = np.ones((1, 128), np.float32)
    CONSTS["iota64"] = np.tile(np.arange(64, dtype=np.float32)[None, :], (128, 1))
make_consts()

def declare_consts(nc, names):
    return {n: nc.dram_tensor("c_" + n, list(CONSTS[n].shape),
                              F32 if CONSTS[n].dtype == np.float32 else BF16,
                              kind="ExternalInput").ap() for n in names}

def const_inputs(names):
    return {"c_" + n: CONSTS[n] for n in names}

def emit_ln_dots(prog, nc, T, xT_dram, sb, with_dots=True):
    s_ld, s_cast, s_pe, s_dve, s_act = T["s_ld"], T["s_cast"], T["s_pe"], T["s_dve"], T["s_act"]
    s_x = [T["s_x0"], T["s_x1"]]
    s_rows = T["s_rows"]
    if with_dots:
        prog.wait("tensor", s_ld)
        for c in range(NCH):
            prog.op("tensor", lambda e, c=c: e.matmul(
                T["p_scr"][0:1, 0:64], sb["ones_col_f32"][:, :], sb["cent"][:, c*64:(c+1)*64],
                start=(c == 0), stop=(c == NCH-1)), s_pe if c == NCH-1 else None)
        v_pe_csum = s_pe.n
        prog.wait("vector", s_pe, v_pe_csum)
        prog.op("vector", lambda e: e.tensor_copy(sb["csrow"][:, :], T["p_scr"][0:1, 0:64]), s_dve)
        v_csrow = s_dve.n
        prog.wait("tensor", s_dve, v_csrow)
        prog.op("tensor", lambda e: e.matmul(T["p_scr2"][:, 0:64], sb["ones_row_f32"][:, :], sb["csrow"][:, :],
                                             start=True, stop=True), s_pe)
        v_pe_csbc = s_pe.n
        prog.wait("vector", s_pe, v_pe_csbc)
        prog.op("vector", lambda e: e.tensor_copy(sb["csbc"][:, :], T["p_scr2"][:, 0:64]), s_dve)
        v_csbc = s_dve.n
        prog.wait("tensor", s_dve, v_csbc)

    ld_base = s_ld.n
    cast_base = s_cast.n
    for c in range(NCH):
        buf = c % 2
        if c >= 2:
            prog.wait("sync", s_pe, T["pe_done_chunk"][c-2])
        v_ld = prog.dma("sync", sb["xbuf"][buf][:, :], xT_dram[c*128:(c+1)*128, :], s_x[buf])
        prog.wait("scalar", s_x[buf], v_ld)
        if c >= 2:
            prog.wait("scalar", s_pe, T["pe_done_chunk"][c-2])
        prog.op("scalar", lambda e, buf=buf: e.activation(sb["sqf"][buf][:, :], sb["xbuf"][buf][:, :], AF.Square), s_act)
        T["act_done_chunk"][c] = s_act.n
        prog.wait("tensor", s_x[buf], v_ld)
        prog.wait("tensor", s_act, T["act_done_chunk"][c])
        for half in range(2):
            prog.op("tensor", lambda e, c=c, buf=buf, half=half: e.matmul(
                T["p_s1"][0:1, half*512:(half+1)*512], sb["ones_col_f32"][:, :],
                sb["xbuf"][buf][:, half*512:(half+1)*512], start=(c == 0), stop=(c == NCH-1)), None)
            prog.op("tensor", lambda e, c=c, buf=buf, half=half: e.matmul(
                T["p_s2"][0:1, half*512:(half+1)*512], sb["ones_col_f32"][:, :],
                sb["sqf"][buf][:, half*512:(half+1)*512], start=(c == 0), stop=(c == NCH-1)),
                None if half == 0 else s_pe)
        T["pe_done_chunk"][c] = s_pe.n
    v_pe_all = s_pe.n

    prog.wait("vector", s_pe, v_pe_all)
    prog.op("vector", lambda e: e.tensor_copy(sb["s1row"][:, :], T["p_s1"][0:1, :]), s_dve)
    prog.op("vector", lambda e: e.tensor_copy(sb["s2row"][:, :], T["p_s2"][0:1, :]), s_dve)
    prog.op("vector", lambda e: e.tensor_scalar(sb["murow"][:, :], sb["s1row"][:, :], 1.0/D, None, op0=ALU.mult), s_dve)
    prog.op("vector", lambda e: e.tensor_scalar(sb["msqrow"][:, :], sb["s2row"][:, :], 1.0/D, None, op0=ALU.mult), s_dve)
    prog.op("vector", lambda e: e.tensor_tensor(sb["m2row"][:, :], sb["murow"][:, :], sb["murow"][:, :], op=ALU.mult), s_dve)
    prog.op("vector", lambda e: e.tensor_tensor(sb["varrow"][:, :], sb["msqrow"][:, :], sb["m2row"][:, :], op=ALU.subtract), s_dve)
    v_var = s_dve.n
    prog.wait("scalar", s_dve, v_var)
    prog.op("scalar", lambda e: e.activation(sb["lnrow"][:, :], sb["varrow"][:, :], AF.Ln, bias=sb["eps1"][:, :]), s_act)
    prog.op("scalar", lambda e: e.activation(sb["rrow"][:, :], sb["lnrow"][:, :], AF.Exp, scale=-0.5), s_act)
    v_rrow = s_act.n
    if not with_dots:
        return v_rrow

    prog.wait("vector", s_act, v_rrow)
    prog.op("vector", lambda e: e.tensor_scalar(sb["negr10row"][:, :], sb["rrow"][:, :], -10.0, None, op0=ALU.mult), s_dve)
    v_negr10 = s_dve.n
    prog.wait("sync", s_dve, v_negr10)
    prog.dma("sync", T["scr_dram"][0:1, :], sb["murow"][:, :], s_rows)
    v_w1 = prog.dma("sync", T["scr_dram"][1:2, :], sb["negr10row"][:, :], s_rows)
    prog.wait("sync", s_rows, v_w1)
    v_rc = prog.dma("sync", sb["rowcols"][:, :], T["scr_dram"].rearrange("a (b p) -> (a b) p", p=128)[:, :], s_rows)
    prog.wait("tensor", s_rows, v_rc)
    prog.op("tensor", lambda e: e.transpose(T["p_scr"][:, 16:32], sb["rowcols"][0:16, :], sb["ident_f32"][0:16, 0:16]), s_pe)
    v_pe_cols = s_pe.n
    prog.wait("vector", s_pe, v_pe_cols)
    prog.op("vector", lambda e: e.tensor_copy(sb["mucols"][:, :], T["p_scr"][:, 16:24]), s_dve)
    prog.op("vector", lambda e: e.tensor_copy(sb["nr10cols"][:, :], T["p_scr"][:, 24:32]), s_dve)
    v_cols = s_dve.n

    v_pd_prev = 0
    v_hits_prev = 0
    v_counts_prev = 0
    v_lg_prev = {}
    for t in range(NTT):
        pd_bank = T["p_dots"][t % 2]
        NXT = len(sb["xtile"])
        prog.wait("tensor", s_dve, v_cols)
        if t >= 2:
            prog.wait("tensor", s_dve, v_lg_prev[t-2])
        for c in range(NCH):
            tb = c % NXT
            gi = t*NCH + c
            if gi >= NXT:
                pg, pc = divmod(gi - NXT, NCH)
                prog.wait("sync", s_pe, T[f"dots_mm_{pg}_{pc}"])
            v_tl = prog.dma("sync", sb["xtile"][tb][:, :], xT_dram[c*128:(c+1)*128, t*128:(t+1)*128],
                            T[f"s_xt{tb}"])
            prog.wait("tensor", T[f"s_xt{tb}"], v_tl)
            prog.op("tensor", lambda e, c=c, tb=tb, pd_bank=pd_bank: e.matmul(
                pd_bank[:, 0:64], sb["xtile"][tb][:, :], sb["cent"][:, c*64:(c+1)*64],
                start=(c == 0), stop=(c == NCH-1)), s_pe)
            T[f"dots_mm_{t}_{c}"] = s_pe.n
        v_dots_t = s_pe.n
        prog.wait("vector", s_pe, v_dots_t)
        if t > 0:
            prog.wait("vector", s_act, v_pd_prev)
        prog.op("vector", lambda e, t=t, pd_bank=pd_bank: e.scalar_tensor_tensor(
            sb["lg"][:, :], sb["csbc"][:, :], sb["mucols"][:, t:t+1], pd_bank[:, 0:64],
            op0=ALU.mult, op1=ALU.subtract), s_dve)
        prog.op("vector", lambda e, t=t: e.tensor_scalar(sb["lg"][:, :], sb["lg"][:, :], sb["nr10cols"][:, t:t+1], None, op0=ALU.mult), s_dve)
        v_lg_prev[t] = s_dve.n
        prog.op("vector", lambda e: e.tensor_reduce(sb["negmax"][:, :], sb["lg"][:, :], axis=AX.X, op=ALU.max, negate=True), s_dve)
        v_lg = s_dve.n
        prog.wait("scalar", s_dve, v_lg)
        if t > 0:
            prog.wait("scalar", s_dve, v_hits_prev)
        prog.op("scalar", lambda e: e.activation(sb["pd"][:, :], sb["lg"][:, :], AF.Exp, bias=sb["negmax"][:, :], accum_out=sb["z"][:, :]), None)
        prog.op("scalar", lambda e: e.activation(sb["z001"][:, :], sb["z"][:, :], AF.Identity, scale=0.01), None)
        prog.op("scalar", lambda e: e.activation(sb["junk"][:, :], sb["z001"][:, :], AF.Identity), s_act)
        v_pd = s_act.n
        prog.wait("vector", s_act, v_pd)
        if t > 0:
            prog.wait("vector", s_pe, v_counts_prev)
        prog.op("vector", lambda e, t=t: e.tensor_scalar(sb["hits"][:, :], sb["pd"][:, :], sb["z001"][:, :], None, op0=ALU.is_gt,
                                                          op1=ALU.max, accum_out=sb["qs_all"][:, t:t+1]), s_dve)
        v_hits_prev = s_dve.n
        v_pd_prev = v_pd
        v_hits = s_dve.n
        prog.wait("tensor", s_dve, v_hits)
        prog.op("tensor", lambda e, t=t: e.matmul(T["p_counts"][0:64, 0:1], sb["hits"][:, :], sb["ones_col_bf16"][:, :],
                                                  start=(t == 0), stop=(t == NTT-1)), s_pe)
        v_counts_prev = s_pe.n
    return s_pe.n

def phase1(nc, prog, SEMS, EX):
    xT = EX["xT"]
    cent = EX["cent_pack"]
    cns = EX["cns"]
    o_counts = EX["c1_loc"]
    o_mu = EX["mu1_d"]
    o_r = EX["r1_d"]
    o_qs = EX["qs1_d"]

    with ExitStack() as st:
        def sbuf(name, shape, dt):
            return st.enter_context(nc.sbuf_tensor("p1_"+name, shape, dt))
        sb = {}
        sb["cent"] = sbuf("cent", [128, NCH*NCL], F32)
        sb["ident_f32"] = sbuf("identf", [128, 128], F32)
        sb["ones_col_f32"] = sbuf("onescf", [128, 1], F32)
        sb["ones_col_bf16"] = sbuf("onescb", [128, 1], BF16)
        sb["ones_row_f32"] = sbuf("onesr", [1, 128], F32)
        sb["xbuf"] = [sbuf(f"xbuf{i}", [128, NT], F32) for i in range(2)]
        sb["sqf"] = [sbuf(f"sqf{i}", [128, NT], F32) for i in range(2)]
        sb["xtile"] = [sbuf(f"xtile{i}", [128, 128], F32) for i in range(4)]
        for n in ["csrow"]: sb[n] = sbuf(n, [1, NCL], F32)
        sb["csbc"] = sbuf("csbc", [128, NCL], F32)
        for n in ["s1row","s2row","murow","msqrow","m2row","varrow","lnrow","rrow","negr10row"]:
            sb[n] = sbuf(n, [1, NT], F32)
        sb["eps1"] = sbuf("eps1", [1, 1], F32)
        sb["mucols"] = sbuf("mucols", [128, NTT], F32)
        sb["nr10cols"] = sbuf("nr10cols", [128, NTT], F32)
        sb["lg"] = sbuf("lg", [128, NCL], F32)
        sb["negmax"] = sbuf("negmax", [128, 1], F32)
        sb["pd"] = sbuf("pd", [128, NCL], F32)
        sb["z"] = sbuf("z", [128, 1], F32)
        sb["zinv"] = sbuf("zinv", [128, 1], F32)
        sb["dn"] = sbuf("dn", [128, NCL], F32)
        sb["hits"] = sbuf("hits", [128, NCL], BF16)
        sb["qs_all"] = sbuf("qs_all", [128, NTT], F32)
        sb["cnt"] = sbuf("cnt", [64, 1], F32)
        sb["junk"] = sbuf("junk", [128, 1], F32)
        sb["z001"] = sbuf("z001", [128, 1], F32)
        sb["rowcols"] = sbuf("rowcols", [16, 128], F32)
        sb["qsT"] = sbuf("qsT", [NTT, 128], F32)

        T = {}
        T["scr_dram"] = EX["scr_rows1"]
        T["p_s1"] = st.enter_context(nc.psum_tensor("p1_p_s1", [1, NT], F32))
        T["p_s2"] = st.enter_context(nc.psum_tensor("p1_p_s2", [1, NT], F32))
        _pd = st.enter_context(nc.psum_tensor("p1_p_dots", [128, 1024], F32))
        T["p_dots"] = [_pd[:, 0:512], _pd[:, 512:1024]]
        T["p_scr"] = st.enter_context(nc.psum_tensor("p1_p_scr", [128, 512], F32))
        _ps2 = st.enter_context(nc.psum_tensor("p1_p_scr2", [128, 512], F32))
        T["p_scr2"] = _ps2[:, 0:256]
        T["p_counts"] = _ps2[:, 256:512]
        T.update(SEMS)
        T["s_cast"] = SEMS["s_ld"]
        T["pe_done_chunk"] = {}
        T["act_done_chunk"] = {}

        prog.dma("sync", sb["cent"][:, :], cent[:, :], T["s_ld"])
        prog.dma("sync", sb["ident_f32"][:, :], cns["ident_f32"][:, :], T["s_ld"])
        prog.dma("sync", sb["ones_col_f32"][:, :], cns["ones_col_f32"][:, :], T["s_ld"])
        prog.dma("sync", sb["ones_col_bf16"][:, :], cns["ones_col_bf16"][:, :], T["s_ld"])
        prog.dma("sync", sb["ones_row_f32"][:, :], cns["ones_row_f32"][:, :], T["s_ld"])
        prog.op("vector", lambda e: e.memset(sb["eps1"][:, :], 1e-5), T["s_dve"])

        v_done = emit_ln_dots(prog, nc, T, xT, sb, with_dots=True)

        prog.wait("vector", T["s_pe"], v_done)
        prog.op("vector", lambda e: e.tensor_copy(sb["cnt"][:, :], T["p_counts"][0:64, 0:1]), T["s_dve"])
        v_cnt = T["s_dve"].n
        prog.wait("tensor", T["s_pe"], v_done)
        prog.op("tensor", lambda e: e.transpose(T["p_scr"][0:NTT, 0:128], sb["qs_all"][:, :], sb["ident_f32"][:, :]), T["s_pe"])
        v_qt = T["s_pe"].n
        prog.wait("vector", T["s_pe"], v_qt)
        prog.op("vector", lambda e: e.tensor_copy(sb["qsT"][:, :], T["p_scr"][0:NTT, 0:128]), T["s_dve"])
        v_qsT = T["s_dve"].n
        prog.wait("sync", T["s_dve"], v_cnt)
        v_cst = prog.dma("sync", o_counts[:, :], sb["cnt"][:, :], T["s_st"])
        prog.dma("sync", o_mu[:, :], sb["murow"][:, :], T["s_st"])
        prog.dma("sync", o_r[:, :], sb["rrow"][:, :], T["s_st"])
        prog.wait("sync", T["s_dve"], v_qsT)
        prog.dma("sync", o_qs[0:1, :].rearrange("a (b p) -> (a b) p", p=128), sb["qsT"][:, :], T["s_st"])

        prog.wait("gpsimd", T["s_st"], v_cst)
        prog.cc("AllReduce", ALU.add, [[0, 1, 2, 3, 4, 5, 6, 7]], EX["c1_loc"], EX["c1_sum"], T["s_cc1"])

def emit_ln_apply(prog, nc, T, sb, xT_dram, murow, rrow, xn_sb_name="xn"):
    s_ld, s_pe, s_dve, s_act, s_gp = T["s_ld"], T["s_pe"], T["s_dve"], T["s_act"], T["s_gp"]
    s_x = [T["s_x0"], T["s_x1"]]
    prog.op("vector", lambda e: e.scalar_tensor_tensor(sb["nmrow"][:, :], murow, -1.0, rrow, op0=ALU.mult, op1=ALU.mult), s_dve)
    v_nm = s_dve.n
    prog.wait("tensor", s_dve, v_nm)
    for half in range(2):
        prog.op("tensor", lambda e, half=half: e.matmul(T["p_s1t"][:, half*512:(half+1)*512], sb["ones_row_f32"][:, :],
                                                        rrow[0:1, half*512:(half+1)*512], start=True, stop=True), None)
        prog.op("tensor", lambda e, half=half: e.matmul(T["p_s2t"][:, half*512:(half+1)*512], sb["ones_row_f32"][:, :],
                                                        sb["nmrow"][0:1, half*512:(half+1)*512], start=True, stop=True), s_pe if half == 1 else None)
    v_bc = s_pe.n
    prog.wait("vector", s_pe, v_bc)
    prog.op("vector", lambda e: e.tensor_copy(sb["rbc"][:, :], T["p_s1t"][:, :]), s_dve)
    prog.op("vector", lambda e: e.tensor_copy(sb["nmbc"][:, :], T["p_s2t"][:, :]), s_dve)
    xn = sb[xn_sb_name]
    done_mark = {}
    for c in range(NCH):
        buf = c % 2
        if c >= 2:
            prog.wait("sync", s_dve, done_mark[c-2][0])
        v_ld = prog.dma("sync", sb["xbuf"][buf][:, :], xT_dram[c*128:(c+1)*128, :], s_x[buf])
        prog.wait("vector", s_x[buf], v_ld)
        prog.op("vector", lambda e, buf=buf: e.tensor_tensor(sb["t1"][buf][:, :], sb["xbuf"][buf][:, :], sb["nmbc"][:, :], op=ALU.add), s_dve)
        v_add = s_dve.n
        prog.op("vector", lambda e, buf=buf, c=c: e.tensor_tensor(xn[:, c*NT:(c+1)*NT], sb["t1"][buf][:, :], sb["rbc"][:, :], op=ALU.mult), s_dve)
        done_mark[c] = (v_add, s_dve.n)
    return s_dve.n

def emit_dense(prog, nc, T, sb, W_dram, xn_view, epilogue, tag):
    s_pe = T["s_pe"]
    s_c = [T["s_c0"], T["s_c1"]]
    prog.wait("gpsimd", s_pe)
    epi_done = {}
    for oc in range(NCH):
        buf = oc % 2
        if oc >= 2:
            prog.wait("gpsimd", s_pe, T[f"dense_mm_done_{tag}"][oc-2])
        v_cast = prog.dma("gpsimd", sb["panel"][buf][:, :], W_dram[oc, :, :], s_c[buf], max_dma_last_dim=8192)
        prog.wait("tensor", s_c[buf], v_cast)
        if oc >= 2:
            eng, val = epi_done[oc-2]
            prog.wait("tensor", T[eng], val)
        for c in range(NCH):
            for half in range(2):
                prog.op("tensor", lambda e, buf=buf, c=c, half=half: e.matmul(
                    T["p_d"][buf][:, half*512:(half+1)*512],
                    sb["panel"][buf][:, c*128:(c+1)*128],
                    xn_view[:, c*NT + half*512: c*NT + (half+1)*512],
                    start=(c == 0), stop=(c == NCH-1)), s_pe if (c == NCH-1 and half == 1) else None)
        T[f"dense_mm_done_{tag}"][oc] = s_pe.n
        epi_done[oc] = epilogue(oc, s_pe.n)
    return epi_done


def emit_routing(prog, nc, T, sb, has_bias=True):
    s_ld, s_pe, s_dve = T["s_ld"], T["s_pe"], T["s_dve"]
    prog.op("vector", lambda e: e.tensor_scalar(sb["cscol"][:, :], sb["cntsum"][:, :], 0.5, None, op0=ALU.is_gt), s_dve)
    v2 = s_dve.n
    prog.wait("tensor", s_dve, v2)
    prog.op("tensor", lambda e: e.transpose(T["p_s2t"][0:1, 0:64], sb["cscol"][:, :], sb["ident_f32"][0:64, 0:64]), s_pe)
    v3 = s_pe.n
    prog.wait("vector", s_pe, v3)
    prog.op("vector", lambda e: e.tensor_copy(sb["csrow1"][:, :], T["p_s2t"][0:1, 0:64]), s_dve)
    v4 = s_dve.n
    prog.wait("tensor", s_dve, v4)
    prog.op("tensor", lambda e: e.matmul(T["p_s1t"][:, 0:64], sb["ones_row_f32"][:, :], sb["csrow1"][:, :],
                                         start=True, stop=True), s_pe)
    for half in range(2):
        prog.op("tensor", lambda e, half=half: e.matmul(T["p_s2t"][:, half*512:(half+1)*512], sb["ones_row_f32"][:, :],
                                             sb["qs1r"][0:1, half*512:(half+1)*512], start=True, stop=True), s_pe if half == 1 else None)
    v5 = s_pe.n
    prog.wait("vector", s_pe, v5)
    prog.op("vector", lambda e: e.tensor_copy(sb["cs_bc"][:, :], T["p_s1t"][:, 0:64]), s_dve)
    prog.op("vector", lambda e: e.tensor_copy(sb["qs_bc"][:, :], T["p_s2t"][:, :]), s_dve)
    prog.op("vector", lambda e: e.tensor_copy(sb["asn_f"][:, :], sb["asn_i"][:, :]), s_dve)
    v6 = s_dve.n
    for c in range(NCH):
        prog.op("vector", lambda e, c=c: e.tensor_scalar(sb["eq"][:, :], sb["iota64"][:, :], sb["asn_f"][:, c:c+1], None, op0=ALU.is_equal), s_dve)
        prog.op("vector", lambda e: e.tensor_tensor(sb["eqp"][:, :], sb["eq"][:, :], sb["cs_bc"][:, :], op=ALU.mult), s_dve)
        prog.op("vector", lambda e, c=c: e.tensor_reduce(sb["rs05"][:, c:c+1], sb["eqp"][:, :], axis=AX.X, op=ALU.add), s_dve)
    prog.op("vector", lambda e: e.tensor_scalar(sb["rs05"][:, :], sb["rs05"][:, :], 0.5, None, op0=ALU.mult), s_dve)
    if has_bias:
        prog.op("vector", lambda e: e.tensor_tensor(sb["brs05"][:, :], sb["bf"][:, :], sb["rs05"][:, :], op=ALU.mult), s_dve)
    return s_dve.n, v6

def phase2(nc, prog, SEMS, EX):
    xT = EX["xT"]
    qs1row = EX["qs1_d"]
    murow1 = EX["mu1_d"]
    rrow1 = EX["r1_d"]
    Wf = EX["Wf_p"]; Wq = EX["Wq_p"]; Wk = EX["Wk_p"]; Wv = EX["Wv_p"]
    bf_p = EX["bf_pack"]; bq_p = EX["bq_pack"]; bk_p = EX["bk_pack"]; bv_p = EX["bv_pack"]
    asn_p = EX["assign_pack"]
    bwT = EX["bwT"]; Tmat = EX["Tmat"]; Dmat = EX["Dmat_tiled"]
    cns = EX["cns"]
    o_x2T = EX["x2T_d"]
    o_kT = EX["kT_loc"]
    o_vT = EX["vT_loc"]
    o_qhT = EX["qhT_d"]
    A_dram = EX["A_scr"]

    with ExitStack() as st:
        def sbuf(name, shape, dt):
            return st.enter_context(nc.sbuf_tensor("p2_" + name, shape, dt))
        sb = {}
        for n in ["ident_f32", "ones_col_f32", "ones_row_f32", "iota64"]:
            sb[n] = sbuf(n, list(CONSTS[n].shape), F32)
        for n in ["ident_bf16", "ones_col_bf16"]:
            sb[n] = sbuf(n, list(CONSTS[n].shape), BF16)
        sb["xn"] = sbuf("xn", [128, NCH*NT], BF16)
        sb["panel"] = [sbuf(f"panel{i}", [128, D], BF16) for i in range(2)]
        sb["xbuf"] = [sbuf(f"xbuf{i}", [128, NT], F32) for i in range(2)]
        sb["t1"] = [sbuf(f"t1_{i}", [128, NT], F32) for i in range(2)]
        sb["scr8k"] = sbuf("scr8k", [128, 2*NT], F32)
        sb["masked"] = [sb["scr8k"][:, 0:NT], sb["scr8k"][:, NT:2*NT]]
        sb["x1row"] = sb["xbuf"]
        sb["x2row"] = [sbuf(f"x2row{i}", [128, NT], F32) for i in range(2)]
        sb["sqf"] = [sbuf(f"sqf{i}", [128, NT], F32) for i in range(2)]
        sb["kvout"] = [sbuf(f"kvout{i}", [128, NT], BF16) for i in range(2)]
        sb["qout"] = [sbuf(f"qout{i}", [128, NT], BF16) for i in range(2)]
        sb["qhout"] = sb["kvout"]
        sb["qs_bc"] = sbuf("qs_bc", [128, NT], F32)
        sb["rbc"] = sbuf("rbc", [128, NT], F32)
        sb["nmbc"] = sbuf("nmbc", [128, NT], F32)
        sb["cntsum"] = sbuf("cntsum", [NCL, 1], F32)
        sb["cscol"] = sbuf("cscol", [NCL, 1], F32)
        sb["csrow1"] = sbuf("csrow1", [1, NCL], F32)
        sb["cs_bc"] = sbuf("cs_bc", [128, NCL], F32)
        sb["asn_i"] = sbuf("asn_i", [128, NCH], I32)
        sb["asn_f"] = sbuf("asn_f", [128, NCH], F32)
        sb["eq"] = sbuf("eq", [128, NCL], F32)
        sb["eqp"] = sbuf("eqp", [128, NCL], F32)
        sb["rs05"] = sbuf("rs05", [128, NCH], F32)
        sb["brs05"] = sbuf("brs05", [128, NCH], F32)
        sb["bf"] = sbuf("bf", [128, NCH], F32)
        sb["bq"] = sbuf("bq", [128, NCH], F32)
        sb["bk"] = sbuf("bk", [128, NCH], F32)
        sb["bv"] = sbuf("bv", [128, NCH], F32)
        sb["bwT"] = sbuf("bwT", [3, NH], F32)
        sb["Tmat"] = sbuf("Tmat", [3, NCL], F32)
        sb["Dmat"] = sbuf("Dmat", [NH, NCL], F32)
        sb["Aflat"] = sbuf("Aflat", [NH, NCL], F32)
        sb["BDf"] = sbuf("BDf", [128, NH*128], F32)
        sb["BD"] = sbuf("BD", [128, NH*128], BF16)
        sb["mu1r"] = sbuf("mu1r", [1, NT], F32)
        sb["r1r"] = sbuf("r1r", [1, NT], F32)
        sb["qs1r"] = sbuf("qs1r", [1, NT], F32)
        for n in ["murow","rrow","nmrow","rowA","rowB"]:
            sb[n] = sbuf(n, [1, NT], F32)
        sb["eps1"] = sbuf("eps1", [1, 1], F32)

        T = {}
        T["p_d"] = [st.enter_context(nc.psum_tensor(f"p2_p_d{i}", [128, NT], F32)) for i in range(2)]
        T["p_s1t"] = st.enter_context(nc.psum_tensor("p2_p_s1t", [128, NT], F32))
        T["p_s2t"] = st.enter_context(nc.psum_tensor("p2_p_s2t", [128, NT], F32))
        T["p_s1"] = T["p_s1t"]
        T["p_s2"] = T["p_s2t"]
        T.update(SEMS)
        T["s_cast"] = SEMS["s_ld"]
        T["pe_done_chunk"] = {}
        T["act_done_chunk"] = {}
        for tag in ["d1", "wq", "wk", "wv"]:
            T[f"dense_mm_done_{tag}"] = {}

        s_ld, s_cast, s_pe, s_dve, s_act, s_st, s_gp = (T[k] for k in ["s_ld","s_cast","s_pe","s_dve","s_act","s_st","s_gp"])

        for n in ["ident_f32", "ones_col_f32", "ones_row_f32", "iota64", "ident_bf16", "ones_col_bf16"]:
            prog.dma("sync", sb[n][:, :], cns[n][:, :], s_ld)
        prog.wait("sync", T["s_cc1"], 1)
        prog.dma("sync", sb["cntsum"][:, :], EX["c1_sum"][:, :], s_ld)
        prog.dma("sync", sb["qs1r"][:, :], qs1row[:, :], s_ld)
        prog.dma("sync", sb["mu1r"][:, :], murow1[:, :], s_ld)
        prog.dma("sync", sb["r1r"][:, :], rrow1[:, :], s_ld)
        prog.dma("sync", sb["bf"][:, :], bf_p[:, :], s_ld)
        prog.dma("sync", sb["bq"][:, :], bq_p[:, :], s_ld)
        prog.dma("sync", sb["bk"][:, :], bk_p[:, :], s_ld)
        prog.dma("sync", sb["bv"][:, :], bv_p[:, :], s_ld)
        prog.dma("sync", sb["asn_i"][:, :], asn_p[:, :], s_ld)
        prog.dma("sync", sb["bwT"][:, :], bwT[:, :], s_ld)
        prog.dma("sync", sb["Tmat"][:, :], Tmat[:, :], s_ld)
        prog.dma("sync", sb["Dmat"][:, :], Dmat[:, :], s_ld)
        v_setup_ld = s_ld.n
        prog.op("vector", lambda e: e.memset(sb["eps1"][:, :], 1e-5), s_dve)

        prog.wait("vector", s_ld, v_setup_ld)
        emit_ln_apply(prog, nc, T, sb, xT, sb["mu1r"][:, :], sb["r1r"][:, :], "xn")
        v_xn1 = T["s_dve"].n

        prog.wait("tensor", s_ld, v_setup_ld)
        prog.wait("tensor", s_dve)
        v_mask_ready, v6 = emit_routing(prog, nc, T, sb, has_bias=True)

        prog.op("vector", lambda e: e.memset(sb["BDf"][:, :], 0.0), s_dve)
        v_bdz = s_dve.n
        prog.wait("tensor", s_ld, v_setup_ld)
        prog.wait("tensor", s_dve, v6)
        prog.op("tensor", lambda e: e.matmul(T["p_s1t"][0:NH, 64:128], sb["bwT"][:, :], sb["Tmat"][:, :], start=True, stop=True), s_pe)
        v_bd1 = s_pe.n
        prog.wait("vector", s_pe, v_bd1)
        prog.op("vector", lambda e: e.tensor_tensor(sb["Aflat"][:, :], T["p_s1t"][0:NH, 64:128], sb["Dmat"][:, :], op=ALU.add), s_dve)
        v_bd2 = s_dve.n
        prog.wait("sync", s_dve, v_bd2)
        v_bd3 = prog.dma("sync", A_dram[:, :], sb["Aflat"][:, :], T["s_bd"])
        prog.wait("sync", T["s_bd"], v_bd3)
        prog.wait("sync", s_dve, v_bdz)
        for d in range(16):
            prog.dma("sync", sb["BDf"][8*d:8*d+8, :].rearrange("i (h j) -> i h j", j=128)[:, :, 8*d:8*d+8],
                     A_dram[:, :].rearrange("h (i j) -> i h j", j=8), T["s_bd"])
        v_bd4 = T["s_bd"].n
        prog.wait("vector", T["s_bd"], v_bd4)
        prog.op("vector", lambda e: e.tensor_copy(sb["BD"][:, :], sb["BDf"][:, :]), s_dve)
        v_bd_ready = s_dve.n

        prog.barrier(["sync", "tensor", "vector", "scalar", "gpsimd"], [s_gp, s_dve])
        prog.wait("tensor", s_dve, v_xn1)
        prog.wait("scalar", s_dve, v_mask_ready)
        x2_state = {}
        stats_pend = {}
        def emit_stats(oc2):
            buf2 = oc2 % 2
            prog.wait("tensor", s_dve, stats_pend[oc2][0])
            prog.wait("tensor", s_act, stats_pend[oc2][1])
            for half in range(2):
                prog.op("tensor", lambda e, buf2=buf2, oc2=oc2, half=half: e.matmul(
                    T["p_s1t"][0:1, half*512:(half+1)*512], sb["ones_col_f32"][:, :],
                    sb["x2row"][buf2][:, half*512:(half+1)*512], start=(oc2 == 0), stop=(oc2 == NCH-1)), None)
                prog.op("tensor", lambda e, buf2=buf2, oc2=oc2, half=half: e.matmul(
                    T["p_s2t"][0:1, half*512:(half+1)*512], sb["ones_col_f32"][:, :],
                    sb["sqf"][buf2][:, half*512:(half+1)*512], start=(oc2 == 0), stop=(oc2 == NCH-1)),
                    s_pe if half == 1 else None)
            x2_state[oc2][2] = s_pe.n
        def epi_d1(oc, v_mm):
            buf = oc % 2
            if oc >= 2:
                emit_stats(oc - 2)
            prog.wait("scalar", s_pe, v_mm)
            if oc >= 2:
                prog.wait("scalar", s_dve, x2_state[oc-2][0])
            prog.op("scalar", lambda e, buf=buf, oc=oc: e.activation(sb["masked"][buf][:, :], T["p_d"][buf][:, :],
                    AF.Identity, bias=sb["brs05"][:, oc:oc+1], scale=sb["rs05"][:, oc:oc+1]), s_act)
            v_epi = s_act.n
            if oc >= 2:
                prog.wait("sync", s_dve, x2_state[oc-2][1])
            v_x1 = prog.dma("sync", sb["x1row"][buf][:, :], xT[oc*128:(oc+1)*128, :], T["s_x0" if buf == 0 else "s_x1"])
            prog.wait("vector", s_act, v_epi)
            prog.op("vector", lambda e, buf=buf: e.tensor_tensor(sb["t1"][buf][:, :], sb["masked"][buf][:, :], sb["qs_bc"][:, :], op=ALU.mult), s_dve)
            v_mult = s_dve.n
            prog.wait("vector", T["s_x0" if buf == 0 else "s_x1"], v_x1)
            if oc >= 2:
                prog.wait("vector", s_pe, x2_state[oc-2][2])
                prog.wait("vector", s_act, stats_pend[oc-2][1])
                prog.wait("vector", T["s_st0" if buf == 0 else "s_st1"], x2_state[oc-2][3])
            prog.op("vector", lambda e, buf=buf: e.tensor_tensor(sb["x2row"][buf][:, :], sb["t1"][buf][:, :], sb["x1row"][buf][:, :], op=ALU.add), s_dve)
            v_x2 = s_dve.n
            prog.wait("sync", s_dve, v_x2)
            v_store = prog.dma("sync", o_x2T[oc*128:(oc+1)*128, :], sb["x2row"][buf][:, :], T["s_st0" if buf == 0 else "s_st1"])
            prog.wait("scalar", s_dve, v_x2)
            prog.op("scalar", lambda e, buf=buf: e.activation(sb["sqf"][buf][:, :], sb["x2row"][buf][:, :], AF.Square), s_act)
            v_sq = s_act.n
            stats_pend[oc] = (v_x2, v_sq)
            x2_state[oc] = [v_mult, v_x2, 0, v_store]
            if oc == NCH - 1:
                emit_stats(NCH - 2)
                emit_stats(NCH - 1)
            return ("s_act", v_epi)
        emit_dense(prog, nc, T, sb, Wf, sb["xn"][:, :], epi_d1, "d1")
        v_d1_stats = T["s_pe"].n

        prog.wait("vector", s_pe, v_d1_stats)
        prog.op("vector", lambda e: e.tensor_scalar(sb["murow"][:, :], T["p_s1t"][0:1, :], 1.0/D, None, op0=ALU.mult), s_dve)
        prog.op("vector", lambda e: e.tensor_scalar(sb["rowA"][:, :], T["p_s2t"][0:1, :], 1.0/D, None, op0=ALU.mult), s_dve)
        prog.op("vector", lambda e: e.tensor_tensor(sb["rowB"][:, :], sb["murow"][:, :], sb["murow"][:, :], op=ALU.mult), s_dve)
        prog.op("vector", lambda e: e.tensor_tensor(sb["rowA"][:, :], sb["rowA"][:, :], sb["rowB"][:, :], op=ALU.subtract), s_dve)
        v_var = s_dve.n
        prog.wait("scalar", s_dve, v_var)
        prog.op("scalar", lambda e: e.activation(sb["rowB"][:, :], sb["rowA"][:, :], AF.Ln, bias=sb["eps1"][:, :]), s_act)
        prog.op("scalar", lambda e: e.activation(sb["rrow"][:, :], sb["rowB"][:, :], AF.Exp, scale=-0.5), s_act)
        v_r2 = s_act.n

        prog.barrier(["sync", "tensor", "vector", "scalar", "gpsimd"], [s_gp, s_dve, s_pe, s_act])
        prog.wait("sync", T["s_st0"])
        prog.wait("sync", T["s_st1"])
        prog.wait("vector", s_act, v_r2)
        emit_ln_apply(prog, nc, T, sb, o_x2T, sb["murow"][:, :], sb["rrow"][:, :], "xn")
        v_xn2 = T["s_dve"].n

        prog.barrier(["sync", "tensor", "vector", "scalar", "gpsimd"], [s_gp, s_dve, s_pe, s_act])
        prog.wait("tensor", s_dve, v_xn2)
        def mk_epi_kv(bias_name, out_dram):
            state = {}
            def epi(oc, v_mm):
                buf = oc % 2
                prog.wait("scalar", s_pe, v_mm)
                if oc >= 2:
                    prog.wait("scalar", T["s_st0" if buf == 0 else "s_st1"], state[oc-2])
                prog.op("scalar", lambda e, buf=buf, oc=oc: e.activation(sb["kvout"][buf][:, :], T["p_d"][buf][:, :],
                        AF.Identity, bias=sb[bias_name][:, oc:oc+1]), s_act)
                v_epi = s_act.n
                prog.wait("sync", s_act, v_epi)
                v_store = prog.dma("sync", out_dram[oc*128:(oc+1)*128, :], sb["kvout"][buf][:, :], T["s_st0" if buf == 0 else "s_st1"])
                state[oc] = v_store
                return ("s_act", v_epi)
            return epi
        emit_dense(prog, nc, T, sb, Wk, sb["xn"][:, :], mk_epi_kv("bk", o_kT), "wk")
        prog.wait("gpsimd", T["s_st0"], T["s_st0"].n)
        prog.wait("gpsimd", T["s_st1"], T["s_st1"].n)
        prog.cc("AllGather", ALU.bypass, GROUPS4, EX["kT_loc"], EX["Kst_d"], T["s_cck"])
        prog.barrier(["sync", "tensor", "vector", "scalar"], [s_act, T["s_st0"], T["s_st1"], s_pe])
        emit_dense(prog, nc, T, sb, Wv, sb["xn"][:, :], mk_epi_kv("bv", o_vT), "wv")
        prog.wait("gpsimd", T["s_st0"], T["s_st0"].n)
        prog.wait("gpsimd", T["s_st1"], T["s_st1"].n)
        prog.cc("AllGather", ALU.bypass, GROUPS4, EX["vT_loc"], EX["Vst_d"], T["s_ccv"])
        prog.barrier(["sync", "tensor", "vector", "scalar"], [s_act, T["s_st0"], T["s_st1"], s_pe, s_dve])
        prog.wait("tensor", s_dve, v_bd_ready)
        q_state = {}
        def epi_q(oc, v_mm):
            buf = oc % 2
            h = oc // 2
            pbank = T["p_s1t"] if buf == 0 else T["p_s2t"]
            prog.wait("scalar", s_pe, v_mm)
            if oc >= 2:
                prog.wait("scalar", s_pe, q_state[oc-2][0])
            prog.op("scalar", lambda e, buf=buf, oc=oc: e.activation(sb["qout"][buf][:, :], T["p_d"][buf][:, :],
                    AF.Identity, bias=sb["bq"][:, oc:oc+1]), s_act)
            v_epi = s_act.n
            prog.wait("tensor", s_act, v_epi)
            if oc >= 2:
                prog.wait("tensor", s_act, q_state[oc-2][1])
            for half in range(2):
                prog.op("tensor", lambda e, buf=buf, h=h, half=half, pbank=pbank: e.matmul(
                    pbank[:, half*512:(half+1)*512], sb["BD"][:, h*128:(h+1)*128],
                    sb["qout"][buf][:, half*512:(half+1)*512], start=True, stop=True), s_pe if half == 1 else None)
            v_bdmm = s_pe.n
            prog.wait("scalar", s_pe, v_bdmm)
            if oc >= 2:
                prog.wait("scalar", T["s_st0" if buf == 0 else "s_st1"], q_state[oc-2][2])
            prog.op("scalar", lambda e, buf=buf, pbank=pbank: e.activation(sb["qhout"][buf][:, :], pbank[:, :], AF.Identity), s_act)
            v_ev = s_act.n
            prog.wait("sync", s_act, v_ev)
            v_store = prog.dma("sync", o_qhT[oc*128:(oc+1)*128, :], sb["qhout"][buf][:, :], T["s_st0" if buf == 0 else "s_st1"])
            q_state[oc] = (v_bdmm, v_ev, v_store)
            return ("s_act", v_epi)
        emit_dense(prog, nc, T, sb, Wq, sb["xn"][:, :], epi_q, "wq")

def phase3(nc, prog, SEMS, EX):
    x2T = EX["x2T_d"]
    Kst = EX["Kst_d"]
    Vst = EX["Vst_d"]
    qhT = EX["qhT_d"]
    Wf = EX["Wf_p"]
    bf_p = EX["bf_pack"]
    cent = EX["cent_pack"]
    cns = EX["cns"]
    o_x3T = EX["x3T_d"]
    o_d2T = EX["d2T_d"]
    o_counts = EX["c3_loc"]
    o_qs = EX["qs3_d"]

    with ExitStack() as st:
        def sbuf(name, shape, dt):
            return st.enter_context(nc.sbuf_tensor("p3_" + name, shape, dt))
        sb = {}
        for n in ["ident_f32", "ones_col_f32", "ones_row_f32"]:
            sb[n] = sbuf(n, list(CONSTS[n].shape), F32)
        for n in ["ident_bf16", "ones_col_bf16"]:
            sb[n] = sbuf(n, list(CONSTS[n].shape), BF16)
        sb["cent"] = sbuf("cent", [128, NCH*NCL], F32)
        sb["xn"] = sbuf("xn", [128, NCH*NT], BF16)
        sb["K"] = sbuf("K", [128, 2*4096], BF16)
        sb["VT"] = sbuf("VT", [128, 2*4096], BF16)
        sb["Vtok"] = sbuf("Vtok", [128, NCH*DH], BF16)
        sb["Q"] = sbuf("Q", [128, 2*NT], BF16)
        sb["X2"] = sbuf("X2", [128, 2*NT], F32)
        sb["S"] = sbuf("S", [128, 4096], F32)
        sb["P"] = sbuf("P", [128, 4096], BF16)
        sb["P2"] = sbuf("P2", [128, 4096], BF16)
        _xnb = sb["xn"][:, :]
        sb["Sb"] = _xnb[:, 0:8192].bitcast(F32)
        sb["Pb"] = _xnb[:, 8192:12288]
        sb["P2b"] = _xnb[:, 12288:16384]
        sb["negmax2"] = sbuf("negmax2", [128, 1], F32)
        sb["z2"] = sbuf("z2", [128, 1], F32)
        sb["lnz2"] = sbuf("lnz2", [128, 1], F32)
        sb["zinv2"] = sbuf("zinv2", [128, 1], F32)
        sb["junk2"] = sbuf("junk2", [128, 1], F32)
        sb["PT"] = sbuf("PT", [128, NCH*DH], BF16)
        sb["negmax"] = sbuf("negmax", [128, 1], F32)
        sb["z"] = sbuf("z", [128, 1], F32)
        sb["lnz"] = sbuf("lnz", [128, 1], F32)
        sb["zinv"] = sbuf("zinv", [128, 1], F32)
        sb["junk"] = sbuf("junk", [128, 1], F32)
        sb["xbuf"] = [sb["S"][:, 0:NT], sb["S"][:, NT:2*NT]]
        _pf32 = sb["P"][:, :].bitcast(F32)
        sb["xtile"] = [_pf32[:, i*128:(i+1)*128] for i in range(4)]
        sb["sqf"] = [sb["S"][:, 2*NT:3*NT], sb["S"][:, 3*NT:4*NT]]
        sb["t1"] = [sbuf(f"t1_{i}", [128, NT], F32) for i in range(2)]
        kf32 = sb["K"][:, :].bitcast(F32)
        sb["rbc"] = kf32[:, 0:NT]
        sb["nmbc"] = kf32[:, NT:2*NT]
        sb["panel"] = [sb["P"][:, :], sb["P2"][:, :]]
        sb["d2out"] = sb["t1"]
        sb["bf"] = sbuf("bf", [128, NCH], F32)
        sb["csrow"] = sbuf("csrow", [1, NCL], F32)
        sb["csbc"] = sbuf("csbc", [128, NCL], F32)
        vtf32 = sb["VT"][:, :].bitcast(F32)
        vtokf32 = sb["Vtok"][:, :].bitcast(F32)
        sb["murow"] = vtf32[0:1, 0:NT]; sb["rrow"] = vtf32[0:1, NT:2*NT]
        sb["nmrow"] = vtf32[0:1, 2*NT:3*NT]; sb["negr10row"] = vtf32[0:1, 3*NT:4*NT]
        sb["s1row"] = vtokf32[0:1, 0:NT]; sb["s2row"] = vtokf32[0:1, NT:2*NT]
        sb["msqrow"] = vtokf32[0:1, 2*NT:3*NT]; sb["m2row"] = vtokf32[0:1, 3*NT:4*NT]
        sb["varrow"] = sb["msqrow"]; sb["lnrow"] = sb["m2row"]
        sb["eps1"] = sbuf("eps1", [1, 1], F32)
        sb["mucols"] = sbuf("mucols", [128, NTT], F32)
        sb["nr10cols"] = sbuf("nr10cols", [128, NTT], F32)
        sb["lg"] = sbuf("lg", [128, NCL], F32)
        sb["pd"] = sbuf("pd", [128, NCL], F32)
        sb["z001"] = sbuf("z001", [128, 1], F32)
        sb["hits"] = sbuf("hits", [128, NCL], BF16)
        sb["qs_all"] = sbuf("qs_all", [128, NTT], F32)
        sb["cnt"] = sbuf("cnt", [64, 1], F32)
        sb["rowcols"] = sbuf("rowcols", [16, 128], F32)
        sb["qsT"] = sbuf("qsT", [NTT, 128], F32)

        T = {}
        T["scr_dram"] = EX["scr_rows3"]
        T["p_q"] = [st.enter_context(nc.psum_tensor(f"p3_p_q{i}", [128, NT], F32)) for i in range(2)]
        T["p_av"] = st.enter_context(nc.psum_tensor("p3_p_av", [128, NT], F32))
        T["p_t"] = [st.enter_context(nc.psum_tensor(f"p3_p_t{i}", [128, NT], BF16)) for i in range(2)]
        T["p_s1"] = T["p_q"][0]
        T["p_s2"] = T["p_q"][1]
        T["p_s1t"] = T["p_q"][0]
        T["p_s2t"] = T["p_q"][1]
        T["p_dots"] = [T["p_av"][:, 0:512], T["p_av"][:, 512:1024]]
        T["p_scr"] = T["p_t"][0][:, :].bitcast(F32)
        T["p_scr2"] = T["p_t"][1][:, :].bitcast(F32)[:, 0:256]
        T["p_counts"] = T["p_t"][1][:, :].bitcast(F32)[:, 256:512]
        T["p_d"] = T["p_q"]
        T.update(SEMS)
        T["s_cast"] = SEMS["s_ld"]
        T["pe_done_chunk"] = {}
        T["act_done_chunk"] = {}
        T["dense_mm_done_d2"] = {}
        s_ld, s_pe, s_dve, s_act, s_st, s_gp = (T[k] for k in ["s_ld","s_pe","s_dve","s_act","s_st","s_gp"])
        s_h = [T["s_h0"], T["s_h1"]]

        for n in ["ident_f32", "ones_col_f32", "ones_row_f32", "ident_bf16", "ones_col_bf16"]:
            prog.dma("sync", sb[n][:, :], cns[n][:, :], s_ld)
        prog.dma("sync", sb["cent"][:, :], cent[:, :], s_ld)
        prog.dma("sync", sb["bf"][:, :], bf_p[:, :], s_ld)
        v_setup = s_ld.n
        prog.op("vector", lambda e: e.memset(sb["eps1"][:, :], 1e-5), s_dve)

        prog.wait("tensor", s_ld, v_setup)
        prog.wait("sync", T["s_cck"], 1)
        prog.wait("sync", T["s_ccv"], 1)
        head_done = {}
        for h in range(NH):
            s_vt = T["s_vt0" if h % 2 == 0 else "s_vt1"]
            s_kq = T["s_kq0" if h % 2 == 0 else "s_kq1"]
            s_xh = T["s_xh0" if h % 2 == 0 else "s_xh1"]
            if h >= 1:
                prog.wait("sync", s_pe, head_done[h-1]["vtrans"])
            for dc in range(2):
                for r2 in range(4):
                    prog.dma("sync", sb["VT"][:, dc*4096 + r2*1024 : dc*4096 + (r2+1)*1024],
                             Vst[r2*D + h*DH + dc*128 : r2*D + h*DH + (dc+1)*128, :], s_vt)
            if h >= 1:
                prog.wait("sync", s_pe, head_done[h-1]["lastS"])
            for dc in range(2):
                for r2 in range(4):
                    prog.dma("sync", sb["K"][:, dc*4096 + r2*1024 : dc*4096 + (r2+1)*1024],
                             Kst[r2*D + h*DH + dc*128 : r2*D + h*DH + (dc+1)*128, :], s_kq)
                prog.dma("sync", sb["Q"][:, dc*NT:(dc+1)*NT], qhT[h*DH + dc*128 : h*DH + (dc+1)*128, :], s_kq)
            if h >= 1:
                prog.wait("sync", T["s_st0" if (h-1) % 2 == 0 else "s_st1"], head_done[h-1]["store"])
            for dc in range(2):
                prog.dma("sync", sb["X2"][:, dc*NT:(dc+1)*NT], x2T[h*DH + dc*128 : h*DH + (dc+1)*128, :], s_xh)
            v_vt, v_kq, v_xh = s_vt.n, s_kq.n, s_xh.n

            prog.wait("tensor", s_vt, v_vt)
            vt_evac = {}
            if h >= 1:
                prog.wait("vector", s_pe, head_done[h-1]["av"])
            for rnd in range(8):
                pt = T["p_t"][rnd % 2]
                if rnd >= 2:
                    prog.wait("tensor", s_act, vt_evac[rnd-2])
                for j in range(8):
                    g = rnd*8 + j
                    mc, dc = g // 2, g % 2
                    prog.op("tensor", lambda e, pt=pt, j=j, mc=mc, dc=dc: e.transpose(
                        pt[:, j*128:(j+1)*128],
                        sb["VT"][:, dc*4096 + mc*128 : dc*4096 + (mc+1)*128],
                        sb["ident_bf16"][:, :]), s_pe if j == 7 else None)
                v_t = s_pe.n
                prog.wait("scalar", s_pe, v_t)
                prog.op("scalar", lambda e, pt=pt, rnd=rnd: e.activation(
                    sb["Vtok"][:, rnd*1024:(rnd+1)*1024], pt[:, 0:1024], AF.Identity), s_act)
                vt_evac[rnd] = s_act.n
            v_vtok = s_act.n
            T["v_vtrans_end"] = s_pe.n
            T.pop("v_max_prev", None); T.pop("v_pscale_prev", None); T.pop("v_ptrans_prev", None); T.pop("v_exp_prev", None)
            st_pscale = {}
            st_exp = {}
            st_max = {}
            st_evac = {}
            st_pt = {}
            st_av = {}
            v_resid = {}
            def SS(qt): return sb["S"][:, :] if qt % 2 == 0 else sb["Sb"]
            def PP(qt): return sb["P"][:, :] if qt % 2 == 0 else sb["Pb"]
            def PP2(qt): return sb["P2"][:, :] if qt % 2 == 0 else sb["P2b"]
            def NM(qt): return sb["negmax"][:, :] if qt % 2 == 0 else sb["negmax2"][:, :]
            def ZZ(qt): return (sb["z"][:, :], sb["lnz"][:, :], sb["zinv"][:, :], sb["junk"][:, :]) if qt % 2 == 0 else (sb["z2"][:, :], sb["lnz2"][:, :], sb["zinv2"][:, :], sb["junk2"][:, :])
            def emit_S(qt):
                prog.wait("tensor", s_kq, v_kq)
                for qq in range(4):
                    pq = T["p_q"][qq % 2]
                    key = f"pq_reader_{qq % 2}"
                    if key in T:
                        eng, val = T[key]
                        prog.wait("tensor", T[eng], val)
                    for msub in range(2):
                        for dc in range(2):
                            prog.op("tensor", lambda e, pq=pq, msub=msub, dc=dc, qq=qq, qt=qt: e.matmul(
                                pq[:, msub*512:(msub+1)*512],
                                sb["Q"][:, dc*NT + qt*128 : dc*NT + (qt+1)*128],
                                sb["K"][:, dc*4096 + qq*1024 + msub*512 : dc*4096 + qq*1024 + (msub+1)*512],
                                start=(dc == 0), stop=(dc == 1)), s_pe if (msub == 1 and dc == 1) else None)
                    v_s = s_pe.n
                    if qq % 2 == 0:
                        prog.wait("vector", s_pe, v_s)
                        prog.wait("vector", s_act, st_exp.get(qt-2, 0))
                        prog.op("vector", lambda e, pq=pq, qq=qq, qt=qt: e.tensor_copy(SS(qt)[:, qq*1024:(qq+1)*1024], pq[:, :]), s_dve)
                        T[key] = ("s_dve", s_dve.n)
                    else:
                        prog.wait("scalar", s_pe, v_s)
                        prog.wait("scalar", s_act, st_exp.get(qt-2, 0))
                        prog.op("scalar", lambda e, pq=pq, qq=qq, qt=qt: e.activation(SS(qt)[:, qq*1024:(qq+1)*1024], pq[:, :], AF.Identity), s_act)
                        T[key] = ("s_act", s_act.n)
                st_evac[qt] = (s_dve.n, s_act.n)
            def emit_chain(qt):
                prog.wait("vector", s_act, st_evac[qt][1])
                prog.wait("vector", s_dve, st_evac[qt][0])
                prog.wait("vector", s_act, st_exp.get(qt-2, 0))
                prog.op("vector", lambda e, qt=qt: e.tensor_reduce(NM(qt), SS(qt), axis=AX.X, op=ALU.max, negate=True), s_dve)
                st_max[qt] = s_dve.n
                z_, lnz_, zinv_, junk_ = ZZ(qt)
                prog.wait("scalar", s_dve, st_max[qt])
                prog.wait("scalar", s_gp, st_pscale.get(qt-2, 0))
                prog.op("scalar", lambda e, qt=qt, z_=z_: e.activation(PP(qt), SS(qt), AF.Exp, bias=NM(qt), accum_out=z_), None)
                prog.op("scalar", lambda e, z_=z_, lnz_=lnz_: e.activation(lnz_, z_, AF.Ln), None)
                prog.op("scalar", lambda e, lnz_=lnz_, zinv_=zinv_: e.activation(zinv_, lnz_, AF.Exp, scale=-1.0), None)
                prog.op("scalar", lambda e, zinv_=zinv_, junk_=junk_: e.activation(junk_, zinv_, AF.Identity), s_act)
                st_exp[qt] = s_act.n
                prog.wait("gpsimd", s_act, st_exp[qt])
                prog.wait("gpsimd", s_pe, st_pt.get(qt-2, 0))
                prog.op("gpsimd", lambda e, qt=qt, zinv_=zinv_: e.tensor_scalar(PP2(qt), PP(qt), zinv_, 1.0, op0=ALU.mult, op1=ALU.mult), s_gp)
                st_pscale[qt] = s_gp.n
            def emit_PT(qt):
                prog.wait("tensor", s_gp, st_pscale[qt])
                prog.wait("tensor", s_pe, st_av.get(qt // 2 - 1, 0))
                for rnd in range(4):
                    pt = T["p_t"][rnd % 2]
                    prog.wait("tensor", s_act, T.get(f"pt_evac_{rnd % 2}", vt_evac[6 + rnd % 2]))
                    for j in range(8):
                        mc = rnd*8 + j
                        prog.op("tensor", lambda e, pt=pt, j=j, mc=mc, qt=qt: e.transpose(
                            pt[:, j*128:(j+1)*128],
                            PP2(qt)[:, mc*128:(mc+1)*128],
                            sb["ident_bf16"][:, :]), s_pe if j == 7 else None)
                    v_tr = s_pe.n
                    prog.wait("scalar", s_pe, v_tr)
                    prog.op("scalar", lambda e, pt=pt, rnd=rnd, qt=qt: e.activation(
                        sb["PT"][:, :].rearrange("p (mc q) -> p mc q", q=DH)[:, rnd*8:(rnd+1)*8, (qt % 2)*128:(qt % 2 + 1)*128],
                        pt[:, 0:1024].rearrange("p (j q) -> p j q", q=128), AF.Identity), s_act)
                    T[f"pt_evac_{rnd % 2}"] = s_act.n
                st_pt[qt] = s_pe.n
                st_pt[(qt, "dve")] = s_act.n
            def emit_AV(qst):
                prog.wait("tensor", s_act, st_pt[(qst*2+1, "dve")])
                if qst - 1 in v_resid:
                    prog.wait("tensor", s_dve, v_resid[qst-1])
                for mc in range(NCH):
                    for dc in range(2):
                        prog.op("tensor", lambda e, mc=mc, dc=dc: e.matmul(
                            T["p_av"][:, dc*512 : dc*512 + 256],
                            sb["Vtok"][:, mc*DH + dc*128 : mc*DH + (dc+1)*128],
                            sb["PT"][:, mc*DH : (mc+1)*DH],
                            start=(mc == 0), stop=(mc == NCH-1)), s_pe if (mc == NCH-1 and dc == 1) else None)
                st_av[qst] = s_pe.n
                prog.wait("vector", s_pe, st_av[qst])
                prog.wait("vector", s_xh, v_xh)
                for dc in range(2):
                    prog.op("vector", lambda e, dc=dc, qst=qst: e.tensor_tensor(
                        sb["X2"][:, dc*NT + qst*256 : dc*NT + (qst+1)*256],
                        T["p_av"][:, dc*512 : dc*512 + 256],
                        sb["X2"][:, dc*NT + qst*256 : dc*NT + (qst+1)*256], op=ALU.add), s_dve)
                v_resid[qst] = s_dve.n
            for idx in range(9):
                if idx < 8:
                    emit_S(idx)
                    if idx == 7:
                        T["v_lastS"] = s_pe.n
                if idx >= 1:
                    emit_PT(idx-1)
                if idx < 8:
                    emit_chain(idx)
                if idx >= 2 and idx % 2 == 0:
                    emit_AV(idx // 2 - 1)
            v_resid_prev = v_resid[3]
            prog.wait("sync", s_dve, v_resid_prev)
            sst = T["s_st0" if h % 2 == 0 else "s_st1"]
            for dc in range(2):
                prog.dma("sync", o_x3T[h*DH + dc*128 : h*DH + (dc+1)*128, :], sb["X2"][:, dc*NT:(dc+1)*NT], sst)
            head_done[h] = {"vtrans": T["v_vtrans_end"], "lastS": T["v_lastS"],
                            "av": s_pe.n, "resid": s_dve.n, "store": sst.n}

        prog.barrier(["sync", "tensor", "vector", "scalar", "gpsimd"],
                     [s_pe, s_dve, s_act, T["s_st0"], T["s_st1"], T["s_h0"], T["s_h1"]])
        v_done3 = emit_ln_dots(prog, nc, T, sb, o_x3T, sb, with_dots=True) if False else None
        v_done3 = emit_ln_dots(prog, nc, T, sb if False else T, o_x3T, sb, with_dots=True) if False else emit_ln_dots(prog, nc, T, o_x3T, sb, with_dots=True)
        prog.wait("vector", s_pe, v_done3)
        prog.op("vector", lambda e: e.tensor_copy(sb["cnt"][:, :], T["p_counts"][0:64, 0:1]), s_dve)
        v_cnt = s_dve.n
        prog.wait("tensor", s_pe, v_done3)
        prog.op("tensor", lambda e: e.transpose(T["p_scr"][0:NTT, 0:128], sb["qs_all"][:, :], sb["ident_f32"][:, :]), s_pe)
        v_qt = s_pe.n
        prog.wait("vector", s_pe, v_qt)
        prog.op("vector", lambda e: e.tensor_copy(sb["qsT"][:, :], T["p_scr"][0:NTT, 0:128]), s_dve)
        v_qsT = s_dve.n
        prog.wait("sync", s_dve, v_cnt)
        v_cst3 = prog.dma("sync", o_counts[:, :], sb["cnt"][:, :], s_st)
        prog.wait("sync", s_dve, v_qsT)
        prog.dma("sync", o_qs[0:1, :].rearrange("a (b p) -> (a b) p", p=128), sb["qsT"][:, :], s_st)
        v_qs_st = s_st.n
        prog.wait("gpsimd", s_st, v_cst3)
        prog.cc("AllReduce", ALU.add, [[0, 1, 2, 3, 4, 5, 6, 7]], EX["c3_loc"], EX["c3_sum"], T["s_cc3"])

        prog.barrier(["sync", "tensor", "vector", "scalar", "gpsimd"], [s_pe, s_dve, s_act, s_gp])
        emit_ln_apply(prog, nc, T, sb, o_x3T, sb["murow"][:, :], sb["rrow"][:, :], "xn")
        v_xn3 = s_dve.n

        prog.barrier(["sync", "tensor", "vector", "scalar", "gpsimd"], [s_pe, s_dve, s_act, s_gp])
        prog.wait("tensor", s_dve, v_xn3)
        sb["qs1r"] = sb["X2"][0:1, 0:NT]
        sb["cntsum"] = sb["X2"][0:64, 1024:1025]
        sb["cscol"] = sb["X2"][0:64, 1025:1026]
        sb["csrow1"] = sb["X2"][0:1, 1026:1090]
        sb["cs_bc"] = sb["X2"][:, 1090:1154]
        sb["asn_f"] = sb["X2"][:, 1154:1186]
        sb["asn_i"] = sb["X2"][:, :].bitcast(I32)[:, 1186:1218]
        sb["eq"] = sb["X2"][:, 1218:1282]
        sb["eqp"] = sb["X2"][:, 1282:1346]
        sb["rs05"] = sb["X2"][:, 1346:1378]
        sb["brs05"] = sb["X2"][:, 1378:1410]
        sb["iota64"] = sb["X2"][:, 1410:1474]
        sb["qs_bc"] = sb["Q"][:, :].bitcast(F32)
        x3r = sb["xbuf"]
        vtokf = sb["Vtok"][:, :].bitcast(F32)
        x4 = [vtokf[:, i*NT:(i+1)*NT] for i in range(4)]
        obuf = [kf32[:, 2048:2560], kf32[:, 2560:3072]]
        pob = [T["p_t"][0][:, :].bitcast(F32), T["p_t"][1][:, :].bitcast(F32)]
        s_sto = [T["s_st0"], T["s_st1"]]
        s_xx = [T["s_x0"], T["s_x1"]]

        prog.wait("sync", T["s_cc3"], 1)
        prog.wait("sync", s_st, v_qs_st)
        prog.dma("sync", sb["cntsum"][:, :], EX["c3_sum"][:, :], s_ld)
        prog.dma("sync", sb["qs1r"][:, :], o_qs[:, :], s_ld)
        prog.dma("sync", sb["asn_i"][:, :], EX["assign_pack"][:, :], s_ld)
        prog.dma("sync", sb["iota64"][:, :], cns["iota64"][:, :], s_ld)
        v_ld3 = s_ld.n
        prog.wait("vector", s_ld, v_ld3)
        prog.wait("tensor", s_ld, v_ld3)
        v_mask3, _ = emit_routing(prog, nc, T, sb, has_bias=True)
        prog.wait("tensor", s_dve, v_mask3)
        prog.wait("scalar", s_dve, v_mask3)

        d2_state = {}
        x4done = {}
        ot_state = {}
        ev_state = {}
        st_state = {}
        def emit_group_tr(g):
            prog.wait("tensor", s_dve, x4done[4*g+3])
            for tt in range(NTT):
                po = pob[tt % 2]
                gate = ev_state.get((g, tt-2)) if tt >= 2 else ev_state.get((g-1, NTT-2+tt))
                if gate:
                    prog.wait("tensor", s_act, gate)
                for ii in range(4):
                    prog.op("tensor", lambda e, po=po, ii=ii, tt=tt: e.transpose(
                        po[:, ii*128:(ii+1)*128], x4[ii][:, tt*128:(tt+1)*128], sb["ident_f32"][:, :]),
                        s_pe if ii == 3 else None)
                v_tr = s_pe.n
                prog.wait("scalar", s_pe, v_tr)
                sgate = st_state.get((g, tt-2)) if tt >= 2 else st_state.get((g-1, NTT-2+tt))
                if sgate:
                    prog.wait("scalar", s_sto[tt % 2], sgate)
                prog.op("scalar", lambda e, po=po, tt=tt: e.activation(obuf[tt % 2], po[:, :], AF.Identity), s_act)
                ev_state[(g, tt)] = s_act.n
                prog.wait("sync", s_act, ev_state[(g, tt)])
                st_state[(g, tt)] = prog.dma("sync", EX["out_tok"][tt*128:(tt+1)*128, g*512:(g+1)*512],
                                             obuf[tt % 2], s_sto[tt % 2])
            ot_state[g] = s_pe.n
        def epi_d2(oc, v_mm):
            g, i = divmod(oc, 4)
            if i == 0 and g >= 1:
                emit_group_tr(g-1)
            buf = oc % 2
            if oc >= 2:
                prog.wait("sync", s_dve, x4done[oc-2])
            v_x3 = prog.dma("sync", x3r[buf], o_x3T[oc*128:(oc+1)*128, :], s_xx[buf])
            prog.wait("scalar", s_pe, v_mm)
            if oc >= 2:
                prog.wait("scalar", s_dve, d2_state[oc-2])
            prog.op("scalar", lambda e, buf=buf, oc=oc: e.activation(sb["d2out"][buf][:, :], T["p_d"][buf][:, :],
                    AF.Identity, bias=sb["brs05"][:, oc:oc+1], scale=sb["rs05"][:, oc:oc+1]), s_act)
            v_epi = s_act.n
            prog.wait("vector", s_act, v_epi)
            if g >= 1:
                prog.wait("vector", s_pe, ot_state[g-1])
            prog.op("vector", lambda e, buf=buf, i=i: e.tensor_tensor(x4[i], sb["d2out"][buf][:, :], sb["qs_bc"][:, :], op=ALU.mult), s_dve)
            d2_state[oc] = s_dve.n
            prog.wait("vector", s_xx[buf], v_x3)
            prog.op("vector", lambda e, buf=buf, i=i: e.tensor_tensor(x4[i], x4[i], x3r[buf], op=ALU.add), s_dve)
            x4done[oc] = s_dve.n
            return ("s_act", v_epi)
        emit_dense(prog, nc, T, sb, Wf, sb["xn"][:, :], epi_d2, "d2")
        emit_group_tr(NCH // 4 - 1)

def phase4(nc, prog, SEMS, EX):
    x3T = EX["x3T_d"]
    d2T = EX["d2T_d"]
    qs3row = EX["qs3_d"]
    asn_p = EX["assign_pack"]
    cns = EX["cns"]
    out = EX["out_tok"]

    with ExitStack() as st:
        def sbuf(name, shape, dt):
            return st.enter_context(nc.sbuf_tensor("p4_" + name, shape, dt))
        sb = {}
        for n in ["ident_f32", "ones_col_f32", "ones_row_f32", "iota64"]:
            sb[n] = sbuf(n, list(CONSTS[n].shape), F32)
        sb["cntsum"] = sbuf("cntsum", [NCL, 1], F32)
        sb["qs1r"] = sbuf("qs1r", [1, NT], F32)
        sb["cscol"] = sbuf("cscol", [NCL, 1], F32)
        sb["csrow1"] = sbuf("csrow1", [1, NCL], F32)
        sb["cs_bc"] = sbuf("cs_bc", [128, NCL], F32)
        sb["qs_bc"] = sbuf("qs_bc", [128, NT], F32)
        sb["asn_i"] = sbuf("asn_i", [128, NCH], I32)
        sb["asn_f"] = sbuf("asn_f", [128, NCH], F32)
        sb["eq"] = sbuf("eq", [128, NCL], F32)
        sb["eqp"] = sbuf("eqp", [128, NCL], F32)
        sb["rs05"] = sbuf("rs05", [128, NCH], F32)
        sb["d2row"] = [sbuf(f"d2row{i}", [128, NT], F32) for i in range(2)]
        sb["x3row"] = [sbuf(f"x3row{i}", [128, NT], F32) for i in range(2)]
        sb["m1"] = [sbuf(f"m1_{i}", [128, NT], F32) for i in range(2)]
        sb["x4"] = [sbuf(f"x4_{i}", [128, NT], F32) for i in range(4)]
        sb["obuf"] = [sbuf(f"obuf{i}", [128, 512], F32) for i in range(2)]

        T = {}
        T["p_s1t"] = st.enter_context(nc.psum_tensor("p4_p_s1t", [128, NT], F32))
        T["p_s2t"] = st.enter_context(nc.psum_tensor("p4_p_s2t", [128, NT], F32))
        T["p_o"] = [st.enter_context(nc.psum_tensor(f"p4_p_o{i}", [128, 512], F32)) for i in range(2)]
        T.update(SEMS)
        s_ld, s_pe, s_dve, s_act, s_st, s_gp = (T[k] for k in ["s_ld","s_pe","s_dve","s_act","s_st","s_gp"])
        s_x = [T["s_x0"], T["s_x1"]]
        s_sto = [T["s_st0"], T["s_st1"]]

        for n in ["ident_f32", "ones_col_f32", "ones_row_f32", "iota64"]:
            prog.dma("sync", sb[n][:, :], cns[n][:, :], s_ld)
        prog.wait("sync", T["s_cc3"], 1)
        prog.dma("sync", sb["cntsum"][:, :], EX["c3_sum"][:, :], s_ld)
        prog.dma("sync", sb["qs1r"][:, :], qs3row[:, :], s_ld)
        prog.dma("sync", sb["asn_i"][:, :], asn_p[:, :], s_ld)
        v_setup = s_ld.n
        prog.wait("tensor", s_ld, v_setup)
        prog.wait("vector", s_ld, v_setup)
        v_mask, _ = emit_routing(prog, nc, T, sb, has_bias=False)

        ostate = {}
        last_m1 = {}
        last_x4 = {}
        last_gp = {}
        for g in range(8):
            x4done = {}
            for i in range(4):
                oc = g*4 + i
                buf = oc % 2
                if buf in last_gp:
                    prog.wait("sync", s_gp, last_gp[buf])
                if buf in last_m1:
                    prog.wait("sync", s_act, last_m1[buf])
                if buf in last_x4:
                    prog.wait("sync", s_dve, last_x4[buf])
                v_d2 = prog.dma("sync", sb["d2row"][buf][:, :], d2T[oc*128:(oc+1)*128, :], s_x[buf])
                v_x3 = prog.dma("sync", sb["x3row"][buf][:, :], x3T[oc*128:(oc+1)*128, :], s_x[buf])
                prog.wait("scalar", s_x[buf], v_x3)
                prog.wait("scalar", s_dve, v_mask)
                if buf in last_gp:
                    prog.wait("scalar", s_gp, last_gp[buf])
                prog.op("scalar", lambda e, buf=buf, oc=oc: e.activation(sb["m1"][buf][:, :], sb["d2row"][buf][:, :],
                        AF.Identity, scale=sb["rs05"][:, oc:oc+1]), s_act)
                v_m1 = s_act.n
                last_m1[buf] = v_m1
                prog.wait("gpsimd", s_act, v_m1)
                if buf in last_x4:
                    prog.wait("gpsimd", s_dve, last_x4[buf])
                prog.op("gpsimd", lambda e, buf=buf: e.tensor_tensor(sb["m1"][buf][:, :], sb["m1"][buf][:, :], sb["qs_bc"][:, :], op=ALU.mult), s_gp)
                v_gp = s_gp.n
                last_gp[buf] = v_gp
                prog.wait("vector", s_gp, v_gp)
                prog.wait("vector", s_x[buf], v_x3)
                if g > 0:
                    prog.wait("vector", s_pe, ostate[g-1][2])
                prog.op("vector", lambda e, buf=buf, i=i: e.tensor_tensor(sb["x4"][i][:, :], sb["m1"][buf][:, :], sb["x3row"][buf][:, :], op=ALU.add), s_dve)
                x4done[i] = s_dve.n
                last_x4[buf] = s_dve.n
            v_tr_last = 0
            for tt in range(NTT):
                po = T["p_o"][tt % 2]
                prog.wait("tensor", s_dve, x4done[3])
                if tt >= 2:
                    prog.wait("tensor", s_dve, T.get(f"o_evac_{tt-2}_{g}", 0))
                elif g > 0:
                    prog.wait("tensor", s_dve, T.get(f"o_evac_{NTT-2+tt}_{g-1}", 0))
                for i in range(4):
                    prog.op("tensor", lambda e, po=po, i=i, tt=tt: e.transpose(
                        po[:, i*128:(i+1)*128], sb["x4"][i][:, tt*128:(tt+1)*128], sb["ident_f32"][:, :]),
                        s_pe if i == 3 else None)
                v_tr = s_pe.n
                v_tr_last = v_tr
                prog.wait("vector", s_pe, v_tr)
                if tt >= 2:
                    prog.wait("vector", s_sto[tt % 2], T.get(f"o_store_{tt-2}_{g}", 0))
                elif g > 0:
                    prog.wait("vector", s_sto[tt % 2], T.get(f"o_store_{NTT-2+tt}_{g-1}", 0))
                prog.op("vector", lambda e, po=po, tt=tt: e.tensor_copy(sb["obuf"][tt % 2][:, :], po[:, :]), s_dve)
                v_ev = s_dve.n
                T[f"o_evac_{tt}_{g}"] = v_ev
                prog.wait("sync", s_dve, v_ev)
                v_st = prog.dma("sync", out[tt*128:(tt+1)*128, g*512:(g+1)*512], sb["obuf"][tt % 2][:, :], s_sto[tt % 2])
                T[f"o_store_{tt}_{g}"] = v_st
            ostate[g] = (s_gp.n, s_dve.n, v_tr_last)
        prog.wait("sync", T["s_st0"])
        prog.wait("sync", T["s_st1"])

GROUPS4 = [[0, 1, 2, 3], [4, 5, 6, 7]]

def build_fused():
    nc = bass.Bass("TRN2", target_bir_lowering=False)
    EX = {}
    EX["xT"] = nc.dram_tensor("xT", [D, NT], F32, kind="ExternalInput").ap()
    EX["cent_pack"] = nc.dram_tensor("cent_pack", [128, NCH*NCL], F32, kind="ExternalInput").ap()
    for w in ["Wf_p", "Wq_p", "Wk_p", "Wv_p"]:
        EX[w] = nc.dram_tensor(w, [NCH, 128, D], F32, kind="ExternalInput").ap()
    for b in ["bf_pack", "bq_pack", "bk_pack", "bv_pack"]:
        EX[b] = nc.dram_tensor(b, [128, NCH], F32, kind="ExternalInput").ap()
    EX["assign_pack"] = nc.dram_tensor("assign_pack", [128, NCH], I32, kind="ExternalInput").ap()
    EX["bwT"] = nc.dram_tensor("bwT", [3, NH], F32, kind="ExternalInput").ap()
    EX["Tmat"] = nc.dram_tensor("Tmat", [3, NCL], F32, kind="ExternalInput").ap()
    EX["Dmat_tiled"] = nc.dram_tensor("Dmat_tiled", [NH, NCL], F32, kind="ExternalInput").ap()
    EX["cns"] = declare_consts(nc, ["ident_f32", "ident_bf16", "ones_col_f32", "ones_col_bf16",
                                    "ones_row_f32", "iota64"])
    EX["out_tok"] = nc.dram_tensor("out_tok", [NT, D], F32, kind="ExternalOutput").ap()
    EX["c1_loc"] = nc.dram_tensor("c1_loc", [NCL, 1], F32, kind="Internal").ap()
    EX["c1_sum"] = nc.dram_tensor("c1_sum", [NCL, 1], F32, kind="Internal").ap()
    EX["c3_loc"] = nc.dram_tensor("c3_loc", [NCL, 1], F32, kind="Internal").ap()
    EX["c3_sum"] = nc.dram_tensor("c3_sum", [NCL, 1], F32, kind="Internal").ap()
    EX["mu1_d"] = nc.dram_tensor("mu1_d", [1, NT], F32, kind="Internal").ap()
    EX["r1_d"] = nc.dram_tensor("r1_d", [1, NT], F32, kind="Internal").ap()
    EX["qs1_d"] = nc.dram_tensor("qs1_d", [1, NT], F32, kind="Internal").ap()
    EX["qs3_d"] = nc.dram_tensor("qs3_d", [1, NT], F32, kind="Internal").ap()
    EX["x2T_d"] = nc.dram_tensor("x2T_d", [D, NT], F32, kind="Internal").ap()
    EX["kT_loc"] = nc.dram_tensor("kT_loc", [D, NT], BF16, kind="Internal").ap()
    EX["vT_loc"] = nc.dram_tensor("vT_loc", [D, NT], BF16, kind="Internal").ap()
    EX["qhT_d"] = nc.dram_tensor("qhT_d", [D, NT], BF16, kind="Internal").ap()
    EX["Kst_d"] = nc.dram_tensor("Kst_d", [4*D, NT], BF16, kind="Internal").ap()
    EX["Vst_d"] = nc.dram_tensor("Vst_d", [4*D, NT], BF16, kind="Internal").ap()
    EX["x3T_d"] = nc.dram_tensor("x3T_d", [D, NT], F32, kind="Internal").ap()
    EX["d2T_d"] = nc.dram_tensor("d2T_d", [D, NT], F32, kind="Internal").ap()
    EX["scr_rows1"] = nc.dram_tensor("scr_rows1", [2, NT], F32, kind="Internal").ap()
    EX["scr_rows3"] = nc.dram_tensor("scr_rows3", [2, NT], F32, kind="Internal").ap()
    EX["A_scr"] = nc.dram_tensor("A_scr", [NH, NCL], F32, kind="Internal").ap()

    prog = Prog()
    SEMS = {n: prog.sem(n) for n in ALL_SEM_NAMES}
    prog.canon = {"vector": SEMS["s_dve"], "scalar": SEMS["s_act"], "gpsimd": SEMS["s_gp"]}
    with ExitStack() as st_outer:
        phase1(nc, prog, SEMS, EX)
        phase_barrier(prog)
        phase2(nc, prog, SEMS, EX)
        phase_barrier(prog)
        phase3(nc, prog, SEMS, EX)
        phase_barrier(prog)
        alloc_sems(nc, prog, st_outer)
        prog.replay(nc)
    return nc

_CACHE = {}

def _tables():
    g = [1.0, 1.0, 1.0]; dim = 3; n = 8
    masks = sorted(range(n), key=lambda m: (bin(m).count('1'), m))
    inv = {m: i for i, m in enumerate(masks)}
    C = np.zeros((n, n, n), dtype=np.float64)
    for i, a in enumerate(masks):
        for j, b in enumerate(masks):
            sw, at = (0, a >> 1)
            while at:
                sw += bin(at & b).count('1'); at >>= 1
            s = -1.0 if sw & 1 else 1.0
            for t in range(dim):
                if a >> t & 1 and b >> t & 1:
                    s *= float(g[t])
            C[i, j, inv[a ^ b]] = s
    grades = np.array([bin(m).count('1') for m in masks])
    rev = np.where(grades * (grades - 1) // 2 % 2 == 1, -1.0, 1.0)
    met = np.array([float(np.prod([g[t] for t in range(dim) if m >> t & 1] + [1.0])) for m in masks])
    return C, rev, met

def kernel(L0_attn_x, L0_attn_Wq, L0_attn_bq, L0_attn_Wk, L0_attn_bk, L0_attn_Wv, L0_attn_bv,
           L0_attn_bias_weight, L0_ffn_weight, L0_ffn_bias, L0_ffn_centroids, L0_ffn_assignments):
    x = np.asarray(L0_attn_x, dtype=np.float32)
    Wq = np.asarray(L0_attn_Wq, dtype=np.float32); bq = np.asarray(L0_attn_bq, dtype=np.float32)
    Wk = np.asarray(L0_attn_Wk, dtype=np.float32); bk = np.asarray(L0_attn_bk, dtype=np.float32)
    Wv = np.asarray(L0_attn_Wv, dtype=np.float32); bv = np.asarray(L0_attn_bv, dtype=np.float32)
    bw = np.asarray(L0_attn_bias_weight, dtype=np.float32)
    Wf = np.asarray(L0_ffn_weight, dtype=np.float32); bff = np.asarray(L0_ffn_bias, dtype=np.float32)
    cent = np.asarray(L0_ffn_centroids, dtype=np.float32)
    assign = np.asarray(L0_ffn_assignments)

    C, rev, met = _tables()
    biv = C[:, :, [4, 5, 6]]
    scale = 1.0 / np.sqrt(DH)
    Tm = np.stack([(rev[:, None] * biv[:, :, c]).reshape(64) for c in range(3)]).astype(np.float32)
    Dm = np.tile(((rev[:, None] * (scale * np.diag(met))).reshape(64))[None, :], (NH, 1)).astype(np.float32)

    if "nc" not in _CACHE:
        _CACHE["nc"] = build_fused()
    nc = _CACHE["nc"]

    shards_tok = [np.ascontiguousarray(x.reshape(8, NT, D)[c]) for c in range(8)]
    shards = [np.ascontiguousarray(s.T) for s in shards_tok]
    cent_pack = pack_cent(cent)
    consts = const_inputs(["ident_f32", "ident_bf16", "ones_col_f32", "ones_col_bf16", "ones_row_f32", "iota64"])
    asn_pack = np.ascontiguousarray(assign.reshape(NCH, 128).T).astype(np.int32)
    packs = dict(Wf_p=pack_w_panels(Wf), Wq_p=pack_w_panels(Wq), Wk_p=pack_w_panels(Wk), Wv_p=pack_w_panels(Wv),
                 bf_pack=pack_col(bff), bq_pack=pack_col(bq), bk_pack=pack_col(bk), bv_pack=pack_col(bv),
                 assign_pack=asn_pack, bwT=np.ascontiguousarray(bw.T), Tmat=Tm, Dmat_tiled=Dm,
                 cent_pack=cent_pack)

    ins = [dict(xT=shards[c], **packs, **consts) for c in range(8)]
    r = run_bass_kernel_spmd(nc, ins, core_ids=list(range(8))).results

    out = np.concatenate([np.asarray(r[c]["out_tok"]) for c in range(8)], axis=0)
    return out.reshape(2, 4096, D).astype(np.float32)
